# version 01
# baseline (speedup 1.0000x reference)
import numpy as np
import ml_dtypes
from contextlib import ExitStack, contextmanager

import concourse.bass as bass
import concourse.mybir as mybir
from concourse.bass_utils import run_bass_kernel_spmd
from bass_rust import DynSlice, RuntimeValue

F32 = mybir.dt.float32
BF16 = mybir.dt.bfloat16
I32 = mybir.dt.int32
AF = mybir.ActivationFunctionType
ALU = mybir.AluOpType
BF = ml_dtypes.bfloat16

B, S, D = 4, 4096, 2048
TOK = 2048
NCK = D // 128
SUB = 512
NSUB = TOK // SUB
CHUNK = 256
DIN = 4096
NH = 32
HD = 128
DS = 64
CONVD = DIN + 2 * DS
DIP = 2 * DIN + 2 * DS + NH
LN_EPS = 1e-5
NEG = -60000.0
NEGBIG = -1e30


_inst_with_update = {}


class Sem:
    __slots__ = ("h", "n")

    def __init__(self, h):
        self.h = h
        self.n = 0

    def inc(self, inst, k=1):
        key = id(inst)
        assert key not in _inst_with_update, "instruction already carries a sem update"
        _inst_with_update[key] = inst
        inst.then_inc(self.h, k)
        self.n += k
        return self.n


class Prog:
    def __init__(self):
        self.nc = bass.Bass("TRN2", target_bir_lowering=False, debug=False)
        self.stack = ExitStack()
        self._names = {}
        self._scopes = [self.stack]

    def _nm(self, name):
        i = self._names.get(name, -1) + 1
        self._names[name] = i
        return name if i == 0 else f"{name}_{i}"

    @contextmanager
    def scope(self):
        st = ExitStack()
        self._scopes.append(st)
        try:
            yield st
        finally:
            self._scopes.pop()
            st.close()

    def sbuf(self, name, shape, dtype, side=None):
        kw = {"side": side} if side else {}
        return self._scopes[-1].enter_context(self.nc.sbuf_tensor(self._nm(name), shape, dtype, **kw))

    def psum(self, name, shape, dtype):
        return self._scopes[-1].enter_context(self.nc.psum_tensor(self._nm(name), shape, dtype))

    def sem(self, name):
        return Sem(self.stack.enter_context(self.nc.semaphore(self._nm(name))))

    def dram(self, name, shape, dtype, kind=None, addr_space=None):
        kw = {}
        if kind is not None:
            kw["kind"] = kind
        if addr_space is not None:
            kw["addr_space"] = addr_space
        return self.nc.dram_tensor(self._nm(name), shape, dtype, **kw)


def dma(q, out, in_, sem):
    q.dma_start(out=out, in_=in_).then_inc(sem.h, 16)
    sem.n += 16


def dma_t(q, out, in_, sem):
    q.dma_start(out=out, in_=in_, transpose=True).then_inc(sem.h, 16)
    sem.n += 16


class Rotor:

    def __init__(self, banks, s_prod: Sem, s_cons: Sem):
        self.banks = list(banks)
        self.free_at = [0] * len(self.banks)
        self.sp = s_prod
        self.sc = s_cons
        self.i = 0
        self._last = None

    def acquire(self, prod_eng):
        b = self.i % len(self.banks)
        self.i += 1
        self._last = b
        if self.free_at[b]:
            prod_eng.wait_ge(self.sc.h, self.free_at[b])
        return self.banks[b]

    def commit(self, last_prod_inst):
        return self.sp.inc(last_prod_inst, 1)

    def release(self, last_cons_inst):
        b = self._rel_pending.pop(0)
        self.free_at[b] = self.sc.inc(last_cons_inst, 1)

    @property
    def _rel_pending(self):
        if not hasattr(self, "_rp"):
            self._rp = []
        return self._rp

    def acquired(self):
        self._rel_pending.append(self._last)


class Ctx:
    pass


ALL_ENGS = ("tensor", "vector", "scalar", "gpsimd", "sync")


def join_all(cx, engines=ALL_ENGS):
    nc = cx.P.nc
    for e in engines:
        eng = getattr(nc, e)
        for s in cx.all_sems:
            if s.n:
                eng.wait_ge(s.h, s.n)


def V(cx, inst):
    n = cx.s_ev.inc(inst, 1)
    cx.P.nc.vector.wait_ge(cx.s_ev.h, n)
    return n


def A(cx, inst):
    n = cx.s_ac.inc(inst, 1)
    cx.P.nc.scalar.wait_ge(cx.s_ac.h, n)
    return n


def alloc_common(P: Prog, io):
    nc = P.nc
    cx = Ctx()
    cx.P = P
    cx.io = io
    cx.xc = P.sbuf("xc", [128, NCK, TOK], BF16)
    cx.stat_bf = P.sbuf("stat_bf", [1, 2 * D], BF16)
    cx.consts = P.sbuf("consts", [128, 8], F32)
    cx.onesrow = P.sbuf("onesrow", [1, 128], BF16)
    cx.onescol = P.sbuf("onescol", [128, 1], BF16)
    cx.ident = P.sbuf("ident", [128, 128], BF16)
    cx.tieb = P.sbuf("tieb", [128, 16], F32)
    cx.ident32 = P.sbuf("ident32", [128, 128], F32)
    cx.pb = [P.psum("pb", [128, 512], F32) for _ in range(4)]
    cx.pm = [P.psum("pm", [128, 512], F32) for _ in range(2)]
    cx.pt = [P.psum("pt", [128, 1024], BF16) for _ in range(2)]
    cx.s_ld = P.sem("s_ld")
    cx.s_mm = P.sem("s_mm")
    cx.s_ev = P.sem("s_ev")
    cx.s_ac = P.sem("s_ac")
    cx.s_st = P.sem("s_st")
    cx.s_gp = P.sem("s_gp")
    cx.all_sems = [cx.s_ld, cx.s_mm, cx.s_ev, cx.s_ac, cx.s_st, cx.s_gp]
    cx.cumdram = None
    cx.prev_ytok = 0
    cx.cumrows_out = None
    cx.ct_all_out = None
    cx.dbg = None
    cx.rot = Rotor(cx.pb, cx.s_mm, cx.s_ev)
    cx.rott = Rotor([t[:, 0:512] for t in cx.pt], cx.s_mm, cx.s_ev)
    cx.rotm = Rotor(cx.pm, cx.s_mm, cx.s_ev)

    dma(nc.sync, cx.consts[:, :], io["consts"][:, :], cx.s_ld)
    dma(nc.sync, cx.onesrow[:, :], io["onesrow"][:, :], cx.s_ld)
    dma(nc.sync, cx.onescol[:, :], io["onescol"][:, :], cx.s_ld)
    dma(nc.sync, cx.ident[:, :], io["ident"][:, :], cx.s_ld)
    dma(nc.sync, cx.tieb[:, :], io["tieb"][:, :], cx.s_ld)
    dma(nc.sync, cx.ident32[:, :], io["ident32"][:, :], cx.s_ld)
    nc.const_aps.register(F32, 0.0, cx.consts[:, 0:1])
    nc.const_aps.register(F32, 1.0, cx.consts[:, 1:2])
    nc.const_aps.register(F32, LN_EPS, cx.consts[:, 2:3])
    return cx


def emit_transpose_blocks(cx, pairs):
    nc = cx.P.nc
    rot = cx.rott
    for g0 in range(0, len(pairs), 4):
        grp = pairs[g0:g0 + 4]
        bank = rot.acquire(nc.tensor)
        rot.acquired()
        last = None
        for k, (s, d) in enumerate(grp):
            last = nc.tensor.transpose(out=bank[:, k * 128:(k + 1) * 128], in_=s,
                                       identity=cx.ident[:, :])
        tok = rot.commit(last)
        nc.vector.wait_ge(rot.sp.h, tok)
        lastv = None
        for k, (s, d) in enumerate(grp):
            lastv = nc.vector.tensor_copy(out=d, in_=bank[:, k * 128:(k + 1) * 128])
        rot.release(lastv)


def emit_stats_center(cx, buf, sq_scratch, center=True, st_out=None):
    nc = cx.P.nc
    P = cx.P
    stsc = P.scope()
    stsc.__enter__()
    cx.stat = P.sbuf("stat", [1, 2 * D], F32)
    cx.stat_sc = P.sbuf("stat_sc", [1, 512], F32)
    prev_group_done = 0
    for tb in range(TOK // 512):
        if prev_group_done:
            nc.scalar.wait_ge(cx.s_mm.h, prev_group_done)
        a = None
        for ck in range(NCK):
            a = nc.scalar.activation(out=sq_scratch[:, ck, :],
                                     in_=buf[:, ck, tb * 512:(tb + 1) * 512],
                                     func=AF.Square)
        amark = cx.s_ac.inc(a, 1)
        bank_s = cx.rotm.acquire(nc.tensor)
        cx.rotm.acquired()
        bank_q = cx.rotm.acquire(nc.tensor)
        cx.rotm.acquired()
        nc.tensor.wait_ge(cx.s_ac.h, amark)
        mmq = None
        for ck in range(NCK):
            nc.tensor.matmul(out=bank_s[0:1, :], lhsT=cx.onescol[:, :],
                             rhs=buf[:, ck, tb * 512:(tb + 1) * 512],
                             start=(ck == 0), stop=(ck == NCK - 1))
            mmq = nc.tensor.matmul(out=bank_q[0:1, :], lhsT=cx.onescol[:, :],
                                   rhs=sq_scratch[:, ck, :],
                                   start=(ck == 0), stop=(ck == NCK - 1))
        tok = cx.rotm.commit(mmq)
        prev_group_done = tok
        nc.vector.wait_ge(cx.rotm.sp.h, tok)
        v1 = nc.vector.tensor_scalar(out=cx.stat[0:1, tb * 512:(tb + 1) * 512], in0=bank_s[0:1, :],
                                     scalar1=1.0 / D, scalar2=None, op0=ALU.mult)
        cx.rotm.release(v1)
        v2 = nc.vector.tensor_scalar(out=cx.stat[0:1, D + tb * 512:D + (tb + 1) * 512], in0=bank_q[0:1, :],
                                     scalar1=1.0 / D, scalar2=None, op0=ALU.mult)
        cx.rotm.release(v2)
    a = None
    for sb in range(D // 512):
        sl = slice(D + sb * 512, D + (sb + 1) * 512)
        nc.vector.tensor_tensor(out=cx.stat_sc[0:1, :], in0=cx.stat[0:1, sb * 512:(sb + 1) * 512],
                                in1=cx.stat[0:1, sb * 512:(sb + 1) * 512], op=ALU.mult)
        v = nc.vector.tensor_tensor(out=cx.stat[0:1, sl], in0=cx.stat[0:1, sl],
                                    in1=cx.stat_sc[0:1, :], op=ALU.subtract)
        ev = cx.s_ev.inc(v, 1)
        nc.scalar.wait_ge(cx.s_ev.h, ev)
        nc.scalar.activation(out=cx.stat[0:1, sl], in_=cx.stat[0:1, sl],
                             func=AF.Ln, bias=cx.consts[0:1, 2:3], scale=1.0)
        a = nc.scalar.activation(out=cx.stat[0:1, sl], in_=cx.stat[0:1, sl],
                                 func=AF.Exp, bias=0.0, scale=-0.5)
    cx.s_ac.inc(a, 1)
    nc.vector.wait_ge(cx.s_ac.h, cx.s_ac.n)
    v = nc.vector.tensor_copy(out=cx.stat_bf[0:1, 0:2 * D], in_=cx.stat[0:1, 0:2 * D])
    cx.s_ev.inc(v, 1)
    if st_out is not None:
        nc.sync.wait_ge(cx.s_ac.h, cx.s_ac.n)
        dma(nc.sync, st_out[0:1, :], cx.stat[0:1, :], cx.s_st)
    join_all(cx, ("vector", "scalar", "tensor", "sync"))
    stsc.__exit__(None, None, None)
    del cx.stat, cx.stat_sc
    if not center:
        return
    nc.tensor.wait_ge(cx.s_ev.h, cx.s_ev.n)
    for tb in range(TOK // 512):
        bank = cx.rotm.acquire(nc.tensor)
        cx.rotm.acquired()
        mm = nc.tensor.matmul(out=bank[:, :], lhsT=cx.onesrow[0:1, :],
                              rhs=cx.stat_bf[0:1, tb * 512:(tb + 1) * 512], start=True, stop=True)
        tok = cx.rotm.commit(mm)
        nc.vector.wait_ge(cx.rotm.sp.h, tok)
        last = None
        for ck in range(NCK):
            last = nc.vector.tensor_tensor(out=buf[:, ck, tb * 512:(tb + 1) * 512],
                                           in0=buf[:, ck, tb * 512:(tb + 1) * 512],
                                           in1=bank[:, :], op=ALU.subtract)
        cx.rotm.release(last)


def emit_bcast_row(cx, src_row, dst, n):
    nc = cx.P.nc
    assert n <= 512
    bank = cx.rotm.acquire(nc.tensor)
    cx.rotm.acquired()
    mm = nc.tensor.matmul(out=bank[:, 0:n], lhsT=cx.onesrow[0:1, :], rhs=src_row, start=True, stop=True)
    tok = cx.rotm.commit(mm)
    nc.vector.wait_ge(cx.rotm.sp.h, tok)
    v = nc.vector.tensor_copy(out=dst, in_=bank[:, 0:n])
    cx.rotm.release(v)


def emit_load_x(cx):
    nc = cx.P.nc
    P = cx.P
    with P.scope():
        utm = P.sbuf("utm", [128, 16, D], BF16)
        for ti in range(16):
            dma(nc.gpsimd, utm[:, ti, :], cx.io["u"][ti * 128:(ti + 1) * 128, :], cx.s_ld)
        nc.tensor.wait_ge(cx.s_ld.h, cx.s_ld.n)
        pairs = []
        for ck in range(NCK):
            for ti in range(16):
                pairs.append((utm[:, ti, ck * 128:(ck + 1) * 128],
                              cx.xc[:, ck, ti * 128:(ti + 1) * 128]))
        emit_transpose_blocks(cx, pairs)
        join_all(cx)


def emit_store_out(cx, src_buf, out_hbm):
    nc = cx.P.nc
    P = cx.P
    with P.scope():
        otm = [P.sbuf("otm", [128, NCK, 128], F32) for _ in range(2)]
        st_free = [0, 0]
        rot = cx.rott
        for ti in range(16):
            o = otm[ti % 2]
            if st_free[ti % 2]:
                nc.vector.wait_ge(cx.s_st.h, st_free[ti % 2])
            for g0 in range(0, NCK, 4):
                bank = rot.acquire(nc.tensor)
                rot.acquired()
                last = None
                for k in range(4):
                    ck = g0 + k
                    last = nc.tensor.transpose(out=bank[:, k * 128:(k + 1) * 128],
                                               in_=src_buf[:, ck, ti * 128:(ti + 1) * 128],
                                               identity=cx.ident[:, :])
                tok = rot.commit(last)
                nc.vector.wait_ge(rot.sp.h, tok)
                lastv = None
                for k in range(4):
                    ck = g0 + k
                    lastv = nc.vector.tensor_copy(out=o[:, ck, :], in_=bank[:, k * 128:(k + 1) * 128])
                rot.release(lastv)
            ev_mark = cx.s_ev.n
            nc.sync.wait_ge(cx.s_ev.h, ev_mark)
            dma(nc.sync, out_hbm[ti * 128:(ti + 1) * 128, :], o[:, :, :].rearrange("p c t -> p (c t)"), cx.s_st)
            st_free[ti % 2] = cx.s_st.n
        nc.sync.wait_ge(cx.s_st.h, cx.s_st.n)
        join_all(cx)


def emit_row_to_col(cx, row_bf, col_dst, n):
    nc = cx.P.nc
    rot = cx.rott
    for g0 in range(0, n, 4):
        cnt = min(4, n - g0)
        bank = rot.acquire(nc.tensor)
        rot.acquired()
        for k in range(cnt):
            i = g0 + k
            nc.tensor.transpose(out=bank[:, k * 2:k * 2 + 1],
                                in_=row_bf[0:1, i * 128:(i + 1) * 128],
                                identity=cx.ident[0:1, 0:1])
        last = nc.tensor.transpose(out=bank[:, 384:512], in_=cx.ident[:, :],
                                   identity=cx.ident[:, :])
        tok = rot.commit(last)
        nc.vector.wait_ge(rot.sp.h, tok)
        lastv = None
        for k in range(cnt):
            lastv = nc.vector.tensor_copy(out=col_dst[:, g0 + k:g0 + k + 1],
                                          in_=bank[:, k * 2:k * 2 + 1])
        rot.release(lastv)


def emit_ffn0(cx, gt, dbg=None):
    nc = cx.P.nc
    P = cx.P
    io = cx.io
    NE = 8
    with P.scope():
        wr = P.sbuf("wrout", [128, NCK, 8], BF16)
        for ck in range(NCK):
            dma(nc.gpsimd, wr[:, ck, :], io["w_router"][ck * 128:(ck + 1) * 128, :], cx.s_ld)
        rcol = P.sbuf("rcol", [128, 16], F32)
        r2w = P.sbuf("r2w", [128, 16], F32)
        wr2 = P.sbuf("wr2", [128, 16, 8], F32)
        lg = P.sbuf("lg", [128, 16, 8], F32)
        tmp = P.sbuf("rtmp", [128, 16], F32)
        tmp8 = P.sbuf("rtmp8", [128, 8], F32)
        thr8 = P.sbuf("rthr8", [128, 8], F32)
        dbg_rt = P.sbuf("dbg_rt", [128, 32], F32)
        m1 = P.sbuf("m1", [128, 1], F32)
        m1n = P.sbuf("m1n", [128, 1], F32)
        m2 = P.sbuf("m2", [128, 1], F32)
        s12 = P.sbuf("s12", [128, 1], F32)
        emit_row_to_col(cx, cx.stat_bf[0:1, D:2 * D], rcol, 16)
        nc.vector.wait_ge(cx.s_ev.h, cx.s_ev.n)
        V(cx, nc.vector.tensor_tensor(out=r2w[:, :], in0=rcol[:, :], in1=rcol[:, :], op=ALU.mult))
        nc.tensor.wait_ge(cx.s_ld.h, cx.s_ld.n)
        for tb in range(4):
            bank = cx.rotm.acquire(nc.tensor)
            cx.rotm.acquired()
            mml = None
            for ck in range(NCK):
                mml = nc.tensor.matmul(out=bank[0:8, :], lhsT=wr[:, ck, :],
                                       rhs=cx.xc[:, ck, tb * 512:(tb + 1) * 512],
                                       start=(ck == 0), stop=(ck == NCK - 1))
            tokp = cx.rotm.commit(mml)
            nc.vector.wait_ge(cx.rotm.sp.h, tokp)
            if tb == 0:
                lgbf = P.sbuf("lgbf", [8, 512], BF16)
                lgbf_free = 0
            if lgbf_free:
                nc.vector.wait_ge(cx.s_mm.h, lgbf_free)
            v = nc.vector.tensor_copy(out=lgbf[0:8, :], in_=bank[0:8, :])
            cx.rotm.release(v)
            ev_mark = cx.s_ev.n
            tbank = cx.rott.acquire(nc.tensor)
            cx.rott.acquired()
            nc.tensor.wait_ge(cx.s_ev.h, ev_mark)
            for k in range(4):
                nc.tensor.transpose(out=tbank[:, k * 8:k * 8 + 8],
                                    in_=lgbf[0:8, k * 128:(k + 1) * 128],
                                    identity=cx.ident[0:8, 0:8])
            last = nc.tensor.transpose(out=tbank[:, 384:512], in_=cx.ident[:, :],
                                       identity=cx.ident[:, :])
            tok2 = cx.rott.commit(last)
            lgbf_free = tok2
            nc.vector.wait_ge(cx.rott.sp.h, tok2)
            for k in range(4):
                ti = tb * 4 + k
                V(cx, nc.vector.tensor_scalar(out=lg[:, ti, :], in0=tbank[:, k * 8:k * 8 + 8],
                                               scalar1=rcol[:, ti:ti + 1], scalar2=None, op0=ALU.mult))
                lastv = nc.vector.tensor_tensor(out=lg[:, ti, :], in0=lg[:, ti, :],
                                                in1=cx.tieb[:, 0:8], op=ALU.add)
                if k < 3:
                    V(cx, lastv)
            cx.rott.release(lastv)
            nc.vector.wait_ge(cx.s_ev.h, cx.s_ev.n)
        nc.vector.wait_ge(cx.s_ev.h, cx.s_ev.n)
        for ti in range(16):
            l = lg[:, ti, :]
            V(cx, nc.vector.tensor_reduce(out=m1[:, :], in_=l, axis=mybir.AxisListType.X, op=ALU.max))
            ev = V(cx, nc.vector.tensor_scalar(out=thr8[:, :], in0=l, scalar1=m1[:, 0:1], scalar2=-1.0,
                                               op0=ALU.subtract, op1=ALU.mult))
            nc.scalar.wait_ge(cx.s_ev.h, ev)
            ac = cx.s_ac.inc(nc.scalar.activation(out=thr8[:, :], in_=thr8[:, :], func=AF.Sign), 1)
            nc.vector.wait_ge(cx.s_ac.h, ac)
            V(cx, nc.vector.tensor_scalar(out=thr8[:, :], in0=thr8[:, :], scalar1=-1.0, scalar2=1.0,
                                          op0=ALU.mult, op1=ALU.add))
            V(cx, nc.vector.tensor_scalar(out=thr8[:, :], in0=thr8[:, :], scalar1=1.0, scalar2=None,
                                          op0=ALU.min))
            V(cx, nc.vector.tensor_scalar(out=tmp8[:, :], in0=l, scalar1=-NEGBIG, scalar2=None,
                                          op0=ALU.add))
            V(cx, nc.vector.tensor_tensor(out=tmp8[:, :], in0=tmp8[:, :], in1=thr8[:, :], op=ALU.mult))
            V(cx, nc.vector.tensor_tensor(out=tmp8[:, :], in0=l, in1=tmp8[:, :], op=ALU.subtract))
            V(cx, nc.vector.tensor_reduce(out=m2[:, :], in_=tmp8[:, :], axis=mybir.AxisListType.X, op=ALU.max))
            V(cx, nc.vector.tensor_scalar(out=s12[:, :], in0=m1[:, :], scalar1=m2[:, 0:1], scalar2=None,
                                          op0=ALU.subtract))
            V(cx, nc.vector.tensor_scalar(out=thr8[:, :], in0=l, scalar1=m2[:, 0:1], scalar2=-1.0,
                                          op0=ALU.subtract, op1=ALU.mult))
            ev = V(cx, nc.vector.tensor_scalar(out=tmp8[:, :], in0=l, scalar1=m1[:, 0:1], scalar2=None,
                                               op0=ALU.subtract))
            nc.scalar.wait_ge(cx.s_ev.h, ev)
            A(cx, nc.scalar.activation(out=s12[:, :], in_=s12[:, :], func=AF.Sigmoid))
            A(cx, nc.scalar.activation(out=thr8[:, :], in_=thr8[:, :], func=AF.Sign))
            ac = cx.s_ac.inc(nc.scalar.activation(out=tmp8[:, :], in_=tmp8[:, :], func=AF.Exp), 1)
            nc.vector.wait_ge(cx.s_ac.h, ac)
            V(cx, nc.vector.tensor_scalar(out=thr8[:, :], in0=thr8[:, :], scalar1=-1.0, scalar2=1.0,
                                          op0=ALU.mult, op1=ALU.add))
            V(cx, nc.vector.tensor_scalar(out=wr2[:, ti, :], in0=thr8[:, :], scalar1=1.0, scalar2=None,
                                          op0=ALU.min))
            V(cx, nc.vector.tensor_tensor(out=wr2[:, ti, :], in0=wr2[:, ti, :], in1=tmp8[:, :], op=ALU.mult))
            V(cx, nc.vector.tensor_scalar(out=wr2[:, ti, :], in0=wr2[:, ti, :], scalar1=s12[:, 0:1],
                                          scalar2=r2w[:, ti:ti + 1], op0=ALU.mult, op1=ALU.mult))
            if dbg is not None and ti == 0:
                nc.vector.tensor_copy(out=dbg_rt[:, 0:1], in_=m1[:, :])
                nc.vector.tensor_copy(out=dbg_rt[:, 1:2], in_=m2[:, :])
                nc.vector.tensor_copy(out=dbg_rt[:, 2:3], in_=s12[:, :])
                nc.vector.tensor_copy(out=dbg_rt[:, 24:32], in_=tmp8[:, :])
        v = nc.vector.tensor_copy(out=m1[:, :], in_=m1[:, :])
        routing_done = cx.s_ev.inc(v, 1)
        if dbg is not None:
            nc.sync.wait_ge(cx.s_ev.h, cx.s_ev.n)
            dma(nc.sync, dbg[0][:, :], wr2[:, :, :].rearrange("p a b -> p (a b)"), cx.s_st)
            dma(nc.sync, dbg[1][:, :], lg[:, :, :].rearrange("p a b -> p (a b)"), cx.s_st)
            dma(nc.sync, dbg[3][:, :], dbg_rt[:, :], cx.s_st)

        wfc = [P.sbuf("wfc", [128, NCK, 128], BF16) for _ in range(2)]
        wpj = [P.sbuf("wpj", [128, NCK, 256], BF16) for _ in range(2)]
        hh = P.sbuf("hh", [128, NCK, 1024], BF16)
        etmp = [P.sbuf("etmp", [128, 512], BF16) for _ in range(2)]
        etmp_free = [0, 0]
        wfc_free = [0, 0]
        wpj_free = [0, 0]
        nc.vector.wait_ge(cx.s_ev.h, routing_done)
        for e in range(8):
            for half in range(2):
                hoff = half * 1024
                hmark = None
                for dk in range(NCK):
                    slot = dk % 2
                    if wfc_free[slot]:
                        nc.gpsimd.wait_ge(cx.s_mm.h, wfc_free[slot])
                    dma(nc.gpsimd, wfc[slot][:, :, :],
                        io["w_fc"][e].rearrange("(ck p) d -> p ck d", p=128)[:, :, dk * 128:(dk + 1) * 128],
                        cx.s_ld)
                    ldmark = cx.s_ld.n
                    nc.tensor.wait_ge(cx.s_ld.h, ldmark)
                    for tb2 in range(2):
                        bank = cx.rot.acquire(nc.tensor)
                        cx.rot.acquired()
                        mm = None
                        for ck in range(NCK):
                            mm = nc.tensor.matmul(out=bank[:, :], lhsT=wfc[slot][:, ck, :],
                                                  rhs=cx.xc[:, ck, hoff + tb2 * 512: hoff + (tb2 + 1) * 512],
                                                  start=(ck == 0), stop=(ck == NCK - 1))
                        wfc_free[slot] = tokm = cx.rot.commit(mm)
                        nc.vector.wait_ge(cx.rot.sp.h, tokm)
                        if etmp_free[tb2]:
                            nc.vector.wait_ge(cx.s_ac.h, etmp_free[tb2])
                        vv = nc.vector.tensor_scalar(out=etmp[tb2][:, :], in0=bank[:, :], scalar1=0.0,
                                                     scalar2=None, op0=ALU.max)
                        cx.rot.release(vv)
                        ev = cx.s_ev.n
                        nc.scalar.wait_ge(cx.s_ev.h, ev)
                        a = nc.scalar.activation(out=hh[:, dk, tb2 * 512:(tb2 + 1) * 512],
                                                 in_=etmp[tb2][:, :], func=AF.Square)
                        hmark = cx.s_ac.inc(a, 1)
                        etmp_free[tb2] = hmark
                if dbg is not None and e == 0 and half == 0:
                    nc.sync.wait_ge(cx.s_ac.h, hmark)
                    dma(nc.sync, dbg[2][:, :], hh[:, :, :].rearrange("p a b -> p (a b)"), cx.s_st)
                for cb in range(8):
                    slot = cb % 2
                    if wpj_free[slot]:
                        nc.gpsimd.wait_ge(cx.s_mm.h, wpj_free[slot])
                    dma(nc.gpsimd, wpj[slot][:, :, :],
                        io["w_proj"][e].rearrange("(dk p) c -> p dk c", p=128)[:, :, cb * 256:(cb + 1) * 256],
                        cx.s_ld)
                    ldmark = cx.s_ld.n
                    nc.tensor.wait_ge(cx.s_ld.h, ldmark)
                    if cb == 0:
                        nc.tensor.wait_ge(cx.s_ac.h, hmark)
                    lastmm = None
                    for tk in range(8):
                        ti = half * 8 + tk
                        bank = cx.rot.acquire(nc.tensor)
                        cx.rot.acquired()
                        mm = None
                        for dk in range(NCK):
                            mm = nc.tensor.matmul(out=bank[:, 0:256], lhsT=hh[:, dk, tk * 128:(tk + 1) * 128],
                                                  rhs=wpj[slot][:, dk, :],
                                                  start=(dk == 0), stop=(dk == NCK - 1))
                        tokm = cx.rot.commit(mm)
                        lastmm = tokm
                        nc.vector.wait_ge(cx.rot.sp.h, tokm)
                        if e == 0:
                            vv = nc.vector.tensor_scalar(out=gt[:, ti, cb * 256:(cb + 1) * 256],
                                                         in0=bank[:, 0:256], scalar1=wr2[:, ti, e:e + 1],
                                                         scalar2=None, op0=ALU.mult)
                        else:
                            vv = nc.vector.scalar_tensor_tensor(out=gt[:, ti, cb * 256:(cb + 1) * 256],
                                                                in0=bank[:, 0:256], scalar=wr2[:, ti, e:e + 1],
                                                                in1=gt[:, ti, cb * 256:(cb + 1) * 256],
                                                                op0=ALU.mult, op1=ALU.add)
                        cx.rot.release(vv)
                    wpj_free[slot] = lastmm
        join_all(cx)


def build_p_ffn0(debug=False):
    P = Prog()
    nc = P.nc
    io = {
        "xc_in": P.dram("xc_in", [128, NCK * TOK], BF16, kind="ExternalInput").ap(),
        "st_in": P.dram("st_in", [1, 2 * D], F32, kind="ExternalInput").ap(),
        "w_router": P.dram("w_router", [D, 8], F32, kind="ExternalInput").ap(),
        "w_fc": P.dram("w_fc", [8, D, D], F32, kind="ExternalInput").ap(),
        "w_proj": P.dram("w_proj", [8, D, D], F32, kind="ExternalInput").ap(),
        "consts": P.dram("consts", [128, 8], F32, kind="ExternalInput").ap(),
        "onesrow": P.dram("onesrow", [1, 128], BF16, kind="ExternalInput").ap(),
        "onescol": P.dram("onescol", [128, 1], BF16, kind="ExternalInput").ap(),
        "ident": P.dram("ident", [128, 128], BF16, kind="ExternalInput").ap(),
        "ident32": P.dram("ident32", [128, 128], F32, kind="ExternalInput").ap(),
        "tieb": P.dram("tieb", [128, 16], F32, kind="ExternalInput").ap(),
    }
    g_out = P.dram("g_out", [128, NCK * TOK], BF16, kind="ExternalOutput").ap()
    if debug:
        dbg_wr2 = P.dram("dbg_wr2", [128, 16 * 8], F32, kind="ExternalOutput").ap()
        dbg_lg = P.dram("dbg_lg", [128, 16 * 8], F32, kind="ExternalOutput").ap()
        dbg_hh = P.dram("dbg_hh", [128, NCK * 1024], BF16, kind="ExternalOutput").ap()
        dbg_rt_o = P.dram("dbg_rt", [128, 32], F32, kind="ExternalOutput").ap()
    with P.stack:
        cx = alloc_common(P, io)
        dma(nc.sync, cx.xc[:, :, :].rearrange("p c t -> p (c t)"), io["xc_in"][:, :], cx.s_ld)
        dma(nc.gpsimd, cx.stat_bf[0:1, :], io["st_in"][0:1, :], cx.s_ld)
        join_all(cx)
        gt = P.sbuf("gt", [128, 16, 2048], BF16)
        emit_ffn0(cx, gt, dbg=(locals().get("dbg_wr2"), locals().get("dbg_lg"), locals().get("dbg_hh"), locals().get("dbg_rt_o")) if debug else None)
        nc.sync.wait_ge(cx.s_ev.h, cx.s_ev.n)
        dma(nc.sync, g_out[:, :], gt[:, :, :].rearrange("p c t -> p (c t)"), cx.s_st)
        nc.sync.wait_ge(cx.s_st.h, cx.s_st.n)
    return P


def emit_ffn1(cx, gt, dbg=None):
    nc = cx.P.nc
    P = cx.P
    io = cx.io
    BIG = -NEGBIG
    with P.scope():
        rcol = P.sbuf("rcol1", [128, 16], F32)
        ebias = P.sbuf("ebias1", [128, 16, 16], F32)
        gwt = P.sbuf("gwt1", [128, NCK, 16], BF16)
        lgb = P.sbuf("lgb1", [128, 16, 16], F32)
        sc = P.sbuf("sc1", [128, 16, 16], F32)
        cw = P.sbuf("cw1", [128, 16, 17], F32)
        gs = P.sbuf("gs1", [128, 4], F32)
        m1 = P.sbuf("m1f", [128, 1], F32)
        m2 = P.sbuf("m2f", [128, 1], F32)
        t4 = P.sbuf("t4f", [128, 4], F32)
        t4b = P.sbuf("t4bf", [128, 4], F32)
        t16 = P.sbuf("t16f", [128, 16], F32)
        t16b = P.sbuf("t16bf", [128, 16], F32)
        sel = P.sbuf("self", [128, 16], F32)
        emask = P.sbuf("emaskf", [128, 16], F32)
        onesq = P.sbuf("onesqf", [128, 4], F32)
        bAt = P.sbuf("bAt1", [128, 4, 512], BF16)
        dma(nc.gpsimd, gwt[:, :, :],
            io["gwT"].rearrange("(ck p) e -> p ck e", p=128)[:, :, :], cx.s_ld)
        dma(nc.sync, ebias[:, :, :].rearrange("p a b -> p (a b)"), io["ebias256"][:, :], cx.s_ld)
        nc.vector.wait_ge(cx.s_ld.h, cx.s_ld.n)
        V(cx, nc.vector.memset(onesq[:, :], 1.0))
        emit_row_to_col(cx, cx.stat_bf[0:1, D:2 * D], rcol, 16)
        nc.vector.wait_ge(cx.s_ev.h, cx.s_ev.n)
        for sub in range(4):
            emit_bcast_row(cx, cx.stat_bf[0:1, D + sub * 512:D + (sub + 1) * 512], bAt[:, sub, :], 512)
        nc.tensor.wait_ge(cx.s_ld.h, cx.s_ld.n)
        nc.tensor.wait_ge(cx.s_ev.h, cx.s_ev.n)
        for ti in range(16):
            bank = cx.rotm.acquire(nc.tensor)
            cx.rotm.acquired()
            mml = None
            for ck in range(NCK):
                mml = nc.tensor.matmul(out=bank[:, 0:16], lhsT=cx.xc[:, ck, ti * 128:(ti + 1) * 128],
                                       rhs=gwt[:, ck, :], start=(ck == 0), stop=(ck == NCK - 1))
            tokp = cx.rotm.commit(mml)
            nc.vector.wait_ge(cx.rotm.sp.h, tokp)
            v = nc.vector.tensor_scalar(out=lgb[:, ti, :], in0=bank[:, 0:16],
                                        scalar1=rcol[:, ti:ti + 1], scalar2=None, op0=ALU.mult)
            cx.rotm.release(v)
        ev = cx.s_ev.n
        nc.scalar.wait_ge(cx.s_ev.h, ev)
        ac = A(cx, nc.scalar.activation(out=sc[:, :, :].rearrange("p a b -> p (a b)"),
                                        in_=lgb[:, :, :].rearrange("p a b -> p (a b)"), func=AF.Sigmoid))
        nc.vector.wait_ge(cx.s_ac.h, ac)
        V(cx, nc.vector.tensor_tensor(out=lgb[:, :, :].rearrange("p a b -> p (a b)"),
                                      in0=sc[:, :, :].rearrange("p a b -> p (a b)"),
                                      in1=ebias[:, :, :].rearrange("p a b -> p (a b)"), op=ALU.add))
        for ti in range(16):
            V(cx, nc.vector.tensor_tensor(out=lgb[:, ti, :], in0=lgb[:, ti, :],
                                          in1=cx.tieb[:, 0:16], op=ALU.add))
        nc.vector.wait_ge(cx.s_ev.h, cx.s_ev.n)
        for ti in range(16):
            b16 = lgb[:, ti, :]
            for g in range(4):
                l4 = b16[:, g * 4:(g + 1) * 4]
                V(cx, nc.vector.tensor_reduce(out=m1[:, :], in_=l4, axis=mybir.AxisListType.X, op=ALU.max))
                ev = V(cx, nc.vector.tensor_scalar(out=t4[:, :], in0=l4, scalar1=m1[:, 0:1],
                                                   scalar2=None, op0=ALU.subtract))
                nc.scalar.wait_ge(cx.s_ev.h, ev)
                ac = cx.s_ac.inc(nc.scalar.activation(out=t4[:, :], in_=t4[:, :], func=AF.Sign), 1)
                nc.vector.wait_ge(cx.s_ac.h, ac)
                V(cx, nc.vector.tensor_scalar(out=t4[:, :], in0=t4[:, :], scalar1=1.0,
                                              scalar2=None, op0=ALU.add))
                V(cx, nc.vector.tensor_scalar(out=t4b[:, :], in0=l4, scalar1=BIG, scalar2=None,
                                              op0=ALU.add))
                V(cx, nc.vector.tensor_tensor(out=t4b[:, :], in0=t4b[:, :], in1=t4[:, :], op=ALU.mult))
                V(cx, nc.vector.tensor_tensor(out=t4b[:, :], in0=l4, in1=t4b[:, :], op=ALU.subtract))
                V(cx, nc.vector.tensor_reduce(out=m2[:, :], in_=t4b[:, :], axis=mybir.AxisListType.X,
                                              op=ALU.max))
                V(cx, nc.vector.tensor_scalar(out=gs[:, g:g + 1], in0=m1[:, :], scalar1=m2[:, 0:1],
                                              scalar2=None, op0=ALU.add))
            V(cx, nc.vector.tensor_reduce(out=m1[:, :], in_=gs[:, :], axis=mybir.AxisListType.X, op=ALU.max))
            ev = V(cx, nc.vector.tensor_scalar(out=t4[:, :], in0=gs[:, :], scalar1=m1[:, 0:1],
                                               scalar2=None, op0=ALU.subtract))
            nc.scalar.wait_ge(cx.s_ev.h, ev)
            ac = cx.s_ac.inc(nc.scalar.activation(out=t4[:, :], in_=t4[:, :], func=AF.Sign), 1)
            nc.vector.wait_ge(cx.s_ac.h, ac)
            V(cx, nc.vector.tensor_scalar(out=t4[:, :], in0=t4[:, :], scalar1=1.0, scalar2=None,
                                          op0=ALU.add))
            V(cx, nc.vector.tensor_scalar(out=t4b[:, :], in0=gs[:, :], scalar1=BIG, scalar2=None,
                                          op0=ALU.add))
            V(cx, nc.vector.tensor_tensor(out=t4b[:, :], in0=t4b[:, :], in1=t4[:, :], op=ALU.mult))
            V(cx, nc.vector.tensor_tensor(out=t4b[:, :], in0=gs[:, :], in1=t4b[:, :], op=ALU.subtract))
            ev = V(cx, nc.vector.tensor_reduce(out=m2[:, :], in_=t4b[:, :], axis=mybir.AxisListType.X,
                                               op=ALU.max))
            V(cx, nc.vector.tensor_scalar(out=t4[:, :], in0=gs[:, :], scalar1=m2[:, 0:1],
                                          scalar2=None, op0=ALU.subtract))
            ev = cx.s_ev.n
            nc.scalar.wait_ge(cx.s_ev.h, ev)
            ac = cx.s_ac.inc(nc.scalar.activation(out=t4[:, :], in_=t4[:, :], func=AF.Sign), 1)
            nc.vector.wait_ge(cx.s_ac.h, ac)
            V(cx, nc.vector.tensor_scalar(out=t4[:, :], in0=t4[:, :], scalar1=1.0, scalar2=1.0,
                                          op0=ALU.add, op1=ALU.min))
            for g in range(4):
                V(cx, nc.vector.tensor_scalar(out=emask[:, g * 4:(g + 1) * 4], in0=onesq[:, :],
                                              scalar1=t4[:, g:g + 1], scalar2=None, op0=ALU.mult))
            V(cx, nc.vector.tensor_scalar(out=t16b[:, :], in0=emask[:, :], scalar1=BIG,
                                          scalar2=-BIG, op0=ALU.mult, op1=ALU.add))
            V(cx, nc.vector.tensor_tensor(out=t16[:, :], in0=b16, in1=emask[:, :], op=ALU.mult))
            V(cx, nc.vector.tensor_tensor(out=t16[:, :], in0=t16[:, :], in1=t16b[:, :], op=ALU.add))
            for it in range(4):
                V(cx, nc.vector.tensor_reduce(out=m1[:, :], in_=t16[:, :], axis=mybir.AxisListType.X,
                                              op=ALU.max))
                ev = V(cx, nc.vector.tensor_scalar(out=t16b[:, :], in0=t16[:, :], scalar1=m1[:, 0:1],
                                                   scalar2=None, op0=ALU.subtract))
                nc.scalar.wait_ge(cx.s_ev.h, ev)
                ac = cx.s_ac.inc(nc.scalar.activation(out=t16b[:, :], in_=t16b[:, :], func=AF.Sign), 1)
                nc.vector.wait_ge(cx.s_ac.h, ac)
                V(cx, nc.vector.tensor_scalar(out=t16b[:, :], in0=t16b[:, :], scalar1=1.0,
                                              scalar2=None, op0=ALU.add))
                if it == 0:
                    V(cx, nc.vector.tensor_copy(out=sel[:, :], in_=t16b[:, :]))
                else:
                    V(cx, nc.vector.tensor_tensor(out=sel[:, :], in0=sel[:, :], in1=t16b[:, :], op=ALU.add))
                if it < 3:
                    V(cx, nc.vector.tensor_scalar(out=t16b[:, :], in0=t16b[:, :], scalar1=BIG,
                                                  scalar2=None, op0=ALU.mult))
                    V(cx, nc.vector.tensor_tensor(out=t16[:, :], in0=t16[:, :], in1=t16b[:, :],
                                                  op=ALU.subtract))
            V(cx, nc.vector.tensor_tensor(out=t16[:, :], in0=sel[:, :], in1=sc[:, ti, :], op=ALU.mult))
            V(cx, nc.vector.tensor_reduce(out=m1[:, :], in_=t16[:, :], axis=mybir.AxisListType.X,
                                          op=ALU.add))
            ev = V(cx, nc.vector.tensor_scalar(out=m1[:, :], in0=m1[:, :], scalar1=1e-20, scalar2=None,
                                               op0=ALU.add))
            V(cx, nc.vector.reciprocal(out=m2[:, :], in_=m1[:, :]))
            V(cx, nc.vector.tensor_scalar(out=m2[:, :], in0=m2[:, :], scalar1=2.5,
                                          scalar2=rcol[:, ti:ti + 1], op0=ALU.mult, op1=ALU.mult))
            V(cx, nc.vector.tensor_scalar(out=cw[:, ti, 0:16], in0=t16[:, :], scalar1=m2[:, 0:1],
                                          scalar2=None, op0=ALU.mult))
            V(cx, nc.vector.tensor_copy(out=cw[:, ti, 16:17], in_=rcol[:, ti:ti + 1]))
            if dbg is not None and ti == 0:
                nc.sync.wait_ge(cx.s_ev.h, cx.s_ev.n)
                dma(nc.sync, dbg[0][:, :], sel[:, :], cx.s_st)
                dma(nc.sync, dbg[1][:, :], sc[:, :, :].rearrange("p a b -> p (a b)"), cx.s_st)
                nc.vector.wait_ge(cx.s_st.h, cx.s_st.n)
        v = nc.vector.tensor_copy(out=m1[:, :], in_=m1[:, :])
        routing_done = cx.s_ev.inc(v, 1)
        nc.vector.wait_ge(cx.s_ev.h, routing_done)
        if dbg is not None:
            nc.sync.wait_ge(cx.s_ev.h, cx.s_ev.n)
            dma(nc.sync, dbg[2][:, :], cw[:, :, :].rearrange("p a b -> p (a b)"), cx.s_st)
            nc.vector.wait_ge(cx.s_st.h, cx.s_st.n)

        wga = [P.sbuf("wga", [128, NCK, 128], BF16) for _ in range(2)]
        wup = [P.sbuf("wup", [128, NCK, 128], BF16) for _ in range(2)]
        wdn = [P.sbuf("wdn", [128, 8, 512], BF16) for _ in range(2)]
        hh1 = P.sbuf("hh1", [128, 8, 1024], BF16)
        et = [P.sbuf("et1", [128, 512], F32) for _ in range(2)]
        et_free = [0, 0]
        w_free = [0, 0]
        wdn_free = [0, 0]
        for e in range(17):
            if e < 16:
                gsrc, usrc, dsrc = io["gT"], io["uT"], io["dT"]
                goff, doff = e * 1024, e * 2048
            else:
                gsrc, usrc, dsrc = io["shgT"], io["shuT"], io["shdT"]
                goff, doff = 0, 0
            for half in range(2):
                hoff = half * 1024
                hmark = None
                for dk in range(8):
                    slot = dk % 2
                    if w_free[slot]:
                        nc.gpsimd.wait_ge(cx.s_mm.h, w_free[slot])
                    dma(nc.gpsimd, wga[slot][:, :, :],
                        gsrc.rearrange("(ck p) f -> p ck f", p=128)[:, :, goff + dk * 128:goff + (dk + 1) * 128],
                        cx.s_ld)
                    dma(nc.gpsimd, wup[slot][:, :, :],
                        usrc.rearrange("(ck p) f -> p ck f", p=128)[:, :, goff + dk * 128:goff + (dk + 1) * 128],
                        cx.s_ld)
                    ldmark = cx.s_ld.n
                    nc.tensor.wait_ge(cx.s_ld.h, ldmark)
                    for tb2 in range(2):
                        sub = half * 2 + tb2
                        t0 = sub * 512
                        bankg = cx.rot.acquire(nc.tensor)
                        cx.rot.acquired()
                        mm = None
                        for ck in range(NCK):
                            mm = nc.tensor.matmul(out=bankg[:, :], lhsT=wga[slot][:, ck, :],
                                                  rhs=cx.xc[:, ck, t0:t0 + 512],
                                                  start=(ck == 0), stop=(ck == NCK - 1))
                        tokg = cx.rot.commit(mm)
                        banku = cx.rot.acquire(nc.tensor)
                        cx.rot.acquired()
                        mm = None
                        for ck in range(NCK):
                            mm = nc.tensor.matmul(out=banku[:, :], lhsT=wup[slot][:, ck, :],
                                                  rhs=cx.xc[:, ck, t0:t0 + 512],
                                                  start=(ck == 0), stop=(ck == NCK - 1))
                        toku = cx.rot.commit(mm)
                        nc.vector.wait_ge(cx.rot.sp.h, toku)
                        if et_free[tb2]:
                            nc.vector.wait_ge(cx.s_ac.h, et_free[tb2])
                        vv = nc.vector.tensor_tensor(out=et[tb2][:, :], in0=bankg[:, :],
                                                     in1=bAt[:, sub, :], op=ALU.mult)
                        cx.rot.release(vv)
                        ev = cx.s_ev.n
                        nc.scalar.wait_ge(cx.s_ev.h, ev)
                        acv = A(cx, nc.scalar.activation(out=et[tb2][:, :], in_=et[tb2][:, :], func=AF.Silu))
                        et_free[tb2] = acv
                        nc.vector.wait_ge(cx.s_ac.h, acv)
                        vv = nc.vector.tensor_tensor(out=hh1[:, dk, tb2 * 512:(tb2 + 1) * 512],
                                                     in0=et[tb2][:, :], in1=banku[:, :], op=ALU.mult)
                        cx.rot.release(vv)
                        hmark = cx.s_ev.n
                    w_free[slot] = cx.s_mm.n
                nc.tensor.wait_ge(cx.s_ev.h, hmark)
                for cb in range(4):
                    slot = cb % 2
                    if wdn_free[slot]:
                        nc.gpsimd.wait_ge(cx.s_mm.h, wdn_free[slot])
                    dma(nc.gpsimd, wdn[slot][:, :, :],
                        dsrc.rearrange("(dk p) c -> p dk c", p=128)[:, :, doff + cb * 512:doff + (cb + 1) * 512],
                        cx.s_ld)
                    ldmark = cx.s_ld.n
                    nc.tensor.wait_ge(cx.s_ld.h, ldmark)
                    lastmm = None
                    for tk in range(8):
                        ti = half * 8 + tk
                        bank = cx.rot.acquire(nc.tensor)
                        cx.rot.acquired()
                        mm = None
                        for dk in range(8):
                            mm = nc.tensor.matmul(out=bank[:, :],
                                                  lhsT=hh1[:, dk, tk * 128:(tk + 1) * 128],
                                                  rhs=wdn[slot][:, dk, :],
                                                  start=(dk == 0), stop=(dk == 7))
                        tokm = cx.rot.commit(mm)
                        lastmm = tokm
                        nc.vector.wait_ge(cx.rot.sp.h, tokm)
                        if e == 0:
                            vv = nc.vector.tensor_scalar(out=gt[:, ti, cb * 512:(cb + 1) * 512],
                                                         in0=bank[:, :], scalar1=cw[:, ti, e:e + 1],
                                                         scalar2=None, op0=ALU.mult)
                        else:
                            vv = nc.vector.scalar_tensor_tensor(out=gt[:, ti, cb * 512:(cb + 1) * 512],
                                                                in0=bank[:, :], scalar=cw[:, ti, e:e + 1],
                                                                in1=gt[:, ti, cb * 512:(cb + 1) * 512],
                                                                op0=ALU.mult, op1=ALU.add)
                        cx.rot.release(vv)
                    wdn_free[slot] = lastmm
        join_all(cx)


def build_p_ffn1(debug=False):
    P = Prog()
    nc = P.nc
    NE = 16
    io = {
        "xc_in": P.dram("xc_in", [128, NCK * TOK], BF16, kind="ExternalInput").ap(),
        "st_in": P.dram("st_in", [1, 2 * D], F32, kind="ExternalInput").ap(),
        "gate_weight": P.dram("gate_weight", [NE, D], F32, kind="ExternalInput").ap(),
        "ebias256": P.dram("ebias256", [128, 256], F32, kind="ExternalInput").ap(),
        "gate_proj": P.dram("gate_proj", [NE, 1024, D], F32, kind="ExternalInput").ap(),
        "up_proj": P.dram("up_proj", [NE, 1024, D], F32, kind="ExternalInput").ap(),
        "down_proj": P.dram("down_proj", [NE, D, 1024], F32, kind="ExternalInput").ap(),
        "shared_gate": P.dram("shared_gate", [1024, D], F32, kind="ExternalInput").ap(),
        "shared_up": P.dram("shared_up", [1024, D], F32, kind="ExternalInput").ap(),
        "shared_down": P.dram("shared_down", [D, 1024], F32, kind="ExternalInput").ap(),
        "consts": P.dram("consts", [128, 8], F32, kind="ExternalInput").ap(),
        "onesrow": P.dram("onesrow", [1, 128], BF16, kind="ExternalInput").ap(),
        "onescol": P.dram("onescol", [128, 1], BF16, kind="ExternalInput").ap(),
        "ident": P.dram("ident", [128, 128], BF16, kind="ExternalInput").ap(),
        "ident32": P.dram("ident32", [128, 128], F32, kind="ExternalInput").ap(),
        "tieb": P.dram("tieb", [128, 16], F32, kind="ExternalInput").ap(),
    }
    g_out = P.dram("g_out", [128, NCK * TOK], BF16, kind="ExternalOutput").ap()
    if debug:
        dbg = (P.dram("dbg_sel", [128, 16], F32, kind="ExternalOutput").ap(),
               P.dram("dbg_sc", [128, 256], F32, kind="ExternalOutput").ap(),
               P.dram("dbg_cw", [128, 16 * 17], F32, kind="ExternalOutput").ap())
    scr = {}
    for nm, rows, cols in [("gw", NE, D), ("g", NE * 1024, D), ("u", NE * 1024, D),
                           ("d", NE * D, 1024), ("shg", 1024, D), ("shu", 1024, D), ("shd", D, 1024)]:
        scr[nm + "_plain"] = P.dram(nm + "_plain", [rows, cols], BF16)
        scr[nm + "T"] = P.dram(nm + "T", [cols, rows], BF16)
    io["gwT"] = scr["gwT"].ap()
    io["gT"] = scr["gT"].ap()
    io["uT"] = scr["uT"].ap()
    io["dT"] = scr["dT"].ap()
    io["shgT"] = scr["shgT"].ap()
    io["shuT"] = scr["shuT"].ap()
    io["shdT"] = scr["shdT"].ap()
    with P.stack:
        cx = alloc_common(P, io)
        dma(nc.sync, cx.xc[:, :, :].rearrange("p c t -> p (c t)"), io["xc_in"][:, :], cx.s_ld)
        dma(nc.gpsimd, cx.stat_bf[0:1, :], io["st_in"][0:1, :], cx.s_ld)
        join_all(cx)
        for nm, src, rows, cols in [
                ("gw", io["gate_weight"], NE, D),
                ("g", io["gate_proj"].rearrange("a b c -> (a b) c"), NE * 1024, D),
                ("u", io["up_proj"].rearrange("a b c -> (a b) c"), NE * 1024, D),
                ("d", io["down_proj"].rearrange("a b c -> (a b) c"), NE * D, 1024),
                ("shg", io["shared_gate"], 1024, D),
                ("shu", io["shared_up"], 1024, D),
                ("shd", io["shared_down"], D, 1024)]:
            emit_build_scratchT(cx, src, scr[nm + "_plain"].ap(), scr[nm + "T"].ap(), rows, cols)
            join_all(cx)
        gt = P.sbuf("gt1", [128, 16, 2048], BF16)
        emit_ffn1(cx, gt, dbg=dbg if debug else None)
        nc.sync.wait_ge(cx.s_ev.h, cx.s_ev.n)
        dma(nc.sync, g_out[:, :], gt[:, :, :].rearrange("p c t -> p (c t)"), cx.s_st)
        nc.sync.wait_ge(cx.s_st.h, cx.s_st.n)
    return P


def emit_attn1A(cx, outs):
    nc = cx.P.nc
    P = cx.P
    io = cx.io
    with P.scope():
        ffmb = P.sbuf("ffmb6", [128, TOK], BF16)
        gdmb = P.sbuf("gdmb6", [128, TOK], BF16)
        c0sav = P.sbuf("c0sav6", [128, 1], F32)
        hbnd = P.sbuf("hbnd6", [128, NCK], F32)
        selab = P.sbuf("selab6", [128, 2], F32)
        bAt = P.sbuf("bAt6", [128, 4, 512], BF16)
        wf = P.sbuf("wf6", [128, NCK, 128], BF16)
        wc = P.sbuf("wc6", [128, NCK, 128], BF16)
        w_free = 0
        et0 = P.sbuf("et06", [128, 512], F32)
        etc = P.sbuf("etc6", [128, 512], F32)
        hstg = P.sbuf("hstg6", [128, 512], F32)
        hbf = [P.sbuf("hbf6", [128, 512], BF16) for _ in range(2)]
        hbf_free = [0, 0]
        dma(nc.sync, hbnd[:, :], io["h_in"][:, :], cx.s_ld)
        dma(nc.sync, selab[:, :], io["selab"][:, :], cx.s_ld)
        for sub in range(4):
            emit_bcast_row(cx, cx.stat_bf[0:1, D + sub * 512:D + (sub + 1) * 512], bAt[:, sub, :], 512)
        nc.vector.wait_ge(cx.s_ld.h, cx.s_ld.n)
        nc.tensor.wait_ge(cx.s_ev.h, cx.s_ev.n)
        et_mark = 0
        for mb in range(NCK):
            if w_free:
                nc.gpsimd.wait_ge(cx.s_mm.h, w_free)
            dma(nc.gpsimd, wf[:, :, :],
                io["Wf"].rearrange("(ck p) e -> p ck e", p=128)[:, :, mb * 128:(mb + 1) * 128], cx.s_ld)
            dma(nc.gpsimd, wc[:, :, :],
                io["Wc"].rearrange("(ck p) e -> p ck e", p=128)[:, :, mb * 128:(mb + 1) * 128], cx.s_ld)
            ldm = cx.s_ld.n
            nc.tensor.wait_ge(cx.s_ld.h, ldm)
            for sub in range(4):
                t0 = sub * 512
                bankf = cx.rot.acquire(nc.tensor)
                cx.rot.acquired()
                mm = None
                for ck in range(NCK):
                    mm = nc.tensor.matmul(out=bankf[:, :], lhsT=wf[:, ck, :],
                                          rhs=cx.xc[:, ck, t0:t0 + 512],
                                          start=(ck == 0), stop=(ck == NCK - 1))
                tokf = cx.rot.commit(mm)
                bankc = cx.rot.acquire(nc.tensor)
                cx.rot.acquired()
                mm = None
                for ck in range(NCK):
                    mm = nc.tensor.matmul(out=bankc[:, :], lhsT=wc[:, ck, :],
                                          rhs=cx.xc[:, ck, t0:t0 + 512],
                                          start=(ck == 0), stop=(ck == NCK - 1))
                tokc = cx.rot.commit(mm)
                nc.vector.wait_ge(cx.rot.sp.h, tokc)
                if et_mark:
                    nc.vector.wait_ge(cx.s_ac.h, et_mark)
                vv = nc.vector.tensor_tensor(out=et0[:, :], in0=bankf[:, :], in1=bAt[:, sub, :],
                                             op=ALU.mult)
                cx.rot.release(vv)
                vv = nc.vector.tensor_tensor(out=etc[:, :], in0=bankc[:, :], in1=bAt[:, sub, :],
                                             op=ALU.mult)
                cx.rot.release(vv)
                ev = cx.s_ev.n
                nc.scalar.wait_ge(cx.s_ev.h, ev)
                nc.scalar.activation(out=ffmb[:, t0:t0 + 512], in_=et0[:, :], func=AF.Sigmoid)
                nc.scalar.activation(out=gdmb[:, t0:t0 + 512], in_=et0[:, :], func=AF.Sigmoid,
                                     bias=0.0, scale=-1.0)
                acm = A(cx, nc.scalar.activation(out=etc[:, :], in_=etc[:, :], func=AF.Silu))
                et_mark = acm
                nc.vector.wait_ge(cx.s_ac.h, acm)
                vv = nc.vector.tensor_tensor(out=gdmb[:, t0:t0 + 512], in0=gdmb[:, t0:t0 + 512],
                                             in1=etc[:, :], op=ALU.mult)
                if sub == 0:
                    V(cx, vv)
                    V(cx, nc.vector.tensor_copy(out=c0sav[:, 0:1], in_=etc[:, 0:1]))
            w_free = cx.s_mm.n
            V(cx, nc.vector.tensor_scalar(out=et0[:, 0:1], in0=gdmb[:, 0:1],
                                          scalar1=selab[:, 0:1], scalar2=None, op0=ALU.mult))
            V(cx, nc.vector.scalar_tensor_tensor(out=gdmb[:, 0:1], in0=c0sav[:, 0:1],
                                                 scalar=selab[:, 1:2], in1=et0[:, 0:1],
                                                 op0=ALU.mult, op1=ALU.add))
            for sub in range(4):
                t0 = sub * 512
                V(cx, nc.vector.tensor_tensor_scan(out=hstg[:, :], data0=ffmb[:, t0:t0 + 512],
                                                   data1=gdmb[:, t0:t0 + 512],
                                                   initial=hbnd[:, mb:mb + 1],
                                                   op0=ALU.mult, op1=ALU.add))
                V(cx, nc.vector.tensor_copy(out=hbnd[:, mb:mb + 1], in_=hstg[:, 511:512]))
                sl = sub % 2
                if hbf_free[sl]:
                    nc.vector.wait_ge(cx.s_st.h, hbf_free[sl])
                ev = V(cx, nc.vector.tensor_copy(out=hbf[sl][:, :], in_=hstg[:, :]))
                nc.sync.wait_ge(cx.s_ev.h, ev)
                dma(nc.sync, outs["h_out"][:, mb * TOK + t0:mb * TOK + t0 + 512], hbf[sl][:, :], cx.s_st)
                hbf_free[sl] = cx.s_st.n
        ev = cx.s_ev.n
        nc.sync.wait_ge(cx.s_ev.h, ev)
        dma(nc.sync, outs["hfin_out"][:, :], hbnd[:, :], cx.s_st)
        nc.sync.wait_ge(cx.s_st.h, cx.s_st.n)
        join_all(cx)


def build_p_attn1A():
    P = Prog()
    nc = P.nc
    io = {
        "xc_in": P.dram("xc_in", [128, NCK * TOK], BF16, kind="ExternalInput").ap(),
        "st_in": P.dram("st_in", [1, 2 * D], F32, kind="ExternalInput").ap(),
        "Wf": P.dram("Wf", [D, D], F32, kind="ExternalInput").ap(),
        "Wc": P.dram("Wc", [D, D], F32, kind="ExternalInput").ap(),
        "h_in": P.dram("h_in", [128, NCK], F32, kind="ExternalInput").ap(),
        "selab": P.dram("selab", [128, 2], F32, kind="ExternalInput").ap(),
        "consts": P.dram("consts", [128, 8], F32, kind="ExternalInput").ap(),
        "onesrow": P.dram("onesrow", [1, 128], BF16, kind="ExternalInput").ap(),
        "onescol": P.dram("onescol", [128, 1], BF16, kind="ExternalInput").ap(),
        "ident": P.dram("ident", [128, 128], BF16, kind="ExternalInput").ap(),
        "ident32": P.dram("ident32", [128, 128], F32, kind="ExternalInput").ap(),
        "tieb": P.dram("tieb", [128, 16], F32, kind="ExternalInput").ap(),
    }
    outs = {
        "h_out": P.dram("h_out", [128, NCK * TOK], BF16, kind="ExternalOutput").ap(),
        "hfin_out": P.dram("hfin_out", [128, NCK], F32, kind="ExternalOutput").ap(),
    }
    with P.stack:
        cx = alloc_common(P, io)
        dma(nc.sync, cx.xc[:, :, :].rearrange("p c t -> p (c t)"), io["xc_in"][:, :], cx.s_ld)
        dma(nc.gpsimd, cx.stat_bf[0:1, :], io["st_in"][0:1, :], cx.s_ld)
        join_all(cx)
        emit_attn1A(cx, outs)
    return P


def emit_attn1B(cx, io, out_f32):
    nc = cx.P.nc
    P = cx.P
    with P.scope():
        bAr = P.sbuf("bAr8", [128, 512], BF16)
        bMU = P.sbuf("bMU8", [128, 512], BF16)
        bRR = P.sbuf("bRR8", [128, 512], F32)
        lnsc = P.sbuf("lnsc8", [128, NCK], F32)
        ones32 = P.sbuf("ones328", [1, 128], F32)
        hs = P.sbuf("hs8", [128, NCK, 512], BF16)
        gc = P.sbuf("gc8", [128, NCK, 512], BF16)
        rs = P.sbuf("rs8", [128, NCK, 512], BF16)
        hp = P.sbuf("hp8", [128, NCK, 512], BF16)
        wr = [P.sbuf("wr8", [128, NCK, 128], BF16) for _ in range(2)]
        wo = [P.sbuf("wo8", [128, NCK, 128], BF16) for _ in range(2)]
        w_free = [0, 0]
        wo_free = [0, 0]
        sq2 = P.sbuf("sq28", [128, 1024], BF16)
        strow = P.sbuf("strow8", [1, 512], F32)
        et = P.sbuf("et8", [128, 512], F32)
        sgt = P.sbuf("sgt8", [128, 512], F32)
        x2b = P.sbuf("x2b8", [128, 512], BF16)
        fstg = [P.sbuf("fstg8", [128, 128], F32) for _ in range(2)]
        fstg_free = [0, 0]
        dma(nc.sync, lnsc[:, :], io["lnsc"][:, :], cx.s_ld)
        nc.vector.wait_ge(cx.s_ld.h, cx.s_ld.n)
        ev = V(cx, nc.vector.memset(ones32[0:1, :], 1.0))
        nc.tensor.wait_ge(cx.s_ev.h, ev)
        for sub in range(NSUB):
            t0 = sub * SUB
            emit_bcast_row(cx, cx.stat_bf[0:1, D + t0:D + t0 + 512], bAr[:, :], 512)
            emit_bcast_row(cx, cx.stat_bf[0:1, t0:t0 + 512], bMU[:, :], 512)
            gl_pre = cx.s_ev.n
            nc.sync.wait_ge(cx.s_ev.h, gl_pre)
            nc.sync.wait_ge(cx.s_ac.h, cx.s_ac.n)
            nc.sync.wait_ge(cx.s_mm.h, cx.s_mm.n)
            dma(nc.sync, hs[:, :, :].rearrange("p a b -> p (a b)"),
                io["h_in_full"].rearrange("p (c t) -> p c t", c=NCK)[:, :, t0:t0 + 512], cx.s_ld)
            for ck in range(NCK):
                dma_t(nc.sync, gc[:, ck, :], io["g_in"][t0:t0 + 512, ck * 128:(ck + 1) * 128], cx.s_ld)
            yldm = cx.s_ld.n
            nc.vector.wait_ge(cx.s_ld.h, yldm)
            nc.tensor.wait_ge(cx.s_ld.h, yldm)
            nc.scalar.wait_ge(cx.s_ld.h, yldm)
            et_mark = 0
            for mb in range(NCK):
                slot = mb % 2
                if w_free[slot]:
                    nc.gpsimd.wait_ge(cx.s_mm.h, w_free[slot])
                dma(nc.gpsimd, wr[slot][:, :, :],
                    io["Wr"].rearrange("(ck p) e -> p ck e", p=128)[:, :, mb * 128:(mb + 1) * 128],
                    cx.s_ld)
                ldm = cx.s_ld.n
                nc.tensor.wait_ge(cx.s_ld.h, ldm)
                bank = cx.rot.acquire(nc.tensor)
                cx.rot.acquired()
                mm = None
                for ck in range(NCK):
                    mm = nc.tensor.matmul(out=bank[:, :], lhsT=wr[slot][:, ck, :],
                                          rhs=cx.xc[:, ck, t0:t0 + 512],
                                          start=(ck == 0), stop=(ck == NCK - 1))
                tokm = cx.rot.commit(mm)
                nc.vector.wait_ge(cx.rot.sp.h, tokm)
                if et_mark:
                    nc.vector.wait_ge(cx.s_ac.h, et_mark)
                vv = nc.vector.tensor_tensor(out=et[:, :], in0=bank[:, :], in1=bAr[:, :], op=ALU.mult)
                cx.rot.release(vv)
                ev = cx.s_ev.n
                nc.scalar.wait_ge(cx.s_ev.h, ev)
                et_mark = A(cx, nc.scalar.activation(out=rs[:, mb, :], in_=et[:, :], func=AF.Silu))
                w_free[slot] = cx.s_mm.n
            banks_ = cx.rotm.acquire(nc.tensor)
            cx.rotm.acquired()
            mmq = None
            prev_q = [0, 0]
            nc.scalar.wait_ge(cx.s_ac.h, cx.s_ac.n)
            for ck in range(NCK):
                sl = sq2[:, (ck % 2) * 512:(ck % 2) * 512 + 512]
                if prev_q[ck % 2]:
                    nc.scalar.wait_ge(cx.s_mm.h, prev_q[ck % 2])
                a = nc.scalar.activation(out=sl, in_=hs[:, ck, :], func=AF.Square)
                am = cx.s_ac.inc(a, 1)
                nc.tensor.wait_ge(cx.s_ac.h, am)
                mmq = nc.tensor.matmul(out=banks_[0:1, :], lhsT=cx.onescol[:, :], rhs=sl,
                                       start=(ck == 0), stop=(ck == NCK - 1), skip_group_check=True)
                if ck < NCK - 1:
                    prev_q[ck % 2] = cx.s_mm.inc(mmq, 1)
            tok = cx.rotm.commit(mmq)
            nc.vector.wait_ge(cx.rotm.sp.h, tok)
            v1 = nc.vector.tensor_scalar(out=strow[0:1, :], in0=banks_[0:1, :],
                                         scalar1=1.0 / D, scalar2=None, op0=ALU.mult)
            cx.rotm.release(v1)
            ev = cx.s_ev.n
            nc.scalar.wait_ge(cx.s_ev.h, ev)
            nc.scalar.activation(out=strow[0:1, :], in_=strow[0:1, :], func=AF.Ln,
                                 bias=cx.consts[0:1, 2:3], scale=1.0)
            ac = A(cx, nc.scalar.activation(out=strow[0:1, :], in_=strow[0:1, :], func=AF.Exp,
                                            bias=0.0, scale=-0.5))
            nc.tensor.wait_ge(cx.s_ac.h, ac)
            emit_bcast_row32(cx, ones32, strow[0:1, :], bRR[:, :], 512)
            evb = cx.s_ev.n
            nc.vector.wait_ge(cx.s_ev.h, evb)
            for ck in range(NCK):
                nc.vector.tensor_tensor(out=et[:, :], in0=hs[:, ck, :], in1=bRR[:, :], op=ALU.mult)
                nc.vector.tensor_scalar(out=et[:, :], in0=et[:, :], scalar1=lnsc[:, ck:ck + 1],
                                        scalar2=None, op0=ALU.mult)
                nc.vector.tensor_tensor(out=hp[:, ck, :], in0=et[:, :], in1=rs[:, ck, :], op=ALU.mult)
            lnd = cx.s_ev.inc(nc.vector.tensor_copy(out=et[:, 0:1], in_=hp[:, 0, 0:1]), 1)
            nc.tensor.wait_ge(cx.s_ev.h, lnd)
            nc.scalar.wait_ge(cx.s_ev.h, lnd)
            prev_vv = 0
            for mb in range(16):
                slot = mb % 2
                if wo_free[slot]:
                    nc.gpsimd.wait_ge(cx.s_mm.h, wo_free[slot])
                dma(nc.gpsimd, wo[slot][:, :, :],
                    io["Wo"].rearrange("(ck p) e -> p ck e", p=128)[:, :, mb * 128:(mb + 1) * 128],
                    cx.s_ld)
                ldm = cx.s_ld.n
                nc.tensor.wait_ge(cx.s_ld.h, ldm)
                bank = cx.rot.acquire(nc.tensor)
                cx.rot.acquired()
                mm = None
                for ck in range(NCK):
                    mm = nc.tensor.matmul(out=bank[:, :], lhsT=wo[slot][:, ck, :],
                                          rhs=hp[:, ck, :],
                                          start=(ck == 0), stop=(ck == NCK - 1))
                tokm = cx.rot.commit(mm)
                if prev_vv:
                    nc.scalar.wait_ge(cx.s_ev.h, prev_vv)
                acv = A(cx, nc.scalar.activation(out=sgt[:, :], in_=gc[:, mb, :], func=AF.Silu))
                nc.vector.wait_ge(cx.s_ac.h, acv)
                nc.vector.wait_ge(cx.rot.sp.h, tokm)
                vv = nc.vector.tensor_tensor(out=et[:, :], in0=bank[:, :], in1=sgt[:, :], op=ALU.mult)
                cx.rot.release(vv)
                prev_vv = cx.s_ev.n
                nc.vector.tensor_tensor(out=et[:, :], in0=et[:, :], in1=bMU[:, :], op=ALU.add)
                ev = V(cx, nc.vector.tensor_tensor(out=x2b[:, :], in0=cx.xc[:, mb, t0:t0 + 512],
                                                   in1=et[:, :], op=ALU.add))
                nc.tensor.wait_ge(cx.s_ev.h, ev)
                for tk in range(4):
                    tb = cx.rott.acquire(nc.tensor)
                    cx.rott.acquired()
                    ttok = cx.rott.commit(nc.tensor.transpose(out=tb[:, 0:128],
                                                              in_=x2b[:, tk * 128:(tk + 1) * 128],
                                                              identity=cx.ident[:, :]))
                    sl = tk % 2
                    if fstg_free[sl]:
                        nc.vector.wait_ge(cx.s_st.h, fstg_free[sl])
                    nc.vector.wait_ge(cx.rott.sp.h, ttok)
                    lv = nc.vector.tensor_copy(out=fstg[sl][:, :], in_=tb[:, 0:128])
                    cx.rott.release(lv)
                    ev = cx.s_ev.n
                    nc.sync.wait_ge(cx.s_ev.h, ev)
                    dma(nc.sync, out_f32[t0 + tk * 128:t0 + (tk + 1) * 128, mb * 128:(mb + 1) * 128],
                        fstg[sl][:, :], cx.s_st)
                    fstg_free[sl] = cx.s_st.n
                wo_free[slot] = cx.s_mm.n
        nc.sync.wait_ge(cx.s_st.h, cx.s_st.n)
        join_all(cx)


def build_p_attn1B():
    P = Prog()
    nc = P.nc
    io = {
        "xc_in": P.dram("xc_in", [128, NCK * TOK], BF16, kind="ExternalInput").ap(),
        "st_in": P.dram("st_in", [1, 2 * D], F32, kind="ExternalInput").ap(),
        "Wr": P.dram("Wr", [D, D], F32, kind="ExternalInput").ap(),
        "Wo": P.dram("Wo", [D, D], F32, kind="ExternalInput").ap(),
        "lnsc": P.dram("lnsc", [128, NCK], F32, kind="ExternalInput").ap(),
        "h_in_full": P.dram("h_in_full", [128, NCK * TOK], BF16, kind="ExternalInput").ap(),
        "g_in": P.dram("g_in", [TOK, D], BF16, kind="ExternalInput").ap(),
        "consts": P.dram("consts", [128, 8], F32, kind="ExternalInput").ap(),
        "onesrow": P.dram("onesrow", [1, 128], BF16, kind="ExternalInput").ap(),
        "onescol": P.dram("onescol", [128, 1], BF16, kind="ExternalInput").ap(),
        "ident": P.dram("ident", [128, 128], BF16, kind="ExternalInput").ap(),
        "ident32": P.dram("ident32", [128, 128], F32, kind="ExternalInput").ap(),
        "tieb": P.dram("tieb", [128, 16], F32, kind="ExternalInput").ap(),
    }
    out_f32 = P.dram("x2_out", [TOK, D], F32, kind="ExternalOutput").ap()
    with P.stack:
        cx = alloc_common(P, io)
        dma(nc.sync, cx.xc[:, :, :].rearrange("p c t -> p (c t)"), io["xc_in"][:, :], cx.s_ld)
        dma(nc.gpsimd, cx.stat_bf[0:1, :], io["st_in"][0:1, :], cx.s_ld)
        join_all(cx)
        emit_attn1B(cx, io, out_f32)
    return P


def emit_build_scratchT(cx, w_fp32, scratch_plain, scratch_T, rows, cols):
    nc = cx.P.nc
    P = cx.P
    with P.scope():
        w = max(cols, ((rows + 127) // 128) * 128)
        stg = P.sbuf("stg", [128, w], BF16, side="right")
        nrt = (rows + 127) // 128
        for i in range(nrt):
            r0 = i * 128
            r1 = min(rows, r0 + 128)
            dma(nc.gpsimd, stg[0:r1 - r0, 0:cols], w_fp32[r0:r1, :], cx.s_ld)
            ldm = cx.s_ld.n
            nc.sync.wait_ge(cx.s_ld.h, ldm)
            dma(nc.sync, scratch_plain[r0:r1, :], stg[0:r1 - r0, 0:cols], cx.s_st)
            stm = cx.s_st.n
            nc.gpsimd.wait_ge(cx.s_st.h, stm)
        r128 = (rows // 128) * 128
        rtail = rows - r128
        nc.sync.wait_ge(cx.s_st.h, cx.s_st.n)
        for kc in range(cols // 128):
            if r128:
                dma_t(nc.sync, stg[:, 0:r128], scratch_plain[0:r128, kc * 128:(kc + 1) * 128], cx.s_ld)
            if rtail:
                nc.sync.dma_start(out=stg[:, r128:r128 + rtail],
                                  in_=scratch_plain.rearrange("r c -> c r")[kc * 128:(kc + 1) * 128, r128:rows],
                                  allow_slow_non_contiguous=True).then_inc(cx.s_ld.h, 16)
                cx.s_ld.n += 16
            ldm = cx.s_ld.n
            nc.sync.wait_ge(cx.s_ld.h, ldm)
            dma(nc.sync, scratch_T[kc * 128:(kc + 1) * 128, :], stg[:, 0:rows], cx.s_st)
            stm = cx.s_st.n
            nc.sync.wait_ge(cx.s_st.h, stm)


NMC = 33
SSD_CHUNKS = TOK // CHUNK
MB_A = [(DIN + 512 * i, 512) for i in range(8)] + [(DIN + 4096, 160)]


def emit_attn0A(cx, outs):
    nc = cx.P.nc
    P = cx.P
    io = cx.io
    with P.scope():
        convw = P.sbuf("convw", [128, NMC, 4], F32)
        convb = P.sbuf("convb", [128, NMC], F32)
        dtb = P.sbuf("dtb", [32, 1], F32)
        alog = P.sbuf("alog", [32, 1], F32)
        nega = P.sbuf("nega", [32, 1], F32)
        tri = P.sbuf("tri", [128, 2, 256], F32)
        maskd = P.sbuf("maskd", [128, 2, 256], F32)
        ones32 = P.sbuf("ones32", [1, 128], F32)
        dma(nc.sync, convw[:, :, :].rearrange("p a b -> p (a b)"), io["conv_w"][:, :], cx.s_ld)
        dma(nc.sync, convb[:, :], io["conv_b"][:, :], cx.s_ld)
        dma(nc.sync, dtb[:, :], io["dt_bias"][:, :], cx.s_ld)
        dma(nc.sync, alog[:, :], io["a_log"][:, :], cx.s_ld)
        dma(nc.sync, tri[:, :, :].rearrange("p a b -> p (a b)"), io["tri"][:, :], cx.s_ld)
        dma(nc.sync, maskd[:, :, :].rearrange("p a b -> p (a b)"), io["maskd"][:, :], cx.s_ld)
        ipba = P.sbuf("ipba", [128, NMC], F32)
        ipbh = P.sbuf("ipbh", [128, NMC], F32)
        ipbdt = P.sbuf("ipbdt", [32, 1], F32)
        dma(nc.sync, ipba[:, :], io["ipb_a"][:, :], cx.s_ld)
        dma(nc.sync, ipbh[:, :], io["ipb_halo"][:, :], cx.s_ld)
        dma(nc.sync, ipbdt[:, :], io["ipb_dt"][:, :], cx.s_ld)
        uprev = P.sbuf("uprevt", [3, D], BF16)
        dma(nc.gpsimd, uprev[:, :], io["uprev"][:, :], cx.s_ld)
        nc.vector.wait_ge(cx.s_ld.h, cx.s_ld.n)
        nc.vector.memset(ones32[0:1, :], 1.0)
        nc.scalar.wait_ge(cx.s_ld.h, cx.s_ld.n)
        A(cx, nc.scalar.activation(out=nega[:, :], in_=alog[:, :], func=AF.Exp))
        A(cx, nc.scalar.mul(out=nega[:, :], in_=nega[:, :], mul=-1.0))

        h3 = P.sbuf("h3", [3, 8], F32)
        xn3t = P.sbuf("xn3t", [3, D], BF16)
        u3sq = xn3t
        xn3c = P.sbuf("xn3c", [128, NCK, 4], BF16)
        nc.vector.memset(xn3c[:, :, :], 0.0)
        V(cx, nc.vector.tensor_reduce(out=h3[0:3, 0:1], in_=uprev[0:3, :], axis=mybir.AxisListType.X, op=ALU.add))
        V(cx, nc.vector.tensor_tensor(out=u3sq[0:3, :], in0=uprev[0:3, :], in1=uprev[0:3, :], op=ALU.mult))
        V(cx, nc.vector.tensor_reduce(out=h3[0:3, 1:2], in_=u3sq[0:3, :], axis=mybir.AxisListType.X, op=ALU.add))
        V(cx, nc.vector.tensor_scalar(out=h3[0:3, 0:2], in0=h3[0:3, 0:2], scalar1=1.0 / D, scalar2=None,
                                      op0=ALU.mult))
        V(cx, nc.vector.tensor_tensor(out=h3[0:3, 2:3], in0=h3[0:3, 0:1], in1=h3[0:3, 0:1], op=ALU.mult))
        ev = V(cx, nc.vector.tensor_tensor(out=h3[0:3, 1:2], in0=h3[0:3, 1:2], in1=h3[0:3, 2:3],
                                           op=ALU.subtract))
        nc.scalar.wait_ge(cx.s_ev.h, ev)
        A(cx, nc.scalar.activation(out=h3[0:3, 1:2], in_=h3[0:3, 1:2], func=AF.Ln,
                                   bias=cx.consts[0:3, 2:3], scale=1.0))
        ac = A(cx, nc.scalar.activation(out=h3[0:3, 1:2], in_=h3[0:3, 1:2], func=AF.Exp, bias=0.0, scale=-0.5))
        nc.vector.wait_ge(cx.s_ac.h, ac)
        V(cx, nc.vector.tensor_scalar(out=xn3t[0:3, :], in0=uprev[0:3, :], scalar1=h3[0:3, 0:1],
                                      scalar2=h3[0:3, 1:2], op0=ALU.subtract, op1=ALU.mult))
        nc.tensor.wait_ge(cx.s_ev.h, cx.s_ev.n)
        rot = cx.rott
        for g0 in range(0, NCK, 4):
            bank = rot.acquire(nc.tensor)
            rot.acquired()
            for k in range(4):
                ck = g0 + k
                nc.tensor.transpose(out=bank[:, k * 4:k * 4 + 3], in_=xn3t[0:3, ck * 128:(ck + 1) * 128],
                                    identity=cx.ident[0:3, 0:3])
            last = nc.tensor.transpose(out=bank[:, 384:512], in_=cx.ident[:, :], identity=cx.ident[:, :])
            tok = rot.commit(last)
            nc.vector.wait_ge(rot.sp.h, tok)
            lastv = None
            for k in range(4):
                ck = g0 + k
                lastv = nc.vector.tensor_copy(out=xn3c[:, ck, 0:3], in_=bank[:, k * 4:k * 4 + 3])
            rot.release(lastv)

        iw = [P.sbuf("iwA", [128, NCK, 256], BF16) for _ in range(2)]
        iw_free = [0, 0]
        xbraw = P.sbuf("xbraw", [128, NMC, 515], BF16)
        xbconv = P.sbuf("xbconv", [128, NMC, 256], BF16)
        bA = P.sbuf("bAs", [128, 512], BF16)
        dtt = P.sbuf("dtt", [32, 512], F32)
        att = P.sbuf("att", [32, 512], F32)
        dts = P.sbuf("dts", [32, 512], F32)
        dttm = P.sbuf("dttm", [128, 2, 32], F32)
        a_tm = P.sbuf("a_tm", [128, 2, 32], F32)
        cum_tm = P.sbuf("cum_tm", [128, 2, 32], F32)
        cum_cm = P.sbuf("cum_cm", [32, 256], F32)
        cumrow = P.sbuf("cumrow", [1, 8 * 256], F32)
        exprow_bf = P.sbuf("exprow_bf", [1, 8 * 256], BF16)
        tmp32 = P.sbuf("tmp32", [32, 128], F32)
        ds = P.sbuf("dss", [128, 2, 32], F32)
        lastraw = P.sbuf("lastraw", [1, 32], F32)
        gam = P.sbuf("gam", [1, 32], F32)
        gamb = P.sbuf("gamb", [1, 32], BF16)
        gamr = P.sbuf("gamr", [64, 32], F32)
        pref = P.sbuf("pref", [1, 256], F32)
        prefsc = P.sbuf("prefsc", [1, 32], F32)
        btm = P.sbuf("btm", [128, 2, 64], BF16)
        btm2 = P.sbuf("btm2", [128, 64], BF16)
        ctbase = P.sbuf("ctbase", [64, 256], BF16)
        xth = P.sbuf("xth", [128, 2, 128], BF16)
        dxs = P.sbuf("dxs", [128, 256], F32)
        mt = P.sbuf("mts", [128, 2, 256], BF16)
        ctc = P.sbuf("ctc", [64, 256], BF16)
        srun_bf = P.sbuf("srun_bf", [64, NH, 128], BF16)
        stmp = P.sbuf("stmp", [64, 128], F32)
        ychunk = P.sbuf("ychunk", [128, NH, 256], BF16)
        nc.vector.memset(srun_bf[:, :, :], 0.0)
        nc.vector.memset(xbraw[:, :, :], 0.0)
        V(cx, nc.vector.memset(prefsc[0:1, :], 1.0))
        Ls = dict(convw=convw, convb=convb, dtb=dtb, nega=nega, tri=tri, maskd=maskd, ones32=ones32,
                  xbraw=xbraw, xbconv=xbconv, bA=bA, dtt=dtt, att=att, dts=dts, dttm=dttm, a_tm=a_tm, cum_tm=cum_tm,
                  cum_cm=cum_cm, cumrow=cumrow, exprow_bf=exprow_bf, tmp32=tmp32,
                  ds=ds, lastraw=lastraw, gam=gam, gamb=gamb, gamr=gamr, pref=pref,
                  prefsc=prefsc, btm=btm, btm2=btm2, ctbase=ctbase, xth=xth, dxs=dxs, mt=mt, ctc=ctc,
                  srun_bf=srun_bf, stmp=stmp, ychunk=ychunk)
        chunk_idx = 0
        for sub in range(NSUB):
            t0 = sub * SUB
            emit_bcast_row(cx, cx.stat_bf[0:1, D + t0:D + t0 + 512], bA[:, :], 512)
            if sub > 0:
                V(cx, nc.vector.tensor_copy(out=xbraw[:, :, 0:3], in_=xbraw[:, :, 512:515]))
            for e0 in range(DIN, DIN + 4256, 256):
                ew = min(256, DIN + 4256 - e0)
                slot = (e0 // 256) % 2
                if iw_free[slot]:
                    nc.gpsimd.wait_ge(cx.s_mm.h, iw_free[slot])
                dma(nc.gpsimd, iw[slot][:, :, 0:ew],
                    io["ipT"].rearrange("(kc p) e -> p kc e", p=128)[:, :, e0:e0 + ew], cx.s_ld)
                ldm = cx.s_ld.n
                nc.tensor.wait_ge(cx.s_ld.h, ldm)
                lastmm = None
                for q in range((ew + 127) // 128):
                    mc = (e0 - DIN) // 128 + q
                    mw = min(128, ew - q * 128)
                    bank = cx.rot.acquire(nc.tensor)
                    cx.rot.acquired()
                    for kc in range(NCK):
                        lastmm = nc.tensor.matmul(out=bank[0:mw, :],
                                                  lhsT=iw[slot][:, kc, q * 128:q * 128 + mw],
                                                  rhs=cx.xc[:, kc, t0:t0 + 512],
                                                  start=(kc == 0), stop=(kc == NCK - 1))
                    tokm = cx.rot.commit(lastmm)
                    nc.vector.wait_ge(cx.rot.sp.h, tokm)
                    if mc < NMC:
                        vv = nc.vector.tensor_tensor(out=xbraw[:, mc, 3:515], in0=bank[:, :],
                                                     in1=bA[:, :], op=ALU.mult)
                        cx.rot.release(vv)
                        nc.vector.tensor_scalar(out=xbraw[:, mc, 3:515], in0=xbraw[:, mc, 3:515],
                                                scalar1=ipba[:, mc:mc + 1], scalar2=None, op0=ALU.add)
                    else:
                        vv = nc.vector.tensor_tensor(out=dtt[0:32, :], in0=bank[0:32, :],
                                                     in1=bA[0:32, :], op=ALU.mult)
                        cx.rot.release(vv)
                    if sub == 0 and mc < NMC:
                        bankh = cx.rotm.acquire(nc.tensor)
                        cx.rotm.acquired()
                        lh = None
                        for kc in range(NCK):
                            lh = nc.tensor.matmul(out=bankh[0:mw, 0:4],
                                                  lhsT=iw[slot][:, kc, q * 128:q * 128 + mw],
                                                  rhs=xn3c[:, kc, 0:4],
                                                  start=(kc == 0), stop=(kc == NCK - 1))
                        tokh = cx.rotm.commit(lh)
                        nc.vector.wait_ge(cx.rotm.sp.h, tokh)
                        vh = nc.vector.tensor_scalar(out=xbraw[:, mc, 0:3], in0=bankh[:, 0:3],
                                                     scalar1=ipbh[:, mc:mc + 1], scalar2=None, op0=ALU.add)
                        cx.rotm.release(vh)
                iw_free[slot] = cx.s_mm.n
            evm = cx.s_ev.n
            nc.vector.wait_ge(cx.s_ev.h, evm)
            V(cx, nc.vector.tensor_scalar(out=att[:, :], in0=dtt[:, :], scalar1=ipbdt[:, 0:1],
                                          scalar2=dtb[:, 0:1], op0=ALU.add, op1=ALU.add))
            ev = cx.s_ev.n
            nc.scalar.wait_ge(cx.s_ev.h, ev)
            A(cx, nc.scalar.activation(out=dtt[:, :], in_=att[:, :], func=AF.Abs))
            A(cx, nc.scalar.activation(out=dtt[:, :], in_=dtt[:, :], func=AF.Exp, bias=0.0, scale=-1.0))
            ac = A(cx, nc.scalar.activation(out=dtt[:, :], in_=dtt[:, :], func=AF.Ln,
                                            bias=cx.consts[0:32, 1:2], scale=1.0))
            nc.vector.wait_ge(cx.s_ac.h, ac)
            V(cx, nc.vector.tensor_scalar(out=att[:, :], in0=att[:, :], scalar1=0.0, scalar2=None,
                                          op0=ALU.max))
            V(cx, nc.vector.tensor_tensor(out=dts[:, :], in0=att[:, :], in1=dtt[:, :], op=ALU.add))
            V(cx, nc.vector.tensor_scalar(out=att[:, :], in0=dts[:, :], scalar1=nega[0:32, 0:1],
                                          scalar2=None, op0=ALU.mult))
            for cch in range(2):
                c0 = cch * 256
                for mc in range(NMC):
                    nc.vector.tensor_scalar(out=xbconv[:, mc, :], in0=xbraw[:, mc, c0:c0 + 256],
                                            scalar1=convw[:, mc, 0:1], scalar2=None, op0=ALU.mult)
                    for k in range(1, 4):
                        nc.vector.scalar_tensor_tensor(out=xbconv[:, mc, :],
                                                       in0=xbraw[:, mc, c0 + k:c0 + k + 256],
                                                       scalar=convw[:, mc, k:k + 1], in1=xbconv[:, mc, :],
                                                       op0=ALU.mult, op1=ALU.add)
                ev = cx.s_ev.inc(nc.vector.tensor_copy(out=dxs[:, 0:1], in_=xbconv[:, 0, 0:1]), 1)
                nc.scalar.wait_ge(cx.s_ev.h, ev)
                acv = None
                for mc in range(NMC):
                    acv = nc.scalar.activation(out=xbconv[:, mc, :], in_=xbconv[:, mc, :], func=AF.Silu,
                                               bias=convb[:, mc:mc + 1], scale=1.0)
                convdone = cx.s_ac.inc(acv, 1)
                nc.vector.wait_ge(cx.s_ac.h, convdone)
                nc.tensor.wait_ge(cx.s_ac.h, convdone)
                if getattr(cx, "dbg", None) is not None and chunk_idx == 0:
                    nc.sync.wait_ge(cx.s_ac.h, convdone)
                    dma(nc.sync, cx.dbg["dbg_xbc"][:, :], xbconv[:, :, :].rearrange("p a b -> p (a b)"), cx.s_st)
                    evx = cx.s_ev.n
                    nc.sync.wait_ge(cx.s_ev.h, evx)
                    dma(nc.sync, cx.dbg["dbg_att"][:, :], att[:, :], cx.s_st)
                    dma(nc.sync, cx.dbg["dbg_halo"].rearrange("p (a b) -> p a b", a=NMC), xbraw[:, :, 0:3], cx.s_st)
                    dma(nc.sync, cx.dbg["dbg_xn3"][:, :], xn3c[:, :, :].rearrange("p a b -> p (a b)"), cx.s_st)
                emit_ssd_chunk(cx, Ls, chunk_idx, cch)
                evm = cx.s_ev.n
                nc.sync.wait_ge(cx.s_ev.h, evm)
                dma(nc.sync, outs["y_local"][:, chunk_idx * NH * 256:(chunk_idx + 1) * NH * 256],
                    ychunk[:, :, :].rearrange("p a b -> p (a b)"), cx.s_st)
                stm = cx.s_st.n
                nc.vector.wait_ge(cx.s_st.h, stm)
                chunk_idx += 1
        evm = cx.s_ev.n
        nc.sync.wait_ge(cx.s_ev.h, evm)
        dma(nc.sync, outs["s_fin"][:, :], srun_bf[:, :, :].rearrange("p a b -> p (a b)"), cx.s_st)
        dma(nc.sync, outs["pref"][:, :], pref[:, :], cx.s_st)
        nc.sync.wait_ge(cx.s_st.h, cx.s_st.n)
        join_all(cx)


def emit_ssd_chunk(cx, L, chunk_idx, cch):
    nc = cx.P.nc
    c0 = cch * 256
    rotm = cx.rotm
    rot = cx.rot
    rott = cx.rott
    att = L["att"]
    xbconv = L["xbconv"]
    bank = rotm.acquire(nc.tensor)
    rotm.acquired()
    nc.tensor.transpose(out=bank[:, 0:32], in_=att[0:32, c0:c0 + 128], identity=cx.ident32[0:32, 0:32])
    nc.tensor.transpose(out=bank[:, 32:64], in_=att[0:32, c0 + 128:c0 + 256],
                        identity=cx.ident32[0:32, 0:32])
    nc.tensor.transpose(out=bank[:, 64:96], in_=L["dts"][0:32, c0:c0 + 128],
                        identity=cx.ident32[0:32, 0:32])
    tok = rotm.commit(nc.tensor.transpose(out=bank[:, 96:128], in_=L["dts"][0:32, c0 + 128:c0 + 256],
                                          identity=cx.ident32[0:32, 0:32]))
    nc.vector.wait_ge(rotm.sp.h, tok)
    nc.vector.tensor_copy(out=L["a_tm"][:, 0, :], in_=bank[:, 0:32])
    nc.vector.tensor_copy(out=L["a_tm"][:, 1, :], in_=bank[:, 32:64])
    V(cx, nc.vector.tensor_copy(out=L["dttm"][:, 0, :], in_=bank[:, 64:96]))
    V(cx, nc.vector.tensor_copy(out=L["dttm"][:, 1, :], in_=bank[:, 96:128]))
    lv = nc.vector.tensor_copy(out=L["a_tm"][:, 0, 0:1], in_=bank[:, 0:1])
    rotm.release(lv)
    atmd = cx.s_ev.n
    nc.tensor.wait_ge(cx.s_ev.h, atmd)
    bank = rotm.acquire(nc.tensor)
    rotm.acquired()
    lt = None
    for i in range(2):
        for j in range(2):
            lt = nc.tensor.matmul(out=bank[:, i * 32:(i + 1) * 32], lhsT=L["tri"][:, j, i * 128:(i + 1) * 128],
                                  rhs=L["a_tm"][:, j, :], start=(j == 0), stop=(j == 1), skip_group_check=True)
    tok = rotm.commit(lt)
    nc.vector.wait_ge(rotm.sp.h, tok)
    nc.vector.tensor_copy(out=L["cum_tm"][:, 0, :], in_=bank[:, 0:32])
    lv = nc.vector.tensor_copy(out=L["cum_tm"][:, 1, :], in_=bank[:, 32:64])
    rotm.release(lv)
    nc.vector.wait_ge(cx.s_ev.h, cx.s_ev.n)
    bank = rotm.acquire(nc.tensor)
    rotm.acquired()
    lt = None
    for j in range(2):
        lt = nc.tensor.matmul(out=bank[0:32, 0:256], lhsT=L["a_tm"][:, j, :], rhs=L["tri"][:, j, :],
                              start=(j == 0), stop=(j == 1))
    tok = rotm.commit(lt)
    nc.vector.wait_ge(rotm.sp.h, tok)
    lv = nc.vector.tensor_copy(out=L["cum_cm"][:, :], in_=bank[0:32, 0:256])
    rotm.release(lv)
    cumd = cx.s_ev.n
    nc.vector.wait_ge(cx.s_ev.h, cumd)
    nc.gpsimd.wait_ge(cx.s_ev.h, cumd)
    dma(nc.gpsimd, cx.cumdram.ap()[:, :], L["cum_cm"][:, :], cx.s_gp)
    gpm = cx.s_gp.n

    bank = rotm.acquire(nc.tensor)
    rotm.acquired()
    tok = rotm.commit(nc.tensor.transpose(out=bank[0:32, 0:128], in_=L["cum_tm"][:, 1, :],
                                          identity=cx.ident32[:, :]))
    nc.vector.wait_ge(rotm.sp.h, tok)
    lv = nc.vector.tensor_copy(out=L["tmp32"][:, :], in_=bank[0:32, 0:128])
    rotm.release(lv)
    ev = cx.s_ev.n
    nc.vector.wait_ge(cx.s_ev.h, ev)
    nc.tensor.wait_ge(cx.s_ev.h, ev)
    bank = rotm.acquire(nc.tensor)
    rotm.acquired()
    tok = rotm.commit(nc.tensor.transpose(out=bank[0:1, 0:32], in_=L["tmp32"][:, 127:128],
                                          identity=cx.ident32[0:32, 0:32]))
    nc.vector.wait_ge(rotm.sp.h, tok)
    lv = nc.vector.tensor_copy(out=L["lastraw"][0:1, :], in_=bank[0:1, 0:32])
    rotm.release(lv)
    ev = cx.s_ev.n
    nc.vector.wait_ge(cx.s_ev.h, ev)
    nc.scalar.wait_ge(cx.s_ev.h, ev)
    ac = A(cx, nc.scalar.activation(out=L["gam"][0:1, :], in_=L["lastraw"][0:1, :], func=AF.Exp))
    nc.vector.wait_ge(cx.s_ac.h, ac)
    V(cx, nc.vector.tensor_copy(out=L["gamb"][0:1, :], in_=L["gam"][0:1, :]))
    V(cx, nc.vector.tensor_copy(out=L["pref"][0:1, chunk_idx * 32:(chunk_idx + 1) * 32], in_=L["prefsc"][0:1, :]))
    V(cx, nc.vector.tensor_tensor(out=L["prefsc"][0:1, :], in0=L["prefsc"][0:1, :], in1=L["gam"][0:1, :],
                                  op=ALU.mult))
    evg = cx.s_ev.n
    nc.tensor.wait_ge(cx.s_ev.h, evg)
    bank = rotm.acquire(nc.tensor)
    rotm.acquired()
    tok = rotm.commit(nc.tensor.matmul(out=bank[0:64, 0:32], lhsT=cx.onesrow[0:1, 0:64],
                                       rhs=L["gamb"][0:1, :], start=True, stop=True))
    nc.vector.wait_ge(rotm.sp.h, tok)
    lv = nc.vector.tensor_copy(out=L["gamr"][:, :], in_=bank[0:64, 0:32])
    rotm.release(lv)
    nc.vector.wait_ge(cx.s_ev.h, cx.s_ev.n)
    bank = rotm.acquire(nc.tensor)
    rotm.acquired()
    tok = rotm.commit(nc.tensor.matmul(out=bank[:, 0:32], lhsT=L["ones32"][0:1, :],
                                       rhs=L["lastraw"][0:1, :], start=True, stop=True))
    nc.vector.wait_ge(rotm.sp.h, tok)
    nc.vector.tensor_tensor(out=L["ds"][:, 0, :], in0=bank[:, 0:32], in1=L["cum_tm"][:, 0, :],
                            op=ALU.subtract)
    lv = nc.vector.tensor_tensor(out=L["ds"][:, 1, :], in0=bank[:, 0:32], in1=L["cum_tm"][:, 1, :],
                                 op=ALU.subtract)
    rotm.release(lv)
    ev = cx.s_ev.n
    nc.vector.wait_ge(cx.s_ev.h, ev)
    nc.scalar.wait_ge(cx.s_ev.h, ev)
    A(cx, nc.scalar.activation(out=L["ds"][:, 0, :], in_=L["ds"][:, 0, :], func=AF.Exp))
    dsd = A(cx, nc.scalar.activation(out=L["ds"][:, 1, :], in_=L["ds"][:, 1, :], func=AF.Exp))
    nc.vector.wait_ge(cx.s_ac.h, dsd)
    nc.tensor.wait_ge(cx.s_ac.h, dsd)
    bank = rott.acquire(nc.tensor)
    rott.acquired()
    nc.tensor.transpose(out=bank[:, 0:64], in_=xbconv[0:64, 32, 0:128], identity=cx.ident[0:64, 0:64])
    tok = rott.commit(nc.tensor.transpose(out=bank[:, 64:128], in_=xbconv[0:64, 32, 128:256],
                                          identity=cx.ident[0:64, 0:64]))
    nc.vector.wait_ge(rott.sp.h, tok)
    nc.vector.tensor_copy(out=L["btm"][:, 0, :], in_=bank[:, 0:64])
    lv = nc.vector.tensor_copy(out=L["btm"][:, 1, :], in_=bank[:, 64:128])
    rott.release(lv)
    V(cx, nc.vector.tensor_copy(out=L["ctbase"][:, :], in_=xbconv[64:128, 32, :]))
    evct = cx.s_ev.n
    nc.sync.wait_ge(cx.s_ev.h, evct)
    dma(nc.sync, cx.ct_all_out[:, chunk_idx * 256:(chunk_idx + 1) * 256], L["ctbase"][:, :], cx.s_st)
    for hh in range(NH):
        if hh % 8 == 0:
            nc.gpsimd.wait_ge(cx.s_gp.h, cx.s_gp.n)
            nc.gpsimd.wait_ge(cx.s_ev.h, cx.s_ev.n)
            nc.gpsimd.wait_ge(cx.s_mm.h, cx.s_mm.n)
            nc.gpsimd.wait_ge(cx.s_st.h, cx.s_st.n)
            dma(nc.gpsimd, L["cumrow"][0:1, :],
                cx.cumdram.ap()[hh:hh + 8, :].rearrange("h l -> (h l)").unsqueeze(0), cx.s_gp)
            gpm = cx.s_gp.n
            nc.tensor.wait_ge(cx.s_gp.h, gpm)
            nc.scalar.wait_ge(cx.s_gp.h, gpm)
            exd = A(cx, nc.scalar.activation(out=L["exprow_bf"][0:1, :], in_=L["cumrow"][0:1, :],
                                             func=AF.Exp))
            if cx.cumrows_out is not None:
                nc.sync.wait_ge(cx.s_ac.h, exd)
                dma(nc.sync, cx.cumrows_out[chunk_idx:chunk_idx + 1, hh * 256:(hh + 8) * 256],
                    L["exprow_bf"][0:1, :], cx.s_st)
            nc.vector.wait_ge(cx.s_ac.h, exd)
        hro = (hh % 8) * 256
        bank = rott.acquire(nc.tensor)
        rott.acquired()
        nc.tensor.transpose(out=bank[:, 0:128], in_=xbconv[:, hh, 0:128], identity=cx.ident[:, :])
        tok = rott.commit(nc.tensor.transpose(out=bank[:, 128:256], in_=xbconv[:, hh, 128:256],
                                              identity=cx.ident[:, :]))
        if cx.prev_ytok:
            nc.vector.wait_ge(cx.s_mm.h, cx.prev_ytok)
        nc.vector.wait_ge(rott.sp.h, tok)
        nc.vector.tensor_copy(out=L["xth"][:, 0, :], in_=bank[:, 0:128])
        lv = nc.vector.tensor_copy(out=L["xth"][:, 1, :], in_=bank[:, 128:256])
        rott.release(lv)
        xtd = cx.s_ev.n
        nc.vector.wait_ge(cx.s_ev.h, xtd)
        nc.tensor.wait_ge(cx.s_ev.h, xtd)
        gbank = rotm.acquire(nc.tensor)
        rotm.acquired()
        lg = None
        for j in range(2):
            lg = nc.tensor.matmul(out=gbank[:, j * 256:(j + 1) * 256],
                                  lhsT=xbconv[0:64, 32, j * 128:(j + 1) * 128],
                                  rhs=L["ctbase"][:, :], start=True, stop=True, skip_group_check=True)
        gtok = rotm.commit(lg)
        dbank = rotm.acquire(nc.tensor)
        rotm.acquired()
        dtok = rotm.commit(nc.tensor.matmul(out=dbank[:, 0:256], lhsT=L["ones32"][0:1, :],
                                            rhs=L["cumrow"][0:1, hro:hro + 256], start=True, stop=True))
        nc.vector.wait_ge(rotm.sp.h, dtok)
        lv = None
        for j in range(2):
            nc.vector.tensor_scalar(out=L["dxs"][:, :], in0=dbank[:, 0:256],
                                    scalar1=L["cum_tm"][:, j, hh:hh + 1], scalar2=None, op0=ALU.subtract)
            nc.vector.tensor_tensor(out=L["dxs"][:, :], in0=L["dxs"][:, :], in1=L["maskd"][:, j, :],
                                    op=ALU.add)
            ev = cx.s_ev.inc(nc.vector.tensor_copy(out=L["tmp32"][0:1, 0:1], in_=L["dxs"][0:1, 0:1]), 1)
            nc.scalar.wait_ge(cx.s_ev.h, ev)
            ac = cx.s_ac.inc(nc.scalar.activation(out=L["dxs"][:, :], in_=L["dxs"][:, :], func=AF.Exp), 1)
            nc.vector.wait_ge(cx.s_ac.h, ac)
            lv = nc.vector.scalar_tensor_tensor(out=L["mt"][:, j, :],
                                                in0=gbank[:, j * 256:(j + 1) * 256],
                                                scalar=L["dttm"][:, j, hh:hh + 1],
                                                in1=L["dxs"][:, :], op0=ALU.mult, op1=ALU.mult)
            if j == 0:
                cx.s_ev.inc(lv, 1)
                nc.vector.wait_ge(cx.s_ev.h, cx.s_ev.n)
        rotm.release(lv)
        lvg = nc.vector.tensor_copy(out=L["dxs"][0:1, 0:1], in_=L["mt"][0:1, 0, 0:1])
        rotm.release(lvg)
        mtd = cx.s_ev.n
        cbank = rotm.acquire(nc.tensor)
        rotm.acquired()
        ctok = rotm.commit(nc.tensor.matmul(out=cbank[0:64, 0:256], lhsT=cx.onesrow[0:1, 0:64],
                                            rhs=L["exprow_bf"][0:1, hro:hro + 256], start=True, stop=True))
        nc.vector.wait_ge(rotm.sp.h, ctok)
        lv = nc.vector.tensor_tensor(out=L["ctc"][:, :], in0=cbank[0:64, 0:256],
                                     in1=L["ctbase"][:, :], op=ALU.mult)
        rotm.release(lv)
        ctcd = cx.s_ev.n
        nc.vector.wait_ge(cx.s_ev.h, ctcd)
        nc.tensor.wait_ge(cx.s_ev.h, ctcd)
        ybank = rot.acquire(nc.tensor)
        rot.acquired()
        for j in range(2):
            nc.tensor.matmul(out=ybank[:, 0:256], lhsT=L["xth"][:, j, :], rhs=L["mt"][:, j, :],
                             start=(j == 0), stop=False)
        ly = nc.tensor.matmul(out=ybank[:, 0:256], lhsT=L["srun_bf"][:, hh, :], rhs=L["ctc"][:, :],
                              start=False, stop=True)
        ytok = rot.commit(ly)
        cx.prev_ytok = ytok
        ev2 = V(cx, nc.vector.tensor_scalar(out=L["btm2"][:, :], in0=L["btm"][:, 0, :],
                                            scalar1=L["ds"][:, 0, hh:hh + 1],
                                            scalar2=L["dttm"][:, 0, hh:hh + 1], op0=ALU.mult, op1=ALU.mult))
        nc.tensor.wait_ge(cx.s_ev.h, ev2)
        sbank = rotm.acquire(nc.tensor)
        rotm.acquired()
        mm0 = nc.tensor.matmul(out=sbank[0:64, 0:128], lhsT=L["btm2"][:, :], rhs=L["xth"][:, 0, :],
                               start=True, stop=False)
        mm0tok = cx.s_mm.inc(mm0, 1)
        nc.vector.wait_ge(cx.s_mm.h, mm0tok)
        ev3 = V(cx, nc.vector.tensor_scalar(out=L["btm2"][:, :], in0=L["btm"][:, 1, :],
                                            scalar1=L["ds"][:, 1, hh:hh + 1],
                                            scalar2=L["dttm"][:, 1, hh:hh + 1], op0=ALU.mult, op1=ALU.mult))
        nc.tensor.wait_ge(cx.s_ev.h, ev3)
        stok = rotm.commit(nc.tensor.matmul(out=sbank[0:64, 0:128], lhsT=L["btm2"][:, :],
                                            rhs=L["xth"][:, 1, :], start=False, stop=True))
        nc.vector.wait_ge(rotm.sp.h, stok)
        nc.vector.tensor_scalar(out=L["stmp"][:, :], in0=L["srun_bf"][:, hh, :],
                                scalar1=L["gamr"][:, hh:hh + 1], scalar2=None, op0=ALU.mult)
        lv = nc.vector.tensor_tensor(out=L["srun_bf"][:, hh, :], in0=L["stmp"][:, :],
                                     in1=sbank[0:64, 0:128], op=ALU.add)
        rotm.release(lv)
        nc.vector.wait_ge(rot.sp.h, ytok)
        lv = nc.vector.tensor_copy(out=L["ychunk"][:, hh, :], in_=ybank[:, 0:256])
        rot.release(lv)


def build_p_attn0A(debug=False):
    P = Prog()
    nc = P.nc
    io = {
        "xc_in": P.dram("xc_in", [128, NCK * TOK], BF16, kind="ExternalInput").ap(),
        "st_in": P.dram("st_in", [1, 2 * D], F32, kind="ExternalInput").ap(),
        "in_proj_w": P.dram("in_proj_w", [DIP, D], F32, kind="ExternalInput").ap(),
        "conv_w": P.dram("conv_w", [128, NMC * 4], F32, kind="ExternalInput").ap(),
        "conv_b": P.dram("conv_b", [128, NMC], F32, kind="ExternalInput").ap(),
        "dt_bias": P.dram("dt_bias", [32, 1], F32, kind="ExternalInput").ap(),
        "a_log": P.dram("a_log", [32, 1], F32, kind="ExternalInput").ap(),
        "tri": P.dram("tri", [128, 2 * 256], F32, kind="ExternalInput").ap(),
        "maskd": P.dram("maskd", [128, 2 * 256], F32, kind="ExternalInput").ap(),
        "uprev": P.dram("uprev", [3, D], F32, kind="ExternalInput").ap(),
        "ipb_a": P.dram("ipb_a", [128, NMC], F32, kind="ExternalInput").ap(),
        "ipb_halo": P.dram("ipb_halo", [128, NMC], F32, kind="ExternalInput").ap(),
        "ipb_dt": P.dram("ipb_dt", [32, 1], F32, kind="ExternalInput").ap(),
        "consts": P.dram("consts", [128, 8], F32, kind="ExternalInput").ap(),
        "onesrow": P.dram("onesrow", [1, 128], BF16, kind="ExternalInput").ap(),
        "onescol": P.dram("onescol", [128, 1], BF16, kind="ExternalInput").ap(),
        "ident": P.dram("ident", [128, 128], BF16, kind="ExternalInput").ap(),
        "ident32": P.dram("ident32", [128, 128], F32, kind="ExternalInput").ap(),
        "tieb": P.dram("tieb", [128, 16], F32, kind="ExternalInput").ap(),
    }
    outs = {
        "y_local": P.dram("y_local", [128, NH * TOK], BF16, kind="ExternalOutput").ap(),
        "s_fin": P.dram("s_fin", [64, NH * 128], BF16, kind="ExternalOutput").ap(),
        "ct_all": P.dram("ct_all", [64, SSD_CHUNKS * 256], BF16, kind="ExternalOutput").ap(),
        "pref": P.dram("pref", [1, 256], F32, kind="ExternalOutput").ap(),
        "cumrows": P.dram("cumrows", [SSD_CHUNKS, NH * 256], BF16, kind="ExternalOutput").ap(),
    }
    if debug:
        outs["dbg_xbc"] = P.dram("dbg_xbc", [128, NMC * 256], BF16, kind="ExternalOutput").ap()
        outs["dbg_att"] = P.dram("dbg_att", [32, 512], F32, kind="ExternalOutput").ap()
        outs["dbg_halo"] = P.dram("dbg_halo", [128, NMC * 3], BF16, kind="ExternalOutput").ap()
        outs["dbg_xn3"] = P.dram("dbg_xn3", [128, NCK * 4], BF16, kind="ExternalOutput").ap()
    ip_plain = P.dram("ip_plain", [DIP, D], BF16)
    ipT = P.dram("ipT", [D, DIP], BF16)
    io["ipT"] = ipT.ap()
    cumdram = P.dram("cumdram", [NH, 256], F32)
    with P.stack:
        cx = alloc_common(P, io)
        cx.cumdram = cumdram
        cx.cumrows_out = outs["cumrows"]
        cx.ct_all_out = outs["ct_all"]
        cx.dbg = outs if debug else None
        dma(nc.sync, cx.xc[:, :, :].rearrange("p c t -> p (c t)"), io["xc_in"][:, :], cx.s_ld)
        dma(nc.gpsimd, cx.stat_bf[0:1, :], io["st_in"][0:1, :], cx.s_ld)
        join_all(cx)
        emit_build_scratchT(cx, io["in_proj_w"], ip_plain.ap(), ipT.ap(), DIP, D)
        join_all(cx)
        emit_attn0A(cx, outs)
    return P


def emit_bcast_row32(cx, ones32, src_row, dst, n):
    nc = cx.P.nc
    assert n <= 512
    p = dst.shape[0]
    bank = cx.rotm.acquire(nc.tensor)
    cx.rotm.acquired()
    tok = cx.rotm.commit(nc.tensor.matmul(out=bank[0:p, 0:n], lhsT=ones32[0:1, 0:p], rhs=src_row,
                                          start=True, stop=True))
    nc.vector.wait_ge(cx.rotm.sp.h, tok)
    v = nc.vector.tensor_copy(out=dst, in_=bank[0:p, 0:n])
    cx.rotm.release(v)


def emit_attn0B(cx, io):
    nc = cx.P.nc
    P = cx.P
    with P.scope():
        lnw = P.sbuf("lnw", [128, 32], F32)
        lnb = P.sbuf("lnb", [128, 32], F32)
        ipbz = P.sbuf("ipbz", [128, 32], F32)
        opb = P.sbuf("opb", [128, 16], F32)
        ones32 = P.sbuf("ones32b", [1, 128], F32)
        sin = P.sbuf("sin", [64, NH, 128], BF16)
        sint = P.sbuf("sint", [64, 128], BF16)
        ctk = P.sbuf("ctk", [64, SSD_CHUNKS, 256], BF16)
        prefrep = P.sbuf("prefrep", [64, 256], F32)
        pr = P.sbuf("prrow", [1, 256], F32)
        expr8 = P.sbuf("expr8b", [1, NH * 256], BF16)
        ctc = P.sbuf("ctcb", [64, 256], BF16)
        ysub = P.sbuf("ysub", [128, NH, 512], BF16)
        gc = P.sbuf("gcs", [128, NCK, 512], BF16)
        sg = P.sbuf("sgs", [128, 512], F32)
        t1 = P.sbuf("t1s", [128, 512], F32)
        bA2 = P.sbuf("bA2", [128, 512], F32)
        bB2 = P.sbuf("bB2", [128, 512], F32)
        bMU = P.sbuf("bMU", [128, 512], BF16)
        bAr = P.sbuf("bAr", [128, 512], BF16)
        sq2 = P.sbuf("sq2", [128, 1024], BF16)
        strow = P.sbuf("strow", [1, 3 * 512], F32)
        opw = [P.sbuf("opw", [128, 32, 128], BF16) for _ in range(2)]
        opw_free = [0, 0]
        iwz = [P.sbuf("iwz", [128, NCK, 256], BF16) for _ in range(2)]
        iwz_free = [0, 0]
        dma(nc.sync, lnw[:, :], io["ln_w"][:, :], cx.s_ld)
        dma(nc.sync, lnb[:, :], io["ln_b"][:, :], cx.s_ld)
        dma(nc.sync, ipbz[:, :], io["ipb_z"][:, :], cx.s_ld)
        dma(nc.sync, opb[:, :], io["opb"][:, :], cx.s_ld)
        dma(nc.gpsimd, sin[:, :, :].rearrange("p a b -> p (a b)"), io["s_in"][:, :], cx.s_ld)
        dma(nc.sync, ctk[:, :, :].rearrange("p a b -> p (a b)"), io["ct_in"][:, :], cx.s_ld)
        dma(nc.sync, pr[0:1, :], io["pref_in"][0:1, :], cx.s_ld)
        nc.vector.wait_ge(cx.s_ld.h, cx.s_ld.n)
        ev = V(cx, nc.vector.memset(ones32[0:1, :], 1.0))
        nc.tensor.wait_ge(cx.s_ev.h, ev)
        nc.tensor.wait_ge(cx.s_ld.h, cx.s_ld.n)
        emit_bcast_row32(cx, ones32, pr[0:1, :], prefrep[:, :], 256)
        for sub in range(NSUB):
            t0 = sub * SUB
            emit_bcast_row(cx, cx.stat_bf[0:1, D + t0:D + t0 + 512], bAr[:, :], 512)
            emit_bcast_row(cx, cx.stat_bf[0:1, t0:t0 + 512], bMU[:, :], 512)
            gl_pre = cx.s_ev.n
            nc.sync.wait_ge(cx.s_ev.h, gl_pre)
            for ck in range(NCK):
                dma_t(nc.sync, gc[:, ck, :], io["g_in"][t0:t0 + 512, ck * 128:(ck + 1) * 128], cx.s_ld)
            nc.sync.wait_ge(cx.s_mm.h, cx.s_mm.n)
            for q in range(2):
                ch = sub * 2 + q
                dma(nc.sync, ysub[:, :, q * 256:(q + 1) * 256],
                    io["y_in"].rearrange("p (c h l) -> p c h l", c=SSD_CHUNKS, h=NH)[:, ch, :, :], cx.s_ld)
            yldm = cx.s_ld.n
            nc.vector.wait_ge(cx.s_ld.h, yldm)
            nc.tensor.wait_ge(cx.s_ld.h, yldm)
            for q in range(2):
                ch = sub * 2 + q
                ldm0 = cx.s_ld.n
                nc.gpsimd.wait_ge(cx.s_mm.h, cx.s_mm.n)
                dma(nc.gpsimd, expr8[0:1, :], io["cum_in"][ch:ch + 1, :], cx.s_ld)
                em = cx.s_ld.n
                nc.tensor.wait_ge(cx.s_ld.h, em)
                for hh in range(NH):
                    hro = hh * 256
                    if cx.prev_ytok:
                        nc.vector.wait_ge(cx.s_mm.h, cx.prev_ytok)
                    ev = V(cx, nc.vector.tensor_scalar(out=sint[:, :], in0=sin[:, hh, :],
                                                       scalar1=prefrep[:, ch * 32 + hh:ch * 32 + hh + 1],
                                                       scalar2=None, op0=ALU.mult))
                    nc.tensor.wait_ge(cx.s_ev.h, ev)
                    cbank = cx.rotm.acquire(nc.tensor)
                    cx.rotm.acquired()
                    ctok = cx.rotm.commit(nc.tensor.matmul(out=cbank[0:64, 0:256],
                                                           lhsT=cx.onesrow[0:1, 0:64],
                                                           rhs=expr8[0:1, hro:hro + 256], start=True, stop=True))
                    nc.vector.wait_ge(cx.rotm.sp.h, ctok)
                    lv = nc.vector.tensor_tensor(out=ctc[:, :], in0=cbank[0:64, 0:256],
                                                 in1=ctk[:, ch, :], op=ALU.mult)
                    cx.rotm.release(lv)
                    ctcd = cx.s_ev.n
                    nc.vector.wait_ge(cx.s_ev.h, ctcd)
                    nc.tensor.wait_ge(cx.s_ev.h, ctcd)
                    ybank = cx.rot.acquire(nc.tensor)
                    cx.rot.acquired()
                    ytok = cx.rot.commit(nc.tensor.matmul(out=ybank[:, 0:256], lhsT=sint[:, :],
                                                          rhs=ctc[:, :], start=True, stop=True))
                    cx.prev_ytok = ytok
                    nc.vector.wait_ge(cx.rot.sp.h, ytok)
                    lv = nc.vector.tensor_tensor(out=ysub[:, hh, q * 256:(q + 1) * 256],
                                                 in0=ysub[:, hh, q * 256:(q + 1) * 256],
                                                 in1=ybank[:, 0:256], op=ALU.add)
                    cx.rot.release(lv)
            if cx.dbg is not None and sub == 0:
                nc.sync.wait_ge(cx.s_ev.h, cx.s_ev.n)
                dma(nc.sync, cx.dbg["dbg_ycorr"][:, :], ysub[:, :, :].rearrange("p a b -> p (a b)"), cx.s_st)
                nc.vector.wait_ge(cx.s_st.h, cx.s_st.n)
            for e0 in range(0, DIN, 256):
                slot = (e0 // 256) % 2
                if iwz_free[slot]:
                    nc.gpsimd.wait_ge(cx.s_mm.h, iwz_free[slot])
                dma(nc.gpsimd, iwz[slot][:, :, :],
                    io["ipT"].rearrange("(kc p) e -> p kc e", p=128)[:, :, e0:e0 + 256], cx.s_ld)
                ldm = cx.s_ld.n
                nc.tensor.wait_ge(cx.s_ld.h, ldm)
                for q in range(2):
                    mc = e0 // 128 + q
                    bank = cx.rot.acquire(nc.tensor)
                    cx.rot.acquired()
                    lastmm = None
                    for kc in range(NCK):
                        lastmm = nc.tensor.matmul(out=bank[:, :],
                                                  lhsT=iwz[slot][:, kc, q * 128:(q + 1) * 128],
                                                  rhs=cx.xc[:, kc, t0:t0 + 512],
                                                  start=(kc == 0), stop=(kc == NCK - 1))
                    tokm = cx.rot.commit(lastmm)
                    nc.vector.wait_ge(cx.rot.sp.h, tokm)
                    vv = nc.vector.tensor_tensor(out=sg[:, :], in0=bank[:, :], in1=bAr[:, :], op=ALU.mult)
                    cx.rot.release(vv)
                    V(cx, nc.vector.tensor_scalar(out=sg[:, :], in0=sg[:, :],
                                                  scalar1=ipbz[:, mc:mc + 1], scalar2=None, op0=ALU.add))
                    ev = cx.s_ev.n
                    nc.scalar.wait_ge(cx.s_ev.h, ev)
                    acv = A(cx, nc.scalar.activation(out=sg[:, :], in_=sg[:, :], func=AF.Silu))
                    nc.vector.wait_ge(cx.s_ac.h, acv)
                    V(cx, nc.vector.tensor_tensor(out=ysub[:, mc, :], in0=ysub[:, mc, :],
                                                  in1=sg[:, :], op=ALU.mult))
                iwz_free[slot] = cx.s_mm.n
            if cx.dbg is not None and sub == 0:
                nc.sync.wait_ge(cx.s_ev.h, cx.s_ev.n)
                dma(nc.sync, cx.dbg["dbg_ygate"][:, :], ysub[:, :, :].rearrange("p a b -> p (a b)"), cx.s_st)
                nc.vector.wait_ge(cx.s_st.h, cx.s_st.n)
                nc.scalar.wait_ge(cx.s_st.h, cx.s_st.n)
            banks_ = cx.rotm.acquire(nc.tensor)
            cx.rotm.acquired()
            bankq_ = cx.rotm.acquire(nc.tensor)
            cx.rotm.acquired()
            mmq = None
            prev_q = [0, 0]
            for mc in range(NH):
                sl = sq2[:, (mc % 2) * 512:(mc % 2) * 512 + 512]
                if prev_q[mc % 2]:
                    nc.scalar.wait_ge(cx.s_mm.h, prev_q[mc % 2])
                a = nc.scalar.activation(out=sl, in_=ysub[:, mc, :], func=AF.Square)
                am = cx.s_ac.inc(a, 1)
                nc.tensor.matmul(out=banks_[0:1, :], lhsT=cx.onescol[:, :], rhs=ysub[:, mc, :],
                                 start=(mc == 0), stop=(mc == NH - 1), skip_group_check=True)
                nc.tensor.wait_ge(cx.s_ac.h, am)
                mmq = nc.tensor.matmul(out=bankq_[0:1, :], lhsT=cx.onescol[:, :], rhs=sl,
                                       start=(mc == 0), stop=(mc == NH - 1), skip_group_check=True)
                if mc < NH - 1:
                    prev_q[mc % 2] = cx.s_mm.inc(mmq, 1)
            tok = cx.rotm.commit(mmq)
            nc.vector.wait_ge(cx.rotm.sp.h, tok)
            v1 = nc.vector.tensor_scalar(out=strow[0:1, 0:512], in0=banks_[0:1, :],
                                         scalar1=1.0 / DIN, scalar2=None, op0=ALU.mult)
            cx.rotm.release(v1)
            v2 = nc.vector.tensor_scalar(out=strow[0:1, 512:1024], in0=bankq_[0:1, :],
                                         scalar1=1.0 / DIN, scalar2=None, op0=ALU.mult)
            cx.rotm.release(v2)
            nc.vector.wait_ge(cx.s_ev.h, cx.s_ev.n)
            nc.vector.tensor_tensor(out=strow[0:1, 1024:1536], in0=strow[0:1, 0:512],
                                    in1=strow[0:1, 0:512], op=ALU.mult)
            v = nc.vector.tensor_tensor(out=strow[0:1, 512:1024], in0=strow[0:1, 512:1024],
                                        in1=strow[0:1, 1024:1536], op=ALU.subtract)
            ev = cx.s_ev.inc(v, 1)
            nc.scalar.wait_ge(cx.s_ev.h, ev)
            nc.scalar.activation(out=strow[0:1, 512:1024], in_=strow[0:1, 512:1024],
                                 func=AF.Ln, bias=cx.consts[0:1, 2:3], scale=1.0)
            ac = A(cx, nc.scalar.activation(out=strow[0:1, 512:1024], in_=strow[0:1, 512:1024],
                                            func=AF.Exp, bias=0.0, scale=-0.5))
            nc.vector.wait_ge(cx.s_ac.h, ac)
            V(cx, nc.vector.tensor_tensor(out=strow[0:1, 1024:1536], in0=strow[0:1, 0:512],
                                          in1=strow[0:1, 512:1024], op=ALU.mult))
            ev = V(cx, nc.vector.tensor_scalar(out=strow[0:1, 1024:1536], in0=strow[0:1, 1024:1536],
                                               scalar1=-1.0, scalar2=None, op0=ALU.mult))
            nc.tensor.wait_ge(cx.s_ev.h, ev)
            emit_bcast_row32(cx, ones32, strow[0:1, 512:1024], bA2[:, :], 512)
            emit_bcast_row32(cx, ones32, strow[0:1, 1024:1536], bB2[:, :], 512)
            evb = cx.s_ev.n
            nc.vector.wait_ge(cx.s_ev.h, evb)
            if cx.dbg is not None and sub == 0:
                nc.sync.wait_ge(cx.s_ev.h, cx.s_ev.n)
                nc.sync.wait_ge(cx.s_ac.h, cx.s_ac.n)
                dma(nc.sync, cx.dbg["dbg_strow"][0:1, :], strow[0:1, :], cx.s_st)
                nc.vector.wait_ge(cx.s_st.h, cx.s_st.n)
                nc.scalar.wait_ge(cx.s_st.h, cx.s_st.n)
            for mc in range(NH):
                nc.vector.tensor_tensor(out=t1[:, :], in0=ysub[:, mc, :], in1=bA2[:, :], op=ALU.mult)
                nc.vector.tensor_tensor(out=t1[:, :], in0=t1[:, :], in1=bB2[:, :], op=ALU.add)
                nc.vector.tensor_scalar(out=ysub[:, mc, :], in0=t1[:, :], scalar1=lnw[:, mc:mc + 1],
                                        scalar2=lnb[:, mc:mc + 1], op0=ALU.mult, op1=ALU.add)
            lnd = cx.s_ev.inc(nc.vector.tensor_copy(out=t1[:, 0:1], in_=ysub[:, 0, 0:1]), 1)
            nc.tensor.wait_ge(cx.s_ev.h, lnd)
            nc.scalar.wait_ge(cx.s_ev.h, lnd)
            if cx.dbg is not None and sub == 0:
                nc.sync.wait_ge(cx.s_ev.h, cx.s_ev.n)
                dma(nc.sync, cx.dbg["dbg_yln"][:, :], ysub[:, :, :].rearrange("p a b -> p (a b)"), cx.s_st)
                nc.vector.wait_ge(cx.s_st.h, cx.s_st.n)
            nc.scalar.wait_ge(cx.s_ld.h, yldm)
            prev_vv = 0
            for mb in range(16):
                slot = mb % 2
                if opw_free[slot]:
                    nc.gpsimd.wait_ge(cx.s_mm.h, opw_free[slot])
                dma(nc.gpsimd, opw[slot][:, :, :],
                    io["opT"].rearrange("(kc p) e -> p kc e", p=128)[:, :, mb * 128:(mb + 1) * 128],
                    cx.s_ld)
                ldm = cx.s_ld.n
                nc.tensor.wait_ge(cx.s_ld.h, ldm)
                for q in range(1):
                    ck = mb
                    bank = cx.rot.acquire(nc.tensor)
                    cx.rot.acquired()
                    lastmm = None
                    for kc in range(32):
                        lastmm = nc.tensor.matmul(out=bank[:, :],
                                                  lhsT=opw[slot][:, kc, :],
                                                  rhs=ysub[:, kc, :],
                                                  start=(kc == 0), stop=(kc == 31))
                    tokm = cx.rot.commit(lastmm)
                    if prev_vv:
                        nc.scalar.wait_ge(cx.s_ev.h, prev_vv)
                    acv = A(cx, nc.scalar.activation(out=sg[:, :], in_=gc[:, ck, :], func=AF.Silu))
                    nc.vector.wait_ge(cx.s_ac.h, acv)
                    nc.vector.wait_ge(cx.rot.sp.h, tokm)
                    if cx.dbg is not None and sub == 0:
                        V(cx, nc.vector.tensor_copy(out=bA2[:, :], in_=bank[:, :]))
                        nc.sync.wait_ge(cx.s_ev.h, cx.s_ev.n)
                        dma(nc.sync, cx.dbg["dbg_sraw"][:, mb * 512:(mb + 1) * 512], bA2[:, :], cx.s_st)
                    vv = nc.vector.scalar_tensor_tensor(out=t1[:, :], in0=bank[:, :],
                                                        scalar=opb[:, mb:mb + 1], in1=sg[:, :],
                                                        op0=ALU.add, op1=ALU.mult)
                    cx.rot.release(vv)
                    prev_vv = cx.s_ev.n
                    nc.vector.tensor_tensor(out=t1[:, :], in0=t1[:, :], in1=bMU[:, :], op=ALU.add)
                    if cx.dbg is not None and sub == 0:
                        V(cx, nc.vector.tensor_copy(out=bB2[:, :], in_=t1[:, :]))
                        nc.sync.wait_ge(cx.s_ev.h, cx.s_ev.n)
                        dma(nc.sync, cx.dbg["dbg_t1"][:, mb * 512:(mb + 1) * 512], bB2[:, :], cx.s_st)
                        nc.vector.wait_ge(cx.s_st.h, cx.s_st.n)
                    V(cx, nc.vector.tensor_tensor(out=cx.xc[:, ck, t0:t0 + 512],
                                                  in0=cx.xc[:, ck, t0:t0 + 512], in1=t1[:, :], op=ALU.add))
                opw_free[slot] = cx.s_mm.n
        join_all(cx)


def build_p_attn0B(debug=False):
    P = Prog()
    nc = P.nc
    io = {
        "xc_in": P.dram("xc_in", [128, NCK * TOK], BF16, kind="ExternalInput").ap(),
        "st_in": P.dram("st_in", [1, 2 * D], F32, kind="ExternalInput").ap(),
        "in_proj_w": P.dram("in_proj_w", [DIP, D], F32, kind="ExternalInput").ap(),
        "out_proj_w": P.dram("out_proj_w", [D, DIN], F32, kind="ExternalInput").ap(),
        "ln_w": P.dram("ln_w", [128, 32], F32, kind="ExternalInput").ap(),
        "ln_b": P.dram("ln_b", [128, 32], F32, kind="ExternalInput").ap(),
        "g_in": P.dram("g_in", [TOK, D], BF16, kind="ExternalInput").ap(),
        "y_in": P.dram("y_in", [128, NH * TOK], BF16, kind="ExternalInput").ap(),
        "s_in": P.dram("s_in", [64, NH * 128], BF16, kind="ExternalInput").ap(),
        "ipb_z": P.dram("ipb_z", [128, 32], F32, kind="ExternalInput").ap(),
        "opb": P.dram("opb", [128, 16], F32, kind="ExternalInput").ap(),
        "ct_in": P.dram("ct_in", [64, SSD_CHUNKS * 256], BF16, kind="ExternalInput").ap(),
        "pref_in": P.dram("pref_in", [1, 256], F32, kind="ExternalInput").ap(),
        "cum_in": P.dram("cum_in", [SSD_CHUNKS, NH * 256], BF16, kind="ExternalInput").ap(),
        "consts": P.dram("consts", [128, 8], F32, kind="ExternalInput").ap(),
        "onesrow": P.dram("onesrow", [1, 128], BF16, kind="ExternalInput").ap(),
        "onescol": P.dram("onescol", [128, 1], BF16, kind="ExternalInput").ap(),
        "ident": P.dram("ident", [128, 128], BF16, kind="ExternalInput").ap(),
        "ident32": P.dram("ident32", [128, 128], F32, kind="ExternalInput").ap(),
        "tieb": P.dram("tieb", [128, 16], F32, kind="ExternalInput").ap(),
    }
    x1_out = P.dram("x1_out", [128, NCK * TOK], BF16, kind="ExternalOutput").ap()
    st1_out = P.dram("st1_out", [1, 2 * D], F32, kind="ExternalOutput").ap()
    ip_plain = P.dram("ip_plain2", [DIP, D], BF16)
    ipT = P.dram("ipT2", [D, DIP], BF16)
    op_plain = P.dram("op_plain", [D, DIN], BF16)
    opT = P.dram("opT", [DIN, D], BF16)
    io["ipT"] = ipT.ap()
    io["opT"] = opT.ap()
    with P.stack:
        cx = alloc_common(P, io)
        if debug:
            cx.dbg = {
                "dbg_ycorr": P.dram("dbg_ycorr", [128, NH * 512], BF16, kind="ExternalOutput").ap(),
                "dbg_ygate": P.dram("dbg_ygate", [128, NH * 512], BF16, kind="ExternalOutput").ap(),
                "dbg_strow": P.dram("dbg_strow", [1, 3 * 512], F32, kind="ExternalOutput").ap(),
                "dbg_yln": P.dram("dbg_yln", [128, NH * 512], BF16, kind="ExternalOutput").ap(),
                "dbg_sraw": P.dram("dbg_sraw", [128, 16 * 512], F32, kind="ExternalOutput").ap(),
                "dbg_t1": P.dram("dbg_t1", [128, 16 * 512], F32, kind="ExternalOutput").ap(),
            }
        dma(nc.sync, cx.xc[:, :, :].rearrange("p c t -> p (c t)"), io["xc_in"][:, :], cx.s_ld)
        dma(nc.gpsimd, cx.stat_bf[0:1, :], io["st_in"][0:1, :], cx.s_ld)
        join_all(cx)
        emit_build_scratchT(cx, io["in_proj_w"], ip_plain.ap(), ipT.ap(), DIP, D)
        join_all(cx)
        emit_build_scratchT(cx, io["out_proj_w"], op_plain.ap(), opT.ap(), D, DIN)
        join_all(cx)
        emit_attn0B(cx, io)
        with P.scope():
            sq = P.sbuf("sqb", [128, NCK, 512], BF16)
            emit_stats_center(cx, cx.xc, sq, st_out=st1_out)
        join_all(cx)
        nc.sync.wait_ge(cx.s_ev.h, cx.s_ev.n)
        dma(nc.sync, x1_out[:, :], cx.xc[:, :, :].rearrange("p c t -> p (c t)"), cx.s_st)
        nc.sync.wait_ge(cx.s_st.h, cx.s_st.n)
    return P


def host_consts():
    consts = np.zeros((128, 8), np.float32)
    consts[:, 1] = 1.0
    consts[:, 2] = LN_EPS
    onesrow = np.ones((1, 128), BF)
    onescol = np.ones((128, 1), BF)
    ident = np.eye(128, dtype=BF)
    tieb = np.broadcast_to(np.arange(16, dtype=np.float32) * -1e-6, (128, 16)).copy()
    return {"consts": consts, "onesrow": onesrow, "onescol": onescol, "ident": ident,
            "ident32": np.eye(128, dtype=np.float32), "tieb": tieb}


def host_ssd_consts(inp):
    tri_full = np.tril(np.ones((256, 256), np.float32), 0).T
    tri = np.zeros((128, 2 * 256), np.float32)
    tri[:, 0:256] = tri_full[0:128, :]
    tri[:, 256:512] = tri_full[128:256, :]
    maskd = np.zeros((128, 2 * 256), np.float32)
    for j in range(2):
        for p in range(128):
            s = j * 128 + p
            maskd[p, j * 256:(j + 1) * 256] = np.where(np.arange(256) >= s, 0.0, NEG)
    convw = np.zeros((128, NMC * 4), np.float32)
    cw = inp["L0_attn_conv_w"].reshape(CONVD, 4)
    for mc in range(NMC):
        convw[:, mc * 4:(mc + 1) * 4] = cw[mc * 128:(mc + 1) * 128, :]
    convb = inp["L0_attn_conv_b"].reshape(NMC, 128).T.copy()
    dtb = inp["L0_attn_dt_bias"].reshape(32, 1)
    alog = inp["L0_attn_A_log"].reshape(32, 1)
    return {"tri": tri, "maskd": maskd, "conv_w": convw, "conv_b": convb,
            "dt_bias": dtb, "a_log": alog}


def host_uprev(u, c):
    b, h = c // 2, c % 2
    if h == 0:
        return np.zeros((3, D), np.float32)
    return np.ascontiguousarray(u[b, h * TOK - 3:h * TOK, :])


def core_slices(full):
    out = []
    for c in range(8):
        b, h = c // 2, c % 2
        out.append(np.ascontiguousarray(full[b, h * TOK:(h + 1) * TOK, :]))
    return out


def build_p_load():
    P = Prog()
    nc = P.nc
    io = {
        "u": P.dram("u", [TOK, D], F32, kind="ExternalInput").ap(),
        "consts": P.dram("consts", [128, 8], F32, kind="ExternalInput").ap(),
        "onesrow": P.dram("onesrow", [1, 128], BF16, kind="ExternalInput").ap(),
        "onescol": P.dram("onescol", [128, 1], BF16, kind="ExternalInput").ap(),
        "ident": P.dram("ident", [128, 128], BF16, kind="ExternalInput").ap(),
        "ident32": P.dram("ident32", [128, 128], F32, kind="ExternalInput").ap(),
        "tieb": P.dram("tieb", [128, 16], F32, kind="ExternalInput").ap(),
    }
    xc_out = P.dram("xc_out", [128, NCK * TOK], BF16, kind="ExternalOutput").ap()
    st_out = P.dram("st_out", [1, 2 * D], F32, kind="ExternalOutput").ap()
    with P.stack:
        cx = alloc_common(P, io)
        with P.scope():
            sq = P.sbuf("sq", [128, NCK, 512], BF16)
            emit_load_x(cx)
            emit_stats_center(cx, cx.xc, sq, st_out=st_out)
            join_all(cx)
        nc.sync.wait_ge(cx.s_ev.h, cx.s_ev.n)
        dma(nc.sync, xc_out[:, :], cx.xc[:, :, :].rearrange("p c t -> p (c t)"), cx.s_st)
        nc.sync.wait_ge(cx.s_st.h, cx.s_st.n)
    return P


def build_p_unload():
    P = Prog()
    nc = P.nc
    io = {
        "xc_in": P.dram("xc_in", [128, NCK * TOK], BF16, kind="ExternalInput").ap(),
        "st_in": P.dram("st_in", [1, 2 * D], F32, kind="ExternalInput").ap(),
        "consts": P.dram("consts", [128, 8], F32, kind="ExternalInput").ap(),
        "onesrow": P.dram("onesrow", [1, 128], BF16, kind="ExternalInput").ap(),
        "onescol": P.dram("onescol", [128, 1], BF16, kind="ExternalInput").ap(),
        "ident": P.dram("ident", [128, 128], BF16, kind="ExternalInput").ap(),
        "ident32": P.dram("ident32", [128, 128], F32, kind="ExternalInput").ap(),
        "tieb": P.dram("tieb", [128, 16], F32, kind="ExternalInput").ap(),
    }
    out = P.dram("out", [TOK, D], F32, kind="ExternalOutput").ap()
    with P.stack:
        cx = alloc_common(P, io)
        dma(nc.sync, cx.xc[:, :, :].rearrange("p c t -> p (c t)"), io["xc_in"][:, :], cx.s_ld)
        dma(nc.gpsimd, cx.stat_bf[0:1, :], io["st_in"][0:1, :], cx.s_ld)
        nc.tensor.wait_ge(cx.s_ld.h, cx.s_ld.n)
        for tb in range(TOK // 512):
            bank = cx.rotm.acquire(nc.tensor)
            cx.rotm.acquired()
            mm = nc.tensor.matmul(out=bank[:, :], lhsT=cx.onesrow[0:1, :],
                                  rhs=cx.stat_bf[0:1, tb * 512:(tb + 1) * 512], start=True, stop=True)
            tok = cx.rotm.commit(mm)
            nc.vector.wait_ge(cx.rotm.sp.h, tok)
            last = None
            for ck in range(NCK):
                last = nc.vector.tensor_tensor(out=cx.xc[:, ck, tb * 512:(tb + 1) * 512],
                                               in0=cx.xc[:, ck, tb * 512:(tb + 1) * 512],
                                               in1=bank[:, :], op=ALU.add)
            cx.rotm.release(last)
        join_all(cx)
        emit_store_out(cx, cx.xc, out)
    return P


def kernel(**inputs):
    from concourse.bass_utils import run_bass_kernel_spmd
    inp = inputs
    u = np.asarray(inp["L0_attn_u"], np.float32)
    consts = host_consts()
    sconsts = host_ssd_consts(inp)
    slices = core_slices(u)

    def run(P, maps):
        return run_bass_kernel_spmd(P.nc, maps, core_ids=list(range(8)))

    ipb = np.asarray(inp["L0_attn_in_proj_b"], np.float32)
    ipb_a = np.ascontiguousarray(ipb[4096:8320].reshape(33, 128).T)
    ipb_zero = np.zeros((128, 33), np.float32)
    ipb_dt = np.ascontiguousarray(ipb[8320:8352].reshape(32, 1))
    ipb_z = np.ascontiguousarray(ipb[0:4096].reshape(32, 128).T)
    opb = np.ascontiguousarray(np.asarray(inp["L0_attn_out_proj_b"], np.float32).reshape(16, 128).T)
    lnw0 = np.ascontiguousarray(np.asarray(inp["L0_attn_ln_w"], np.float32).reshape(32, 128).T)
    lnb0 = np.ascontiguousarray(np.asarray(inp["L0_attn_ln_b"], np.float32).reshape(32, 128).T)
    ebias256 = np.ascontiguousarray(
        np.broadcast_to(np.asarray(inp["L1_ffn_expert_bias"], np.float32)[None, None, :],
                        (128, 16, 16)).reshape(128, 256))
    lnsc1 = np.ascontiguousarray(np.asarray(inp["L1_attn_ln_scale"], np.float32).reshape(16, 128).T)

    r1 = run(build_p_load(), [dict(consts, u=s) for s in slices])

    maps = [dict(consts, xc_in=r1.results[c]["xc_out"], st_in=r1.results[c]["st_out"],
                 w_router=inp["L0_ffn_w_router"], w_fc=inp["L0_ffn_w_fc"],
                 w_proj=inp["L0_ffn_w_proj"]) for c in range(8)]
    r2 = run(build_p_ffn0(), maps)

    maps = []
    for c in range(8):
        m = dict(consts, **sconsts)
        m["xc_in"] = r1.results[c]["xc_out"]
        m["st_in"] = r1.results[c]["st_out"]
        m["in_proj_w"] = inp["L0_attn_in_proj_w"]
        m["uprev"] = host_uprev(u, c)
        m["ipb_a"] = ipb_a
        m["ipb_halo"] = ipb_a if c % 2 == 1 else ipb_zero
        m["ipb_dt"] = ipb_dt
        maps.append(m)
    r3 = run(build_p_attn0A(), maps)

    maps = []
    for c in range(8):
        m = dict(consts)
        m["xc_in"] = r1.results[c]["xc_out"]
        m["st_in"] = r1.results[c]["st_out"]
        m["in_proj_w"] = inp["L0_attn_in_proj_w"]
        m["out_proj_w"] = inp["L0_attn_out_proj_w"]
        m["ln_w"] = lnw0
        m["ln_b"] = lnb0
        m["ipb_z"] = ipb_z
        m["opb"] = opb
        g2 = r2.results[c]["g_out"].reshape(128, 16, 2048)
        m["g_in"] = np.ascontiguousarray(np.transpose(g2, (1, 0, 2)).reshape(2048, 2048))
        m["y_in"] = r3.results[c]["y_local"]
        peer = c - 1 if c % 2 == 1 else None
        m["s_in"] = (r3.results[peer]["s_fin"] if peer is not None
                     else np.zeros((64, 32 * 128), ml_dtypes.bfloat16))
        m["ct_in"] = r3.results[c]["ct_all"]
        m["pref_in"] = r3.results[c]["pref"]
        m["cum_in"] = r3.results[c]["cumrows"]
        maps.append(m)
    r4 = run(build_p_attn0B(), maps)

    maps = []
    for c in range(8):
        m = dict(consts)
        m["xc_in"] = r4.results[c]["x1_out"]
        m["st_in"] = r4.results[c]["st1_out"]
        m["gate_weight"] = inp["L1_ffn_gate_weight"]
        m["ebias256"] = ebias256
        for k in ["gate_proj", "up_proj", "down_proj", "shared_gate", "shared_up", "shared_down"]:
            m[k] = inp["L1_ffn_" + k]
        maps.append(m)
    r5 = run(build_p_ffn1(), maps)

    P6 = build_p_attn1A()

    def a1maps(hins):
        maps = []
        for c in range(8):
            m = dict(consts)
            m["xc_in"] = r4.results[c]["x1_out"]
            m["st_in"] = r4.results[c]["st1_out"]
            m["Wf"] = inp["L1_attn_Wf"]
            m["Wc"] = inp["L1_attn_Wc"]
            m["h_in"] = hins[c]
            sela = 1.0 if c % 2 else 0.0
            m["selab"] = np.ascontiguousarray(
                np.broadcast_to(np.array([sela, 1.0 - sela], np.float32)[None, :], (128, 2)))
            maps.append(m)
        return maps

    z16 = np.zeros((128, 16), np.float32)
    r6a = run(P6, a1maps([z16] * 8))
    hins = [z16 if c % 2 == 0 else r6a.results[c - 1]["hfin_out"] for c in range(8)]
    r6b = run(P6, a1maps(hins))
    hres = [r6a.results[c] if c % 2 == 0 else r6b.results[c] for c in range(8)]

    maps = []
    for c in range(8):
        m = dict(consts)
        m["xc_in"] = r4.results[c]["x1_out"]
        m["st_in"] = r4.results[c]["st1_out"]
        m["Wr"] = inp["L1_attn_Wr"]
        m["Wo"] = inp["L1_attn_Wo"]
        m["lnsc"] = lnsc1
        m["h_in_full"] = hres[c]["h_out"]
        g2 = r5.results[c]["g_out"].reshape(128, 16, 2048)
        m["g_in"] = np.ascontiguousarray(np.transpose(g2, (1, 0, 2)).reshape(2048, 2048))
        maps.append(m)
    r8 = run(build_p_attn1B(), maps)

    out = np.empty((4, 4096, 2048), np.float32)
    for c in range(8):
        b, h = c // 2, c % 2
        out[b, h * TOK:(h + 1) * TOK, :] = r8.results[c]["x2_out"]
    return out


# version 02 (final)
# speedup vs baseline: 1.0083x; 1.0083x over previous
import numpy as np
import ml_dtypes
from contextlib import ExitStack, contextmanager

import concourse.bass as bass
import concourse.mybir as mybir
from concourse.bass_utils import run_bass_kernel_spmd
from bass_rust import DynSlice, RuntimeValue

F32 = mybir.dt.float32
BF16 = mybir.dt.bfloat16
I32 = mybir.dt.int32
AF = mybir.ActivationFunctionType
ALU = mybir.AluOpType
BF = ml_dtypes.bfloat16

B, S, D = 4, 4096, 2048
TOK = 2048
NCK = D // 128
SUB = 512
NSUB = TOK // SUB
CHUNK = 256
DIN = 4096
NH = 32
HD = 128
DS = 64
CONVD = DIN + 2 * DS
DIP = 2 * DIN + 2 * DS + NH
LN_EPS = 1e-5
NEG = -60000.0
NEGBIG = -1e30


_inst_with_update = {}


class Sem:
    __slots__ = ("h", "n")

    def __init__(self, h):
        self.h = h
        self.n = 0

    def inc(self, inst, k=1):
        key = id(inst)
        assert key not in _inst_with_update, "instruction already carries a sem update"
        _inst_with_update[key] = inst
        inst.then_inc(self.h, k)
        self.n += k
        return self.n


class Prog:
    def __init__(self):
        self.nc = bass.Bass("TRN2", target_bir_lowering=False, debug=False)
        self.stack = ExitStack()
        self._names = {}
        self._scopes = [self.stack]

    def _nm(self, name):
        i = self._names.get(name, -1) + 1
        self._names[name] = i
        return name if i == 0 else f"{name}_{i}"

    @contextmanager
    def scope(self):
        st = ExitStack()
        self._scopes.append(st)
        try:
            yield st
        finally:
            self._scopes.pop()
            st.close()

    def sbuf(self, name, shape, dtype, side=None):
        kw = {"side": side} if side else {}
        return self._scopes[-1].enter_context(self.nc.sbuf_tensor(self._nm(name), shape, dtype, **kw))

    def psum(self, name, shape, dtype):
        return self._scopes[-1].enter_context(self.nc.psum_tensor(self._nm(name), shape, dtype))

    def sem(self, name):
        return Sem(self.stack.enter_context(self.nc.semaphore(self._nm(name))))

    def dram(self, name, shape, dtype, kind=None, addr_space=None):
        kw = {}
        if kind is not None:
            kw["kind"] = kind
        if addr_space is not None:
            kw["addr_space"] = addr_space
        return self.nc.dram_tensor(self._nm(name), shape, dtype, **kw)


def dma(q, out, in_, sem):
    q.dma_start(out=out, in_=in_).then_inc(sem.h, 16)
    sem.n += 16


def dma_t(q, out, in_, sem):
    q.dma_start(out=out, in_=in_, transpose=True).then_inc(sem.h, 16)
    sem.n += 16


class Rotor:

    def __init__(self, banks, s_prod: Sem, s_cons: Sem):
        self.banks = list(banks)
        self.free_at = [0] * len(self.banks)
        self.sp = s_prod
        self.sc = s_cons
        self.i = 0
        self._last = None

    def acquire(self, prod_eng):
        b = self.i % len(self.banks)
        self.i += 1
        self._last = b
        if self.free_at[b]:
            prod_eng.wait_ge(self.sc.h, self.free_at[b])
        return self.banks[b]

    def commit(self, last_prod_inst):
        return self.sp.inc(last_prod_inst, 1)

    def release(self, last_cons_inst):
        b = self._rel_pending.pop(0)
        self.free_at[b] = self.sc.inc(last_cons_inst, 1)

    @property
    def _rel_pending(self):
        if not hasattr(self, "_rp"):
            self._rp = []
        return self._rp

    def acquired(self):
        self._rel_pending.append(self._last)


class Ctx:
    pass


ALL_ENGS = ("tensor", "vector", "scalar", "gpsimd", "sync")


def join_all(cx, engines=ALL_ENGS):
    nc = cx.P.nc
    for e in engines:
        eng = getattr(nc, e)
        for s in cx.all_sems:
            if s.n:
                eng.wait_ge(s.h, s.n)


def V(cx, inst):
    n = cx.s_ev.inc(inst, 1)
    cx.P.nc.vector.wait_ge(cx.s_ev.h, n)
    return n


def A(cx, inst):
    n = cx.s_ac.inc(inst, 1)
    cx.P.nc.scalar.wait_ge(cx.s_ac.h, n)
    return n


def alloc_common(P: Prog, io):
    nc = P.nc
    cx = Ctx()
    cx.P = P
    cx.io = io
    cx.xc = P.sbuf("xc", [128, NCK, TOK], BF16)
    cx.stat_bf = P.sbuf("stat_bf", [1, 2 * D], BF16)
    cx.consts = P.sbuf("consts", [128, 8], F32)
    cx.onesrow = P.sbuf("onesrow", [1, 128], BF16)
    cx.onescol = P.sbuf("onescol", [128, 1], BF16)
    cx.ident = P.sbuf("ident", [128, 128], BF16)
    cx.tieb = P.sbuf("tieb", [128, 16], F32)
    cx.ident32 = P.sbuf("ident32", [128, 128], F32)
    cx.pb = [P.psum("pb", [128, 512], F32) for _ in range(4)]
    cx.pm = [P.psum("pm", [128, 512], F32) for _ in range(2)]
    cx.pt = [P.psum("pt", [128, 1024], BF16) for _ in range(2)]
    cx.s_ld = P.sem("s_ld")
    cx.s_mm = P.sem("s_mm")
    cx.s_ev = P.sem("s_ev")
    cx.s_ac = P.sem("s_ac")
    cx.s_st = P.sem("s_st")
    cx.s_gp = P.sem("s_gp")
    cx.all_sems = [cx.s_ld, cx.s_mm, cx.s_ev, cx.s_ac, cx.s_st, cx.s_gp]
    cx.cumdram = None
    cx.prev_ytok = 0
    cx.cumrows_out = None
    cx.ct_all_out = None
    cx.dbg = None
    cx.rot = Rotor(cx.pb, cx.s_mm, cx.s_ev)
    cx.rott = Rotor([t[:, 0:512] for t in cx.pt], cx.s_mm, cx.s_ev)
    cx.rotm = Rotor(cx.pm, cx.s_mm, cx.s_ev)

    dma(nc.sync, cx.consts[:, :], io["consts"][:, :], cx.s_ld)
    dma(nc.sync, cx.onesrow[:, :], io["onesrow"][:, :], cx.s_ld)
    dma(nc.sync, cx.onescol[:, :], io["onescol"][:, :], cx.s_ld)
    dma(nc.sync, cx.ident[:, :], io["ident"][:, :], cx.s_ld)
    dma(nc.sync, cx.tieb[:, :], io["tieb"][:, :], cx.s_ld)
    dma(nc.sync, cx.ident32[:, :], io["ident32"][:, :], cx.s_ld)
    nc.const_aps.register(F32, 0.0, cx.consts[:, 0:1])
    nc.const_aps.register(F32, 1.0, cx.consts[:, 1:2])
    nc.const_aps.register(F32, LN_EPS, cx.consts[:, 2:3])
    return cx


def emit_transpose_blocks(cx, pairs):
    nc = cx.P.nc
    rot = cx.rott
    for g0 in range(0, len(pairs), 4):
        grp = pairs[g0:g0 + 4]
        bank = rot.acquire(nc.tensor)
        rot.acquired()
        last = None
        for k, (s, d) in enumerate(grp):
            last = nc.tensor.transpose(out=bank[:, k * 128:(k + 1) * 128], in_=s,
                                       identity=cx.ident[:, :])
        tok = rot.commit(last)
        nc.vector.wait_ge(rot.sp.h, tok)
        lastv = None
        for k, (s, d) in enumerate(grp):
            lastv = nc.vector.tensor_copy(out=d, in_=bank[:, k * 128:(k + 1) * 128])
        rot.release(lastv)


def emit_stats_center(cx, buf, sq_scratch, center=True, st_out=None):
    nc = cx.P.nc
    P = cx.P
    stsc = P.scope()
    stsc.__enter__()
    cx.stat = P.sbuf("stat", [1, 2 * D], F32)
    cx.stat_sc = P.sbuf("stat_sc", [1, 512], F32)
    prev_group_done = 0
    for tb in range(TOK // 512):
        if prev_group_done:
            nc.scalar.wait_ge(cx.s_mm.h, prev_group_done)
        a = None
        for ck in range(NCK):
            a = nc.scalar.activation(out=sq_scratch[:, ck, :],
                                     in_=buf[:, ck, tb * 512:(tb + 1) * 512],
                                     func=AF.Square)
        amark = cx.s_ac.inc(a, 1)
        bank_s = cx.rotm.acquire(nc.tensor)
        cx.rotm.acquired()
        bank_q = cx.rotm.acquire(nc.tensor)
        cx.rotm.acquired()
        nc.tensor.wait_ge(cx.s_ac.h, amark)
        mmq = None
        for ck in range(NCK):
            nc.tensor.matmul(out=bank_s[0:1, :], lhsT=cx.onescol[:, :],
                             rhs=buf[:, ck, tb * 512:(tb + 1) * 512],
                             start=(ck == 0), stop=(ck == NCK - 1))
            mmq = nc.tensor.matmul(out=bank_q[0:1, :], lhsT=cx.onescol[:, :],
                                   rhs=sq_scratch[:, ck, :],
                                   start=(ck == 0), stop=(ck == NCK - 1))
        tok = cx.rotm.commit(mmq)
        prev_group_done = tok
        nc.vector.wait_ge(cx.rotm.sp.h, tok)
        v1 = nc.vector.tensor_scalar(out=cx.stat[0:1, tb * 512:(tb + 1) * 512], in0=bank_s[0:1, :],
                                     scalar1=1.0 / D, scalar2=None, op0=ALU.mult)
        cx.rotm.release(v1)
        v2 = nc.vector.tensor_scalar(out=cx.stat[0:1, D + tb * 512:D + (tb + 1) * 512], in0=bank_q[0:1, :],
                                     scalar1=1.0 / D, scalar2=None, op0=ALU.mult)
        cx.rotm.release(v2)
    a = None
    for sb in range(D // 512):
        sl = slice(D + sb * 512, D + (sb + 1) * 512)
        nc.vector.tensor_tensor(out=cx.stat_sc[0:1, :], in0=cx.stat[0:1, sb * 512:(sb + 1) * 512],
                                in1=cx.stat[0:1, sb * 512:(sb + 1) * 512], op=ALU.mult)
        v = nc.vector.tensor_tensor(out=cx.stat[0:1, sl], in0=cx.stat[0:1, sl],
                                    in1=cx.stat_sc[0:1, :], op=ALU.subtract)
        ev = cx.s_ev.inc(v, 1)
        nc.scalar.wait_ge(cx.s_ev.h, ev)
        nc.scalar.activation(out=cx.stat[0:1, sl], in_=cx.stat[0:1, sl],
                             func=AF.Ln, bias=cx.consts[0:1, 2:3], scale=1.0)
        a = nc.scalar.activation(out=cx.stat[0:1, sl], in_=cx.stat[0:1, sl],
                                 func=AF.Exp, bias=0.0, scale=-0.5)
    cx.s_ac.inc(a, 1)
    nc.vector.wait_ge(cx.s_ac.h, cx.s_ac.n)
    v = nc.vector.tensor_copy(out=cx.stat_bf[0:1, 0:2 * D], in_=cx.stat[0:1, 0:2 * D])
    cx.s_ev.inc(v, 1)
    if st_out is not None:
        nc.sync.wait_ge(cx.s_ac.h, cx.s_ac.n)
        dma(nc.sync, st_out[0:1, :], cx.stat[0:1, :], cx.s_st)
    join_all(cx, ("vector", "scalar", "tensor", "sync"))
    stsc.__exit__(None, None, None)
    del cx.stat, cx.stat_sc
    if not center:
        return
    nc.tensor.wait_ge(cx.s_ev.h, cx.s_ev.n)
    for tb in range(TOK // 512):
        bank = cx.rotm.acquire(nc.tensor)
        cx.rotm.acquired()
        mm = nc.tensor.matmul(out=bank[:, :], lhsT=cx.onesrow[0:1, :],
                              rhs=cx.stat_bf[0:1, tb * 512:(tb + 1) * 512], start=True, stop=True)
        tok = cx.rotm.commit(mm)
        nc.vector.wait_ge(cx.rotm.sp.h, tok)
        last = None
        for ck in range(NCK):
            last = nc.vector.tensor_tensor(out=buf[:, ck, tb * 512:(tb + 1) * 512],
                                           in0=buf[:, ck, tb * 512:(tb + 1) * 512],
                                           in1=bank[:, :], op=ALU.subtract)
        cx.rotm.release(last)


def emit_bcast_row(cx, src_row, dst, n):
    nc = cx.P.nc
    assert n <= 512
    bank = cx.rotm.acquire(nc.tensor)
    cx.rotm.acquired()
    mm = nc.tensor.matmul(out=bank[:, 0:n], lhsT=cx.onesrow[0:1, :], rhs=src_row, start=True, stop=True)
    tok = cx.rotm.commit(mm)
    nc.vector.wait_ge(cx.rotm.sp.h, tok)
    v = nc.vector.tensor_copy(out=dst, in_=bank[:, 0:n])
    cx.rotm.release(v)


def emit_load_x(cx):
    nc = cx.P.nc
    P = cx.P
    with P.scope():
        utm = P.sbuf("utm", [128, 16, D], BF16)
        for ti in range(16):
            dma(nc.gpsimd, utm[:, ti, :], cx.io["u"][ti * 128:(ti + 1) * 128, :], cx.s_ld)
        nc.tensor.wait_ge(cx.s_ld.h, cx.s_ld.n)
        pairs = []
        for ck in range(NCK):
            for ti in range(16):
                pairs.append((utm[:, ti, ck * 128:(ck + 1) * 128],
                              cx.xc[:, ck, ti * 128:(ti + 1) * 128]))
        emit_transpose_blocks(cx, pairs)
        join_all(cx)


def emit_store_out(cx, src_buf, out_hbm):
    nc = cx.P.nc
    P = cx.P
    with P.scope():
        otm = [P.sbuf("otm", [128, NCK, 128], F32) for _ in range(2)]
        st_free = [0, 0]
        rot = cx.rott
        for ti in range(16):
            o = otm[ti % 2]
            if st_free[ti % 2]:
                nc.vector.wait_ge(cx.s_st.h, st_free[ti % 2])
            for g0 in range(0, NCK, 4):
                bank = rot.acquire(nc.tensor)
                rot.acquired()
                last = None
                for k in range(4):
                    ck = g0 + k
                    last = nc.tensor.transpose(out=bank[:, k * 128:(k + 1) * 128],
                                               in_=src_buf[:, ck, ti * 128:(ti + 1) * 128],
                                               identity=cx.ident[:, :])
                tok = rot.commit(last)
                nc.vector.wait_ge(rot.sp.h, tok)
                lastv = None
                for k in range(4):
                    ck = g0 + k
                    lastv = nc.vector.tensor_copy(out=o[:, ck, :], in_=bank[:, k * 128:(k + 1) * 128])
                rot.release(lastv)
            ev_mark = cx.s_ev.n
            nc.sync.wait_ge(cx.s_ev.h, ev_mark)
            dma(nc.sync, out_hbm[ti * 128:(ti + 1) * 128, :], o[:, :, :].rearrange("p c t -> p (c t)"), cx.s_st)
            st_free[ti % 2] = cx.s_st.n
        nc.sync.wait_ge(cx.s_st.h, cx.s_st.n)
        join_all(cx)


def emit_row_to_col(cx, row_bf, col_dst, n):
    nc = cx.P.nc
    rot = cx.rott
    for g0 in range(0, n, 4):
        cnt = min(4, n - g0)
        bank = rot.acquire(nc.tensor)
        rot.acquired()
        for k in range(cnt):
            i = g0 + k
            nc.tensor.transpose(out=bank[:, k * 2:k * 2 + 1],
                                in_=row_bf[0:1, i * 128:(i + 1) * 128],
                                identity=cx.ident[0:1, 0:1])
        last = nc.tensor.transpose(out=bank[:, 384:512], in_=cx.ident[:, :],
                                   identity=cx.ident[:, :])
        tok = rot.commit(last)
        nc.vector.wait_ge(rot.sp.h, tok)
        lastv = None
        for k in range(cnt):
            lastv = nc.vector.tensor_copy(out=col_dst[:, g0 + k:g0 + k + 1],
                                          in_=bank[:, k * 2:k * 2 + 1])
        rot.release(lastv)


def emit_ffn0(cx, gt, dbg=None):
    nc = cx.P.nc
    P = cx.P
    io = cx.io
    NE = 8
    with P.scope():
        wr = P.sbuf("wrout", [128, NCK, 8], BF16)
        for ck in range(NCK):
            dma(nc.gpsimd, wr[:, ck, :], io["w_router"][ck * 128:(ck + 1) * 128, :], cx.s_ld)
        rcol = P.sbuf("rcol", [128, 16], F32)
        r2w = P.sbuf("r2w", [128, 16], F32)
        wr2 = P.sbuf("wr2", [128, 16, 8], F32)
        lg = P.sbuf("lg", [128, 16, 8], F32)
        tmp = P.sbuf("rtmp", [128, 16], F32)
        tmp8 = P.sbuf("rtmp8", [128, 8], F32)
        thr8 = P.sbuf("rthr8", [128, 8], F32)
        dbg_rt = P.sbuf("dbg_rt", [128, 32], F32)
        m1 = P.sbuf("m1", [128, 1], F32)
        m1n = P.sbuf("m1n", [128, 1], F32)
        m2 = P.sbuf("m2", [128, 1], F32)
        s12 = P.sbuf("s12", [128, 1], F32)
        emit_row_to_col(cx, cx.stat_bf[0:1, D:2 * D], rcol, 16)
        nc.vector.wait_ge(cx.s_ev.h, cx.s_ev.n)
        V(cx, nc.vector.tensor_tensor(out=r2w[:, :], in0=rcol[:, :], in1=rcol[:, :], op=ALU.mult))
        nc.tensor.wait_ge(cx.s_ld.h, cx.s_ld.n)
        for tb in range(4):
            bank = cx.rotm.acquire(nc.tensor)
            cx.rotm.acquired()
            mml = None
            for ck in range(NCK):
                mml = nc.tensor.matmul(out=bank[0:8, :], lhsT=wr[:, ck, :],
                                       rhs=cx.xc[:, ck, tb * 512:(tb + 1) * 512],
                                       start=(ck == 0), stop=(ck == NCK - 1))
            tokp = cx.rotm.commit(mml)
            nc.vector.wait_ge(cx.rotm.sp.h, tokp)
            if tb == 0:
                lgbf = P.sbuf("lgbf", [8, 512], BF16)
                lgbf_free = 0
            if lgbf_free:
                nc.vector.wait_ge(cx.s_mm.h, lgbf_free)
            v = nc.vector.tensor_copy(out=lgbf[0:8, :], in_=bank[0:8, :])
            cx.rotm.release(v)
            ev_mark = cx.s_ev.n
            tbank = cx.rott.acquire(nc.tensor)
            cx.rott.acquired()
            nc.tensor.wait_ge(cx.s_ev.h, ev_mark)
            for k in range(4):
                nc.tensor.transpose(out=tbank[:, k * 8:k * 8 + 8],
                                    in_=lgbf[0:8, k * 128:(k + 1) * 128],
                                    identity=cx.ident[0:8, 0:8])
            last = nc.tensor.transpose(out=tbank[:, 384:512], in_=cx.ident[:, :],
                                       identity=cx.ident[:, :])
            tok2 = cx.rott.commit(last)
            lgbf_free = tok2
            nc.vector.wait_ge(cx.rott.sp.h, tok2)
            for k in range(4):
                ti = tb * 4 + k
                V(cx, nc.vector.tensor_scalar(out=lg[:, ti, :], in0=tbank[:, k * 8:k * 8 + 8],
                                               scalar1=rcol[:, ti:ti + 1], scalar2=None, op0=ALU.mult))
                lastv = nc.vector.tensor_tensor(out=lg[:, ti, :], in0=lg[:, ti, :],
                                                in1=cx.tieb[:, 0:8], op=ALU.add)
                if k < 3:
                    V(cx, lastv)
            cx.rott.release(lastv)
            nc.vector.wait_ge(cx.s_ev.h, cx.s_ev.n)
        nc.vector.wait_ge(cx.s_ev.h, cx.s_ev.n)
        for ti in range(16):
            l = lg[:, ti, :]
            V(cx, nc.vector.tensor_reduce(out=m1[:, :], in_=l, axis=mybir.AxisListType.X, op=ALU.max))
            ev = V(cx, nc.vector.tensor_scalar(out=thr8[:, :], in0=l, scalar1=m1[:, 0:1], scalar2=-1.0,
                                               op0=ALU.subtract, op1=ALU.mult))
            nc.scalar.wait_ge(cx.s_ev.h, ev)
            ac = cx.s_ac.inc(nc.scalar.activation(out=thr8[:, :], in_=thr8[:, :], func=AF.Sign), 1)
            nc.vector.wait_ge(cx.s_ac.h, ac)
            V(cx, nc.vector.tensor_scalar(out=thr8[:, :], in0=thr8[:, :], scalar1=-1.0, scalar2=1.0,
                                          op0=ALU.mult, op1=ALU.add))
            V(cx, nc.vector.tensor_scalar(out=thr8[:, :], in0=thr8[:, :], scalar1=1.0, scalar2=None,
                                          op0=ALU.min))
            V(cx, nc.vector.tensor_scalar(out=tmp8[:, :], in0=l, scalar1=-NEGBIG, scalar2=None,
                                          op0=ALU.add))
            V(cx, nc.vector.tensor_tensor(out=tmp8[:, :], in0=tmp8[:, :], in1=thr8[:, :], op=ALU.mult))
            V(cx, nc.vector.tensor_tensor(out=tmp8[:, :], in0=l, in1=tmp8[:, :], op=ALU.subtract))
            V(cx, nc.vector.tensor_reduce(out=m2[:, :], in_=tmp8[:, :], axis=mybir.AxisListType.X, op=ALU.max))
            V(cx, nc.vector.tensor_scalar(out=s12[:, :], in0=m1[:, :], scalar1=m2[:, 0:1], scalar2=None,
                                          op0=ALU.subtract))
            V(cx, nc.vector.tensor_scalar(out=thr8[:, :], in0=l, scalar1=m2[:, 0:1], scalar2=-1.0,
                                          op0=ALU.subtract, op1=ALU.mult))
            ev = V(cx, nc.vector.tensor_scalar(out=tmp8[:, :], in0=l, scalar1=m1[:, 0:1], scalar2=None,
                                               op0=ALU.subtract))
            nc.scalar.wait_ge(cx.s_ev.h, ev)
            A(cx, nc.scalar.activation(out=s12[:, :], in_=s12[:, :], func=AF.Sigmoid))
            A(cx, nc.scalar.activation(out=thr8[:, :], in_=thr8[:, :], func=AF.Sign))
            ac = cx.s_ac.inc(nc.scalar.activation(out=tmp8[:, :], in_=tmp8[:, :], func=AF.Exp), 1)
            nc.vector.wait_ge(cx.s_ac.h, ac)
            V(cx, nc.vector.tensor_scalar(out=thr8[:, :], in0=thr8[:, :], scalar1=-1.0, scalar2=1.0,
                                          op0=ALU.mult, op1=ALU.add))
            V(cx, nc.vector.tensor_scalar(out=wr2[:, ti, :], in0=thr8[:, :], scalar1=1.0, scalar2=None,
                                          op0=ALU.min))
            V(cx, nc.vector.tensor_tensor(out=wr2[:, ti, :], in0=wr2[:, ti, :], in1=tmp8[:, :], op=ALU.mult))
            V(cx, nc.vector.tensor_scalar(out=wr2[:, ti, :], in0=wr2[:, ti, :], scalar1=s12[:, 0:1],
                                          scalar2=r2w[:, ti:ti + 1], op0=ALU.mult, op1=ALU.mult))
            if dbg is not None and ti == 0:
                nc.vector.tensor_copy(out=dbg_rt[:, 0:1], in_=m1[:, :])
                nc.vector.tensor_copy(out=dbg_rt[:, 1:2], in_=m2[:, :])
                nc.vector.tensor_copy(out=dbg_rt[:, 2:3], in_=s12[:, :])
                nc.vector.tensor_copy(out=dbg_rt[:, 24:32], in_=tmp8[:, :])
        v = nc.vector.tensor_copy(out=m1[:, :], in_=m1[:, :])
        routing_done = cx.s_ev.inc(v, 1)
        if dbg is not None:
            nc.sync.wait_ge(cx.s_ev.h, cx.s_ev.n)
            dma(nc.sync, dbg[0][:, :], wr2[:, :, :].rearrange("p a b -> p (a b)"), cx.s_st)
            dma(nc.sync, dbg[1][:, :], lg[:, :, :].rearrange("p a b -> p (a b)"), cx.s_st)
            dma(nc.sync, dbg[3][:, :], dbg_rt[:, :], cx.s_st)

        wfc = [P.sbuf("wfc", [128, NCK, 128], BF16) for _ in range(2)]
        wpj = [P.sbuf("wpj", [128, NCK, 256], BF16) for _ in range(2)]
        hh = P.sbuf("hh", [128, NCK, 1024], BF16)
        etmp = [P.sbuf("etmp", [128, 512], BF16) for _ in range(2)]
        etmp_free = [0, 0]
        wfc_free = [0, 0]
        wpj_free = [0, 0]
        nc.vector.wait_ge(cx.s_ev.h, routing_done)
        for e in range(8):
            for half in range(2):
                hoff = half * 1024
                hmark = None
                for dk in range(NCK):
                    slot = dk % 2
                    if wfc_free[slot]:
                        nc.gpsimd.wait_ge(cx.s_mm.h, wfc_free[slot])
                    dma(nc.gpsimd, wfc[slot][:, :, :],
                        io["w_fc"][e].rearrange("(ck p) d -> p ck d", p=128)[:, :, dk * 128:(dk + 1) * 128],
                        cx.s_ld)
                    ldmark = cx.s_ld.n
                    nc.tensor.wait_ge(cx.s_ld.h, ldmark)
                    for tb2 in range(2):
                        bank = cx.rot.acquire(nc.tensor)
                        cx.rot.acquired()
                        mm = None
                        for ck in range(NCK):
                            mm = nc.tensor.matmul(out=bank[:, :], lhsT=wfc[slot][:, ck, :],
                                                  rhs=cx.xc[:, ck, hoff + tb2 * 512: hoff + (tb2 + 1) * 512],
                                                  start=(ck == 0), stop=(ck == NCK - 1))
                        wfc_free[slot] = tokm = cx.rot.commit(mm)
                        nc.vector.wait_ge(cx.rot.sp.h, tokm)
                        if etmp_free[tb2]:
                            nc.vector.wait_ge(cx.s_ac.h, etmp_free[tb2])
                        vv = nc.vector.tensor_scalar(out=etmp[tb2][:, :], in0=bank[:, :], scalar1=0.0,
                                                     scalar2=None, op0=ALU.max)
                        cx.rot.release(vv)
                        ev = cx.s_ev.n
                        nc.scalar.wait_ge(cx.s_ev.h, ev)
                        a = nc.scalar.activation(out=hh[:, dk, tb2 * 512:(tb2 + 1) * 512],
                                                 in_=etmp[tb2][:, :], func=AF.Square)
                        hmark = cx.s_ac.inc(a, 1)
                        etmp_free[tb2] = hmark
                if dbg is not None and e == 0 and half == 0:
                    nc.sync.wait_ge(cx.s_ac.h, hmark)
                    dma(nc.sync, dbg[2][:, :], hh[:, :, :].rearrange("p a b -> p (a b)"), cx.s_st)
                for cb in range(8):
                    slot = cb % 2
                    if wpj_free[slot]:
                        nc.gpsimd.wait_ge(cx.s_mm.h, wpj_free[slot])
                    dma(nc.gpsimd, wpj[slot][:, :, :],
                        io["w_proj"][e].rearrange("(dk p) c -> p dk c", p=128)[:, :, cb * 256:(cb + 1) * 256],
                        cx.s_ld)
                    ldmark = cx.s_ld.n
                    nc.tensor.wait_ge(cx.s_ld.h, ldmark)
                    if cb == 0:
                        nc.tensor.wait_ge(cx.s_ac.h, hmark)
                    lastmm = None
                    for tk in range(8):
                        ti = half * 8 + tk
                        bank = cx.rot.acquire(nc.tensor)
                        cx.rot.acquired()
                        mm = None
                        for dk in range(NCK):
                            mm = nc.tensor.matmul(out=bank[:, 0:256], lhsT=hh[:, dk, tk * 128:(tk + 1) * 128],
                                                  rhs=wpj[slot][:, dk, :],
                                                  start=(dk == 0), stop=(dk == NCK - 1))
                        tokm = cx.rot.commit(mm)
                        lastmm = tokm
                        nc.vector.wait_ge(cx.rot.sp.h, tokm)
                        if e == 0:
                            vv = nc.vector.tensor_scalar(out=gt[:, ti, cb * 256:(cb + 1) * 256],
                                                         in0=bank[:, 0:256], scalar1=wr2[:, ti, e:e + 1],
                                                         scalar2=None, op0=ALU.mult)
                        else:
                            vv = nc.vector.scalar_tensor_tensor(out=gt[:, ti, cb * 256:(cb + 1) * 256],
                                                                in0=bank[:, 0:256], scalar=wr2[:, ti, e:e + 1],
                                                                in1=gt[:, ti, cb * 256:(cb + 1) * 256],
                                                                op0=ALU.mult, op1=ALU.add)
                        cx.rot.release(vv)
                    wpj_free[slot] = lastmm
        join_all(cx)


def build_p_ffn0(debug=False):
    P = Prog()
    nc = P.nc
    io = {
        "xc_in": P.dram("xc_in", [128, NCK * TOK], BF16, kind="ExternalInput").ap(),
        "st_in": P.dram("st_in", [1, 2 * D], F32, kind="ExternalInput").ap(),
        "w_router": P.dram("w_router", [D, 8], F32, kind="ExternalInput").ap(),
        "w_fc": P.dram("w_fc", [8, D, D], F32, kind="ExternalInput").ap(),
        "w_proj": P.dram("w_proj", [8, D, D], F32, kind="ExternalInput").ap(),
        "consts": P.dram("consts", [128, 8], F32, kind="ExternalInput").ap(),
        "onesrow": P.dram("onesrow", [1, 128], BF16, kind="ExternalInput").ap(),
        "onescol": P.dram("onescol", [128, 1], BF16, kind="ExternalInput").ap(),
        "ident": P.dram("ident", [128, 128], BF16, kind="ExternalInput").ap(),
        "ident32": P.dram("ident32", [128, 128], F32, kind="ExternalInput").ap(),
        "tieb": P.dram("tieb", [128, 16], F32, kind="ExternalInput").ap(),
    }
    g_out = P.dram("g_out", [128, NCK * TOK], BF16, kind="ExternalOutput").ap()
    if debug:
        dbg_wr2 = P.dram("dbg_wr2", [128, 16 * 8], F32, kind="ExternalOutput").ap()
        dbg_lg = P.dram("dbg_lg", [128, 16 * 8], F32, kind="ExternalOutput").ap()
        dbg_hh = P.dram("dbg_hh", [128, NCK * 1024], BF16, kind="ExternalOutput").ap()
        dbg_rt_o = P.dram("dbg_rt", [128, 32], F32, kind="ExternalOutput").ap()
    with P.stack:
        cx = alloc_common(P, io)
        dma(nc.sync, cx.xc[:, :, :].rearrange("p c t -> p (c t)"), io["xc_in"][:, :], cx.s_ld)
        dma(nc.gpsimd, cx.stat_bf[0:1, :], io["st_in"][0:1, :], cx.s_ld)
        join_all(cx)
        gt = P.sbuf("gt", [128, 16, 2048], BF16)
        emit_ffn0(cx, gt, dbg=(locals().get("dbg_wr2"), locals().get("dbg_lg"), locals().get("dbg_hh"), locals().get("dbg_rt_o")) if debug else None)
        nc.sync.wait_ge(cx.s_ev.h, cx.s_ev.n)
        dma(nc.sync, g_out[:, :], gt[:, :, :].rearrange("p c t -> p (c t)"), cx.s_st)
        nc.sync.wait_ge(cx.s_st.h, cx.s_st.n)
    return P


def emit_ffn1(cx, gt, dbg=None):
    nc = cx.P.nc
    P = cx.P
    io = cx.io
    BIG = -NEGBIG
    with P.scope():
        rcol = P.sbuf("rcol1", [128, 16], F32)
        ebias = P.sbuf("ebias1", [128, 16, 16], F32)
        gwt = P.sbuf("gwt1", [128, NCK, 16], BF16)
        lgb = P.sbuf("lgb1", [128, 16, 16], F32)
        sc = P.sbuf("sc1", [128, 16, 16], F32)
        cw = P.sbuf("cw1", [128, 16, 17], F32)
        gs = P.sbuf("gs1", [128, 4], F32)
        m1 = P.sbuf("m1f", [128, 1], F32)
        m2 = P.sbuf("m2f", [128, 1], F32)
        t4 = P.sbuf("t4f", [128, 4], F32)
        t4b = P.sbuf("t4bf", [128, 4], F32)
        t16 = P.sbuf("t16f", [128, 16], F32)
        t16b = P.sbuf("t16bf", [128, 16], F32)
        sel = P.sbuf("self", [128, 16], F32)
        emask = P.sbuf("emaskf", [128, 16], F32)
        onesq = P.sbuf("onesqf", [128, 4], F32)
        bAt = P.sbuf("bAt1", [128, 4, 512], BF16)
        dma(nc.gpsimd, gwt[:, :, :],
            io["gwT"].rearrange("(ck p) e -> p ck e", p=128)[:, :, :], cx.s_ld)
        dma(nc.sync, ebias[:, :, :].rearrange("p a b -> p (a b)"), io["ebias256"][:, :], cx.s_ld)
        nc.vector.wait_ge(cx.s_ld.h, cx.s_ld.n)
        V(cx, nc.vector.memset(onesq[:, :], 1.0))
        emit_row_to_col(cx, cx.stat_bf[0:1, D:2 * D], rcol, 16)
        nc.vector.wait_ge(cx.s_ev.h, cx.s_ev.n)
        for sub in range(4):
            emit_bcast_row(cx, cx.stat_bf[0:1, D + sub * 512:D + (sub + 1) * 512], bAt[:, sub, :], 512)
        nc.tensor.wait_ge(cx.s_ld.h, cx.s_ld.n)
        nc.tensor.wait_ge(cx.s_ev.h, cx.s_ev.n)
        for ti in range(16):
            bank = cx.rotm.acquire(nc.tensor)
            cx.rotm.acquired()
            mml = None
            for ck in range(NCK):
                mml = nc.tensor.matmul(out=bank[:, 0:16], lhsT=cx.xc[:, ck, ti * 128:(ti + 1) * 128],
                                       rhs=gwt[:, ck, :], start=(ck == 0), stop=(ck == NCK - 1))
            tokp = cx.rotm.commit(mml)
            nc.vector.wait_ge(cx.rotm.sp.h, tokp)
            v = nc.vector.tensor_scalar(out=lgb[:, ti, :], in0=bank[:, 0:16],
                                        scalar1=rcol[:, ti:ti + 1], scalar2=None, op0=ALU.mult)
            cx.rotm.release(v)
        ev = cx.s_ev.n
        nc.scalar.wait_ge(cx.s_ev.h, ev)
        ac = A(cx, nc.scalar.activation(out=sc[:, :, :].rearrange("p a b -> p (a b)"),
                                        in_=lgb[:, :, :].rearrange("p a b -> p (a b)"), func=AF.Sigmoid))
        nc.vector.wait_ge(cx.s_ac.h, ac)
        V(cx, nc.vector.tensor_tensor(out=lgb[:, :, :].rearrange("p a b -> p (a b)"),
                                      in0=sc[:, :, :].rearrange("p a b -> p (a b)"),
                                      in1=ebias[:, :, :].rearrange("p a b -> p (a b)"), op=ALU.add))
        for ti in range(16):
            V(cx, nc.vector.tensor_tensor(out=lgb[:, ti, :], in0=lgb[:, ti, :],
                                          in1=cx.tieb[:, 0:16], op=ALU.add))
        nc.vector.wait_ge(cx.s_ev.h, cx.s_ev.n)
        for ti in range(16):
            b16 = lgb[:, ti, :]
            for g in range(4):
                l4 = b16[:, g * 4:(g + 1) * 4]
                V(cx, nc.vector.tensor_reduce(out=m1[:, :], in_=l4, axis=mybir.AxisListType.X, op=ALU.max))
                ev = V(cx, nc.vector.tensor_scalar(out=t4[:, :], in0=l4, scalar1=m1[:, 0:1],
                                                   scalar2=None, op0=ALU.subtract))
                nc.scalar.wait_ge(cx.s_ev.h, ev)
                ac = cx.s_ac.inc(nc.scalar.activation(out=t4[:, :], in_=t4[:, :], func=AF.Sign), 1)
                nc.vector.wait_ge(cx.s_ac.h, ac)
                V(cx, nc.vector.tensor_scalar(out=t4[:, :], in0=t4[:, :], scalar1=1.0,
                                              scalar2=None, op0=ALU.add))
                V(cx, nc.vector.tensor_scalar(out=t4b[:, :], in0=l4, scalar1=BIG, scalar2=None,
                                              op0=ALU.add))
                V(cx, nc.vector.tensor_tensor(out=t4b[:, :], in0=t4b[:, :], in1=t4[:, :], op=ALU.mult))
                V(cx, nc.vector.tensor_tensor(out=t4b[:, :], in0=l4, in1=t4b[:, :], op=ALU.subtract))
                V(cx, nc.vector.tensor_reduce(out=m2[:, :], in_=t4b[:, :], axis=mybir.AxisListType.X,
                                              op=ALU.max))
                V(cx, nc.vector.tensor_scalar(out=gs[:, g:g + 1], in0=m1[:, :], scalar1=m2[:, 0:1],
                                              scalar2=None, op0=ALU.add))
            V(cx, nc.vector.tensor_reduce(out=m1[:, :], in_=gs[:, :], axis=mybir.AxisListType.X, op=ALU.max))
            ev = V(cx, nc.vector.tensor_scalar(out=t4[:, :], in0=gs[:, :], scalar1=m1[:, 0:1],
                                               scalar2=None, op0=ALU.subtract))
            nc.scalar.wait_ge(cx.s_ev.h, ev)
            ac = cx.s_ac.inc(nc.scalar.activation(out=t4[:, :], in_=t4[:, :], func=AF.Sign), 1)
            nc.vector.wait_ge(cx.s_ac.h, ac)
            V(cx, nc.vector.tensor_scalar(out=t4[:, :], in0=t4[:, :], scalar1=1.0, scalar2=None,
                                          op0=ALU.add))
            V(cx, nc.vector.tensor_scalar(out=t4b[:, :], in0=gs[:, :], scalar1=BIG, scalar2=None,
                                          op0=ALU.add))
            V(cx, nc.vector.tensor_tensor(out=t4b[:, :], in0=t4b[:, :], in1=t4[:, :], op=ALU.mult))
            V(cx, nc.vector.tensor_tensor(out=t4b[:, :], in0=gs[:, :], in1=t4b[:, :], op=ALU.subtract))
            ev = V(cx, nc.vector.tensor_reduce(out=m2[:, :], in_=t4b[:, :], axis=mybir.AxisListType.X,
                                               op=ALU.max))
            V(cx, nc.vector.tensor_scalar(out=t4[:, :], in0=gs[:, :], scalar1=m2[:, 0:1],
                                          scalar2=None, op0=ALU.subtract))
            ev = cx.s_ev.n
            nc.scalar.wait_ge(cx.s_ev.h, ev)
            ac = cx.s_ac.inc(nc.scalar.activation(out=t4[:, :], in_=t4[:, :], func=AF.Sign), 1)
            nc.vector.wait_ge(cx.s_ac.h, ac)
            V(cx, nc.vector.tensor_scalar(out=t4[:, :], in0=t4[:, :], scalar1=1.0, scalar2=1.0,
                                          op0=ALU.add, op1=ALU.min))
            for g in range(4):
                V(cx, nc.vector.tensor_scalar(out=emask[:, g * 4:(g + 1) * 4], in0=onesq[:, :],
                                              scalar1=t4[:, g:g + 1], scalar2=None, op0=ALU.mult))
            V(cx, nc.vector.tensor_scalar(out=t16b[:, :], in0=emask[:, :], scalar1=BIG,
                                          scalar2=-BIG, op0=ALU.mult, op1=ALU.add))
            V(cx, nc.vector.tensor_tensor(out=t16[:, :], in0=b16, in1=emask[:, :], op=ALU.mult))
            V(cx, nc.vector.tensor_tensor(out=t16[:, :], in0=t16[:, :], in1=t16b[:, :], op=ALU.add))
            for it in range(4):
                V(cx, nc.vector.tensor_reduce(out=m1[:, :], in_=t16[:, :], axis=mybir.AxisListType.X,
                                              op=ALU.max))
                ev = V(cx, nc.vector.tensor_scalar(out=t16b[:, :], in0=t16[:, :], scalar1=m1[:, 0:1],
                                                   scalar2=None, op0=ALU.subtract))
                nc.scalar.wait_ge(cx.s_ev.h, ev)
                ac = cx.s_ac.inc(nc.scalar.activation(out=t16b[:, :], in_=t16b[:, :], func=AF.Sign), 1)
                nc.vector.wait_ge(cx.s_ac.h, ac)
                V(cx, nc.vector.tensor_scalar(out=t16b[:, :], in0=t16b[:, :], scalar1=1.0,
                                              scalar2=None, op0=ALU.add))
                if it == 0:
                    V(cx, nc.vector.tensor_copy(out=sel[:, :], in_=t16b[:, :]))
                else:
                    V(cx, nc.vector.tensor_tensor(out=sel[:, :], in0=sel[:, :], in1=t16b[:, :], op=ALU.add))
                if it < 3:
                    V(cx, nc.vector.tensor_scalar(out=t16b[:, :], in0=t16b[:, :], scalar1=BIG,
                                                  scalar2=None, op0=ALU.mult))
                    V(cx, nc.vector.tensor_tensor(out=t16[:, :], in0=t16[:, :], in1=t16b[:, :],
                                                  op=ALU.subtract))
            V(cx, nc.vector.tensor_tensor(out=t16[:, :], in0=sel[:, :], in1=sc[:, ti, :], op=ALU.mult))
            V(cx, nc.vector.tensor_reduce(out=m1[:, :], in_=t16[:, :], axis=mybir.AxisListType.X,
                                          op=ALU.add))
            ev = V(cx, nc.vector.tensor_scalar(out=m1[:, :], in0=m1[:, :], scalar1=1e-20, scalar2=None,
                                               op0=ALU.add))
            V(cx, nc.vector.reciprocal(out=m2[:, :], in_=m1[:, :]))
            V(cx, nc.vector.tensor_scalar(out=m2[:, :], in0=m2[:, :], scalar1=2.5,
                                          scalar2=rcol[:, ti:ti + 1], op0=ALU.mult, op1=ALU.mult))
            V(cx, nc.vector.tensor_scalar(out=cw[:, ti, 0:16], in0=t16[:, :], scalar1=m2[:, 0:1],
                                          scalar2=None, op0=ALU.mult))
            V(cx, nc.vector.tensor_copy(out=cw[:, ti, 16:17], in_=rcol[:, ti:ti + 1]))
            if dbg is not None and ti == 0:
                nc.sync.wait_ge(cx.s_ev.h, cx.s_ev.n)
                dma(nc.sync, dbg[0][:, :], sel[:, :], cx.s_st)
                dma(nc.sync, dbg[1][:, :], sc[:, :, :].rearrange("p a b -> p (a b)"), cx.s_st)
                nc.vector.wait_ge(cx.s_st.h, cx.s_st.n)
        v = nc.vector.tensor_copy(out=m1[:, :], in_=m1[:, :])
        routing_done = cx.s_ev.inc(v, 1)
        nc.vector.wait_ge(cx.s_ev.h, routing_done)
        if dbg is not None:
            nc.sync.wait_ge(cx.s_ev.h, cx.s_ev.n)
            dma(nc.sync, dbg[2][:, :], cw[:, :, :].rearrange("p a b -> p (a b)"), cx.s_st)
            nc.vector.wait_ge(cx.s_st.h, cx.s_st.n)

        wga = [P.sbuf("wga", [128, NCK, 128], BF16) for _ in range(2)]
        wup = [P.sbuf("wup", [128, NCK, 128], BF16) for _ in range(2)]
        wdn = [P.sbuf("wdn", [128, 8, 512], BF16) for _ in range(2)]
        hh1 = P.sbuf("hh1", [128, 8, 1024], BF16)
        et = [P.sbuf("et1", [128, 512], F32) for _ in range(2)]
        et_free = [0, 0]
        w_free = [0, 0]
        wdn_free = [0, 0]
        for e in range(17):
            if e < 16:
                gsrc, usrc, dsrc = io["gT"], io["uT"], io["dT"]
                goff, doff = e * 1024, e * 2048
            else:
                gsrc, usrc, dsrc = io["shgT"], io["shuT"], io["shdT"]
                goff, doff = 0, 0
            for half in range(2):
                hoff = half * 1024
                hmark = None
                for dk in range(8):
                    slot = dk % 2
                    if w_free[slot]:
                        nc.gpsimd.wait_ge(cx.s_mm.h, w_free[slot])
                    dma(nc.gpsimd, wga[slot][:, :, :],
                        gsrc.rearrange("(ck p) f -> p ck f", p=128)[:, :, goff + dk * 128:goff + (dk + 1) * 128],
                        cx.s_ld)
                    dma(nc.gpsimd, wup[slot][:, :, :],
                        usrc.rearrange("(ck p) f -> p ck f", p=128)[:, :, goff + dk * 128:goff + (dk + 1) * 128],
                        cx.s_ld)
                    ldmark = cx.s_ld.n
                    nc.tensor.wait_ge(cx.s_ld.h, ldmark)
                    for tb2 in range(2):
                        sub = half * 2 + tb2
                        t0 = sub * 512
                        bankg = cx.rot.acquire(nc.tensor)
                        cx.rot.acquired()
                        mm = None
                        for ck in range(NCK):
                            mm = nc.tensor.matmul(out=bankg[:, :], lhsT=wga[slot][:, ck, :],
                                                  rhs=cx.xc[:, ck, t0:t0 + 512],
                                                  start=(ck == 0), stop=(ck == NCK - 1))
                        tokg = cx.rot.commit(mm)
                        banku = cx.rot.acquire(nc.tensor)
                        cx.rot.acquired()
                        mm = None
                        for ck in range(NCK):
                            mm = nc.tensor.matmul(out=banku[:, :], lhsT=wup[slot][:, ck, :],
                                                  rhs=cx.xc[:, ck, t0:t0 + 512],
                                                  start=(ck == 0), stop=(ck == NCK - 1))
                        toku = cx.rot.commit(mm)
                        nc.vector.wait_ge(cx.rot.sp.h, toku)
                        if et_free[tb2]:
                            nc.vector.wait_ge(cx.s_ac.h, et_free[tb2])
                        vv = nc.vector.tensor_tensor(out=et[tb2][:, :], in0=bankg[:, :],
                                                     in1=bAt[:, sub, :], op=ALU.mult)
                        cx.rot.release(vv)
                        ev = cx.s_ev.n
                        nc.scalar.wait_ge(cx.s_ev.h, ev)
                        acv = A(cx, nc.scalar.activation(out=et[tb2][:, :], in_=et[tb2][:, :], func=AF.Silu))
                        et_free[tb2] = acv
                        nc.vector.wait_ge(cx.s_ac.h, acv)
                        vv = nc.vector.tensor_tensor(out=hh1[:, dk, tb2 * 512:(tb2 + 1) * 512],
                                                     in0=et[tb2][:, :], in1=banku[:, :], op=ALU.mult)
                        cx.rot.release(vv)
                        hmark = cx.s_ev.n
                    w_free[slot] = cx.s_mm.n
                nc.tensor.wait_ge(cx.s_ev.h, hmark)
                for cb in range(4):
                    slot = cb % 2
                    if wdn_free[slot]:
                        nc.gpsimd.wait_ge(cx.s_mm.h, wdn_free[slot])
                    dma(nc.gpsimd, wdn[slot][:, :, :],
                        dsrc.rearrange("(dk p) c -> p dk c", p=128)[:, :, doff + cb * 512:doff + (cb + 1) * 512],
                        cx.s_ld)
                    ldmark = cx.s_ld.n
                    nc.tensor.wait_ge(cx.s_ld.h, ldmark)
                    lastmm = None
                    for tk in range(8):
                        ti = half * 8 + tk
                        bank = cx.rot.acquire(nc.tensor)
                        cx.rot.acquired()
                        mm = None
                        for dk in range(8):
                            mm = nc.tensor.matmul(out=bank[:, :],
                                                  lhsT=hh1[:, dk, tk * 128:(tk + 1) * 128],
                                                  rhs=wdn[slot][:, dk, :],
                                                  start=(dk == 0), stop=(dk == 7))
                        tokm = cx.rot.commit(mm)
                        lastmm = tokm
                        nc.vector.wait_ge(cx.rot.sp.h, tokm)
                        if e == 0:
                            vv = nc.vector.tensor_scalar(out=gt[:, ti, cb * 512:(cb + 1) * 512],
                                                         in0=bank[:, :], scalar1=cw[:, ti, e:e + 1],
                                                         scalar2=None, op0=ALU.mult)
                        else:
                            vv = nc.vector.scalar_tensor_tensor(out=gt[:, ti, cb * 512:(cb + 1) * 512],
                                                                in0=bank[:, :], scalar=cw[:, ti, e:e + 1],
                                                                in1=gt[:, ti, cb * 512:(cb + 1) * 512],
                                                                op0=ALU.mult, op1=ALU.add)
                        cx.rot.release(vv)
                    wdn_free[slot] = lastmm
        join_all(cx)


def build_p_ffn1(debug=False):
    P = Prog()
    nc = P.nc
    NE = 16
    io = {
        "xc_in": P.dram("xc_in", [128, NCK * TOK], BF16, kind="ExternalInput").ap(),
        "st_in": P.dram("st_in", [1, 2 * D], F32, kind="ExternalInput").ap(),
        "gate_weight": P.dram("gate_weight", [NE, D], F32, kind="ExternalInput").ap(),
        "ebias256": P.dram("ebias256", [128, 256], F32, kind="ExternalInput").ap(),
        "gate_proj": P.dram("gate_proj", [NE, 1024, D], F32, kind="ExternalInput").ap(),
        "up_proj": P.dram("up_proj", [NE, 1024, D], F32, kind="ExternalInput").ap(),
        "down_proj": P.dram("down_proj", [NE, D, 1024], F32, kind="ExternalInput").ap(),
        "shared_gate": P.dram("shared_gate", [1024, D], F32, kind="ExternalInput").ap(),
        "shared_up": P.dram("shared_up", [1024, D], F32, kind="ExternalInput").ap(),
        "shared_down": P.dram("shared_down", [D, 1024], F32, kind="ExternalInput").ap(),
        "consts": P.dram("consts", [128, 8], F32, kind="ExternalInput").ap(),
        "onesrow": P.dram("onesrow", [1, 128], BF16, kind="ExternalInput").ap(),
        "onescol": P.dram("onescol", [128, 1], BF16, kind="ExternalInput").ap(),
        "ident": P.dram("ident", [128, 128], BF16, kind="ExternalInput").ap(),
        "ident32": P.dram("ident32", [128, 128], F32, kind="ExternalInput").ap(),
        "tieb": P.dram("tieb", [128, 16], F32, kind="ExternalInput").ap(),
    }
    g_out = P.dram("g_out", [128, NCK * TOK], BF16, kind="ExternalOutput").ap()
    if debug:
        dbg = (P.dram("dbg_sel", [128, 16], F32, kind="ExternalOutput").ap(),
               P.dram("dbg_sc", [128, 256], F32, kind="ExternalOutput").ap(),
               P.dram("dbg_cw", [128, 16 * 17], F32, kind="ExternalOutput").ap())
    scr = {}
    for nm, rows, cols in [("gw", NE, D), ("g", NE * 1024, D), ("u", NE * 1024, D),
                           ("d", NE * D, 1024), ("shg", 1024, D), ("shu", 1024, D), ("shd", D, 1024)]:
        scr[nm + "_plain"] = P.dram(nm + "_plain", [rows, cols], BF16)
        scr[nm + "T"] = P.dram(nm + "T", [cols, rows], BF16)
    io["gwT"] = scr["gwT"].ap()
    io["gT"] = scr["gT"].ap()
    io["uT"] = scr["uT"].ap()
    io["dT"] = scr["dT"].ap()
    io["shgT"] = scr["shgT"].ap()
    io["shuT"] = scr["shuT"].ap()
    io["shdT"] = scr["shdT"].ap()
    with P.stack:
        cx = alloc_common(P, io)
        dma(nc.sync, cx.xc[:, :, :].rearrange("p c t -> p (c t)"), io["xc_in"][:, :], cx.s_ld)
        dma(nc.gpsimd, cx.stat_bf[0:1, :], io["st_in"][0:1, :], cx.s_ld)
        join_all(cx)
        for nm, src, rows, cols in [
                ("gw", io["gate_weight"], NE, D),
                ("g", io["gate_proj"].rearrange("a b c -> (a b) c"), NE * 1024, D),
                ("u", io["up_proj"].rearrange("a b c -> (a b) c"), NE * 1024, D),
                ("d", io["down_proj"].rearrange("a b c -> (a b) c"), NE * D, 1024),
                ("shg", io["shared_gate"], 1024, D),
                ("shu", io["shared_up"], 1024, D),
                ("shd", io["shared_down"], D, 1024)]:
            emit_build_scratchT(cx, src, scr[nm + "_plain"].ap(), scr[nm + "T"].ap(), rows, cols)
            join_all(cx)
        gt = P.sbuf("gt1", [128, 16, 2048], BF16)
        emit_ffn1(cx, gt, dbg=dbg if debug else None)
        nc.sync.wait_ge(cx.s_ev.h, cx.s_ev.n)
        dma(nc.sync, g_out[:, :], gt[:, :, :].rearrange("p c t -> p (c t)"), cx.s_st)
        nc.sync.wait_ge(cx.s_st.h, cx.s_st.n)
    return P


def emit_attn1A(cx, outs):
    nc = cx.P.nc
    P = cx.P
    io = cx.io
    with P.scope():
        ffmb = P.sbuf("ffmb6", [128, TOK], BF16)
        gdmb = P.sbuf("gdmb6", [128, TOK], BF16)
        c0sav = P.sbuf("c0sav6", [128, 1], F32)
        hbnd = P.sbuf("hbnd6", [128, NCK], F32)
        selab = P.sbuf("selab6", [128, 2], F32)
        bAt = P.sbuf("bAt6", [128, 4, 512], BF16)
        wf = P.sbuf("wf6", [128, NCK, 128], BF16)
        wc = P.sbuf("wc6", [128, NCK, 128], BF16)
        w_free = 0
        et0 = P.sbuf("et06", [128, 512], F32)
        etc = P.sbuf("etc6", [128, 512], F32)
        hstg = P.sbuf("hstg6", [128, 512], F32)
        hbf = [P.sbuf("hbf6", [128, 512], BF16) for _ in range(2)]
        hbf_free = [0, 0]
        dma(nc.sync, hbnd[:, :], io["h_in"][:, :], cx.s_ld)
        dma(nc.sync, selab[:, :], io["selab"][:, :], cx.s_ld)
        for sub in range(4):
            emit_bcast_row(cx, cx.stat_bf[0:1, D + sub * 512:D + (sub + 1) * 512], bAt[:, sub, :], 512)
        nc.vector.wait_ge(cx.s_ld.h, cx.s_ld.n)
        nc.tensor.wait_ge(cx.s_ev.h, cx.s_ev.n)
        et_mark = 0
        for mb in range(NCK):
            if w_free:
                nc.gpsimd.wait_ge(cx.s_mm.h, w_free)
            dma(nc.gpsimd, wf[:, :, :],
                io["Wf"].rearrange("(ck p) e -> p ck e", p=128)[:, :, mb * 128:(mb + 1) * 128], cx.s_ld)
            dma(nc.gpsimd, wc[:, :, :],
                io["Wc"].rearrange("(ck p) e -> p ck e", p=128)[:, :, mb * 128:(mb + 1) * 128], cx.s_ld)
            ldm = cx.s_ld.n
            nc.tensor.wait_ge(cx.s_ld.h, ldm)
            for sub in range(4):
                t0 = sub * 512
                bankf = cx.rot.acquire(nc.tensor)
                cx.rot.acquired()
                mm = None
                for ck in range(NCK):
                    mm = nc.tensor.matmul(out=bankf[:, :], lhsT=wf[:, ck, :],
                                          rhs=cx.xc[:, ck, t0:t0 + 512],
                                          start=(ck == 0), stop=(ck == NCK - 1))
                tokf = cx.rot.commit(mm)
                bankc = cx.rot.acquire(nc.tensor)
                cx.rot.acquired()
                mm = None
                for ck in range(NCK):
                    mm = nc.tensor.matmul(out=bankc[:, :], lhsT=wc[:, ck, :],
                                          rhs=cx.xc[:, ck, t0:t0 + 512],
                                          start=(ck == 0), stop=(ck == NCK - 1))
                tokc = cx.rot.commit(mm)
                nc.vector.wait_ge(cx.rot.sp.h, tokc)
                if et_mark:
                    nc.vector.wait_ge(cx.s_ac.h, et_mark)
                vv = nc.vector.tensor_tensor(out=et0[:, :], in0=bankf[:, :], in1=bAt[:, sub, :],
                                             op=ALU.mult)
                cx.rot.release(vv)
                vv = nc.vector.tensor_tensor(out=etc[:, :], in0=bankc[:, :], in1=bAt[:, sub, :],
                                             op=ALU.mult)
                cx.rot.release(vv)
                ev = cx.s_ev.n
                nc.scalar.wait_ge(cx.s_ev.h, ev)
                nc.scalar.activation(out=ffmb[:, t0:t0 + 512], in_=et0[:, :], func=AF.Sigmoid)
                nc.scalar.activation(out=gdmb[:, t0:t0 + 512], in_=et0[:, :], func=AF.Sigmoid,
                                     bias=0.0, scale=-1.0)
                acm = A(cx, nc.scalar.activation(out=etc[:, :], in_=etc[:, :], func=AF.Silu))
                et_mark = acm
                nc.vector.wait_ge(cx.s_ac.h, acm)
                vv = nc.vector.tensor_tensor(out=gdmb[:, t0:t0 + 512], in0=gdmb[:, t0:t0 + 512],
                                             in1=etc[:, :], op=ALU.mult)
                if sub == 0:
                    V(cx, vv)
                    V(cx, nc.vector.tensor_copy(out=c0sav[:, 0:1], in_=etc[:, 0:1]))
            w_free = cx.s_mm.n
            V(cx, nc.vector.tensor_scalar(out=et0[:, 0:1], in0=gdmb[:, 0:1],
                                          scalar1=selab[:, 0:1], scalar2=None, op0=ALU.mult))
            V(cx, nc.vector.scalar_tensor_tensor(out=gdmb[:, 0:1], in0=c0sav[:, 0:1],
                                                 scalar=selab[:, 1:2], in1=et0[:, 0:1],
                                                 op0=ALU.mult, op1=ALU.add))
            for sub in range(4):
                t0 = sub * 512
                V(cx, nc.vector.tensor_tensor_scan(out=hstg[:, :], data0=ffmb[:, t0:t0 + 512],
                                                   data1=gdmb[:, t0:t0 + 512],
                                                   initial=hbnd[:, mb:mb + 1],
                                                   op0=ALU.mult, op1=ALU.add))
                V(cx, nc.vector.tensor_copy(out=hbnd[:, mb:mb + 1], in_=hstg[:, 511:512]))
                sl = sub % 2
                if hbf_free[sl]:
                    nc.vector.wait_ge(cx.s_st.h, hbf_free[sl])
                ev = V(cx, nc.vector.tensor_copy(out=hbf[sl][:, :], in_=hstg[:, :]))
                nc.sync.wait_ge(cx.s_ev.h, ev)
                dma(nc.sync, outs["h_out"][:, mb * TOK + t0:mb * TOK + t0 + 512], hbf[sl][:, :], cx.s_st)
                hbf_free[sl] = cx.s_st.n
        ev = cx.s_ev.n
        nc.sync.wait_ge(cx.s_ev.h, ev)
        dma(nc.sync, outs["hfin_out"][:, :], hbnd[:, :], cx.s_st)
        nc.sync.wait_ge(cx.s_st.h, cx.s_st.n)
        join_all(cx)


def build_p_attn1A():
    P = Prog()
    nc = P.nc
    io = {
        "xc_in": P.dram("xc_in", [128, NCK * TOK], BF16, kind="ExternalInput").ap(),
        "st_in": P.dram("st_in", [1, 2 * D], F32, kind="ExternalInput").ap(),
        "Wf": P.dram("Wf", [D, D], F32, kind="ExternalInput").ap(),
        "Wc": P.dram("Wc", [D, D], F32, kind="ExternalInput").ap(),
        "h_in": P.dram("h_in", [128, NCK], F32, kind="ExternalInput").ap(),
        "selab": P.dram("selab", [128, 2], F32, kind="ExternalInput").ap(),
        "consts": P.dram("consts", [128, 8], F32, kind="ExternalInput").ap(),
        "onesrow": P.dram("onesrow", [1, 128], BF16, kind="ExternalInput").ap(),
        "onescol": P.dram("onescol", [128, 1], BF16, kind="ExternalInput").ap(),
        "ident": P.dram("ident", [128, 128], BF16, kind="ExternalInput").ap(),
        "ident32": P.dram("ident32", [128, 128], F32, kind="ExternalInput").ap(),
        "tieb": P.dram("tieb", [128, 16], F32, kind="ExternalInput").ap(),
    }
    outs = {
        "h_out": P.dram("h_out", [128, NCK * TOK], BF16, kind="ExternalOutput").ap(),
        "hfin_out": P.dram("hfin_out", [128, NCK], F32, kind="ExternalOutput").ap(),
    }
    with P.stack:
        cx = alloc_common(P, io)
        dma(nc.sync, cx.xc[:, :, :].rearrange("p c t -> p (c t)"), io["xc_in"][:, :], cx.s_ld)
        dma(nc.gpsimd, cx.stat_bf[0:1, :], io["st_in"][0:1, :], cx.s_ld)
        join_all(cx)
        emit_attn1A(cx, outs)
    return P


def emit_attn1B(cx, io, out_f32):
    nc = cx.P.nc
    P = cx.P
    with P.scope():
        bAr = P.sbuf("bAr8", [128, 512], BF16)
        bMU = P.sbuf("bMU8", [128, 512], BF16)
        bRR = P.sbuf("bRR8", [128, 512], F32)
        lnsc = P.sbuf("lnsc8", [128, NCK], F32)
        ones32 = P.sbuf("ones328", [1, 128], F32)
        hs = P.sbuf("hs8", [128, NCK, 512], BF16)
        gc = P.sbuf("gc8", [128, NCK, 512], BF16)
        rs = P.sbuf("rs8", [128, NCK, 512], BF16)
        hp = P.sbuf("hp8", [128, NCK, 512], BF16)
        wr = [P.sbuf("wr8", [128, NCK, 128], BF16) for _ in range(2)]
        wo = [P.sbuf("wo8", [128, NCK, 128], BF16) for _ in range(2)]
        w_free = [0, 0]
        wo_free = [0, 0]
        sq2 = P.sbuf("sq28", [128, 1024], BF16)
        strow = P.sbuf("strow8", [1, 512], F32)
        et = P.sbuf("et8", [128, 512], F32)
        sgt = P.sbuf("sgt8", [128, 512], F32)
        x2b = P.sbuf("x2b8", [128, 512], BF16)
        fstg = [P.sbuf("fstg8", [128, 128], F32) for _ in range(2)]
        fstg_free = [0, 0]
        dma(nc.sync, lnsc[:, :], io["lnsc"][:, :], cx.s_ld)
        nc.vector.wait_ge(cx.s_ld.h, cx.s_ld.n)
        ev = V(cx, nc.vector.memset(ones32[0:1, :], 1.0))
        nc.tensor.wait_ge(cx.s_ev.h, ev)
        for sub in range(NSUB):
            t0 = sub * SUB
            emit_bcast_row(cx, cx.stat_bf[0:1, D + t0:D + t0 + 512], bAr[:, :], 512)
            emit_bcast_row(cx, cx.stat_bf[0:1, t0:t0 + 512], bMU[:, :], 512)
            gl_pre = cx.s_ev.n
            nc.sync.wait_ge(cx.s_ev.h, gl_pre)
            nc.sync.wait_ge(cx.s_ac.h, cx.s_ac.n)
            nc.sync.wait_ge(cx.s_mm.h, cx.s_mm.n)
            dma(nc.sync, hs[:, :, :].rearrange("p a b -> p (a b)"),
                io["h_in_full"].rearrange("p (c t) -> p c t", c=NCK)[:, :, t0:t0 + 512], cx.s_ld)
            for ck in range(NCK):
                dma_t(nc.sync, gc[:, ck, :], io["g_in"][t0:t0 + 512, ck * 128:(ck + 1) * 128], cx.s_ld)
            yldm = cx.s_ld.n
            nc.vector.wait_ge(cx.s_ld.h, yldm)
            nc.tensor.wait_ge(cx.s_ld.h, yldm)
            nc.scalar.wait_ge(cx.s_ld.h, yldm)
            et_mark = 0
            for mb in range(NCK):
                slot = mb % 2
                if w_free[slot]:
                    nc.gpsimd.wait_ge(cx.s_mm.h, w_free[slot])
                dma(nc.gpsimd, wr[slot][:, :, :],
                    io["Wr"].rearrange("(ck p) e -> p ck e", p=128)[:, :, mb * 128:(mb + 1) * 128],
                    cx.s_ld)
                ldm = cx.s_ld.n
                nc.tensor.wait_ge(cx.s_ld.h, ldm)
                bank = cx.rot.acquire(nc.tensor)
                cx.rot.acquired()
                mm = None
                for ck in range(NCK):
                    mm = nc.tensor.matmul(out=bank[:, :], lhsT=wr[slot][:, ck, :],
                                          rhs=cx.xc[:, ck, t0:t0 + 512],
                                          start=(ck == 0), stop=(ck == NCK - 1))
                tokm = cx.rot.commit(mm)
                nc.vector.wait_ge(cx.rot.sp.h, tokm)
                if et_mark:
                    nc.vector.wait_ge(cx.s_ac.h, et_mark)
                vv = nc.vector.tensor_tensor(out=et[:, :], in0=bank[:, :], in1=bAr[:, :], op=ALU.mult)
                cx.rot.release(vv)
                ev = cx.s_ev.n
                nc.scalar.wait_ge(cx.s_ev.h, ev)
                et_mark = A(cx, nc.scalar.activation(out=rs[:, mb, :], in_=et[:, :], func=AF.Silu))
                w_free[slot] = cx.s_mm.n
            banks_ = cx.rotm.acquire(nc.tensor)
            cx.rotm.acquired()
            mmq = None
            prev_q = [0, 0]
            nc.scalar.wait_ge(cx.s_ac.h, cx.s_ac.n)
            for ck in range(NCK):
                sl = sq2[:, (ck % 2) * 512:(ck % 2) * 512 + 512]
                if prev_q[ck % 2]:
                    nc.scalar.wait_ge(cx.s_mm.h, prev_q[ck % 2])
                a = nc.scalar.activation(out=sl, in_=hs[:, ck, :], func=AF.Square)
                am = cx.s_ac.inc(a, 1)
                nc.tensor.wait_ge(cx.s_ac.h, am)
                mmq = nc.tensor.matmul(out=banks_[0:1, :], lhsT=cx.onescol[:, :], rhs=sl,
                                       start=(ck == 0), stop=(ck == NCK - 1), skip_group_check=True)
                if ck < NCK - 1:
                    prev_q[ck % 2] = cx.s_mm.inc(mmq, 1)
            tok = cx.rotm.commit(mmq)
            nc.vector.wait_ge(cx.rotm.sp.h, tok)
            v1 = nc.vector.tensor_scalar(out=strow[0:1, :], in0=banks_[0:1, :],
                                         scalar1=1.0 / D, scalar2=None, op0=ALU.mult)
            cx.rotm.release(v1)
            ev = cx.s_ev.n
            nc.scalar.wait_ge(cx.s_ev.h, ev)
            nc.scalar.activation(out=strow[0:1, :], in_=strow[0:1, :], func=AF.Ln,
                                 bias=cx.consts[0:1, 2:3], scale=1.0)
            ac = A(cx, nc.scalar.activation(out=strow[0:1, :], in_=strow[0:1, :], func=AF.Exp,
                                            bias=0.0, scale=-0.5))
            nc.tensor.wait_ge(cx.s_ac.h, ac)
            emit_bcast_row32(cx, ones32, strow[0:1, :], bRR[:, :], 512)
            evb = cx.s_ev.n
            nc.vector.wait_ge(cx.s_ev.h, evb)
            for ck in range(NCK):
                nc.vector.tensor_tensor(out=et[:, :], in0=hs[:, ck, :], in1=bRR[:, :], op=ALU.mult)
                nc.vector.tensor_scalar(out=et[:, :], in0=et[:, :], scalar1=lnsc[:, ck:ck + 1],
                                        scalar2=None, op0=ALU.mult)
                nc.vector.tensor_tensor(out=hp[:, ck, :], in0=et[:, :], in1=rs[:, ck, :], op=ALU.mult)
            lnd = cx.s_ev.inc(nc.vector.tensor_copy(out=et[:, 0:1], in_=hp[:, 0, 0:1]), 1)
            nc.tensor.wait_ge(cx.s_ev.h, lnd)
            nc.scalar.wait_ge(cx.s_ev.h, lnd)
            prev_vv = 0
            for mb in range(16):
                slot = mb % 2
                if wo_free[slot]:
                    nc.gpsimd.wait_ge(cx.s_mm.h, wo_free[slot])
                dma(nc.gpsimd, wo[slot][:, :, :],
                    io["Wo"].rearrange("(ck p) e -> p ck e", p=128)[:, :, mb * 128:(mb + 1) * 128],
                    cx.s_ld)
                ldm = cx.s_ld.n
                nc.tensor.wait_ge(cx.s_ld.h, ldm)
                bank = cx.rot.acquire(nc.tensor)
                cx.rot.acquired()
                mm = None
                for ck in range(NCK):
                    mm = nc.tensor.matmul(out=bank[:, :], lhsT=wo[slot][:, ck, :],
                                          rhs=hp[:, ck, :],
                                          start=(ck == 0), stop=(ck == NCK - 1))
                tokm = cx.rot.commit(mm)
                if prev_vv:
                    nc.scalar.wait_ge(cx.s_ev.h, prev_vv)
                acv = A(cx, nc.scalar.activation(out=sgt[:, :], in_=gc[:, mb, :], func=AF.Silu))
                nc.vector.wait_ge(cx.s_ac.h, acv)
                nc.vector.wait_ge(cx.rot.sp.h, tokm)
                vv = nc.vector.tensor_tensor(out=et[:, :], in0=bank[:, :], in1=sgt[:, :], op=ALU.mult)
                cx.rot.release(vv)
                prev_vv = cx.s_ev.n
                nc.vector.tensor_tensor(out=et[:, :], in0=et[:, :], in1=bMU[:, :], op=ALU.add)
                ev = V(cx, nc.vector.tensor_tensor(out=x2b[:, :], in0=cx.xc[:, mb, t0:t0 + 512],
                                                   in1=et[:, :], op=ALU.add))
                nc.tensor.wait_ge(cx.s_ev.h, ev)
                for tk in range(4):
                    tb = cx.rott.acquire(nc.tensor)
                    cx.rott.acquired()
                    ttok = cx.rott.commit(nc.tensor.transpose(out=tb[:, 0:128],
                                                              in_=x2b[:, tk * 128:(tk + 1) * 128],
                                                              identity=cx.ident[:, :]))
                    sl = tk % 2
                    if fstg_free[sl]:
                        nc.vector.wait_ge(cx.s_st.h, fstg_free[sl])
                    nc.vector.wait_ge(cx.rott.sp.h, ttok)
                    lv = nc.vector.tensor_copy(out=fstg[sl][:, :], in_=tb[:, 0:128])
                    cx.rott.release(lv)
                    ev = cx.s_ev.n
                    nc.sync.wait_ge(cx.s_ev.h, ev)
                    dma(nc.sync, out_f32[t0 + tk * 128:t0 + (tk + 1) * 128, mb * 128:(mb + 1) * 128],
                        fstg[sl][:, :], cx.s_st)
                    fstg_free[sl] = cx.s_st.n
                wo_free[slot] = cx.s_mm.n
        nc.sync.wait_ge(cx.s_st.h, cx.s_st.n)
        join_all(cx)


def build_p_attn1B():
    P = Prog()
    nc = P.nc
    io = {
        "xc_in": P.dram("xc_in", [128, NCK * TOK], BF16, kind="ExternalInput").ap(),
        "st_in": P.dram("st_in", [1, 2 * D], F32, kind="ExternalInput").ap(),
        "Wr": P.dram("Wr", [D, D], F32, kind="ExternalInput").ap(),
        "Wo": P.dram("Wo", [D, D], F32, kind="ExternalInput").ap(),
        "lnsc": P.dram("lnsc", [128, NCK], F32, kind="ExternalInput").ap(),
        "h_in_full": P.dram("h_in_full", [128, NCK * TOK], BF16, kind="ExternalInput").ap(),
        "g_in": P.dram("g_in", [TOK, D], BF16, kind="ExternalInput").ap(),
        "consts": P.dram("consts", [128, 8], F32, kind="ExternalInput").ap(),
        "onesrow": P.dram("onesrow", [1, 128], BF16, kind="ExternalInput").ap(),
        "onescol": P.dram("onescol", [128, 1], BF16, kind="ExternalInput").ap(),
        "ident": P.dram("ident", [128, 128], BF16, kind="ExternalInput").ap(),
        "ident32": P.dram("ident32", [128, 128], F32, kind="ExternalInput").ap(),
        "tieb": P.dram("tieb", [128, 16], F32, kind="ExternalInput").ap(),
    }
    out_f32 = P.dram("x2_out", [TOK, D], F32, kind="ExternalOutput").ap()
    with P.stack:
        cx = alloc_common(P, io)
        dma(nc.sync, cx.xc[:, :, :].rearrange("p c t -> p (c t)"), io["xc_in"][:, :], cx.s_ld)
        dma(nc.gpsimd, cx.stat_bf[0:1, :], io["st_in"][0:1, :], cx.s_ld)
        join_all(cx)
        emit_attn1B(cx, io, out_f32)
    return P


def emit_build_scratchT(cx, w_fp32, scratch_plain, scratch_T, rows, cols):
    nc = cx.P.nc
    P = cx.P
    with P.scope():
        w = max(cols, ((rows + 127) // 128) * 128)
        stg = P.sbuf("stg", [128, w], BF16, side="right")
        nrt = (rows + 127) // 128
        for i in range(nrt):
            r0 = i * 128
            r1 = min(rows, r0 + 128)
            dma(nc.gpsimd, stg[0:r1 - r0, 0:cols], w_fp32[r0:r1, :], cx.s_ld)
            ldm = cx.s_ld.n
            nc.sync.wait_ge(cx.s_ld.h, ldm)
            dma(nc.sync, scratch_plain[r0:r1, :], stg[0:r1 - r0, 0:cols], cx.s_st)
            stm = cx.s_st.n
            nc.gpsimd.wait_ge(cx.s_st.h, stm)
        r128 = (rows // 128) * 128
        rtail = rows - r128
        nc.sync.wait_ge(cx.s_st.h, cx.s_st.n)
        for kc in range(cols // 128):
            if r128:
                dma_t(nc.sync, stg[:, 0:r128], scratch_plain[0:r128, kc * 128:(kc + 1) * 128], cx.s_ld)
            if rtail:
                nc.sync.dma_start(out=stg[:, r128:r128 + rtail],
                                  in_=scratch_plain.rearrange("r c -> c r")[kc * 128:(kc + 1) * 128, r128:rows],
                                  allow_slow_non_contiguous=True).then_inc(cx.s_ld.h, 16)
                cx.s_ld.n += 16
            ldm = cx.s_ld.n
            nc.sync.wait_ge(cx.s_ld.h, ldm)
            dma(nc.sync, scratch_T[kc * 128:(kc + 1) * 128, :], stg[:, 0:rows], cx.s_st)
            stm = cx.s_st.n
            nc.sync.wait_ge(cx.s_st.h, stm)


NMC = 33
SSD_CHUNKS = TOK // CHUNK
MB_A = [(DIN + 512 * i, 512) for i in range(8)] + [(DIN + 4096, 160)]


def emit_attn0A(cx, outs):
    nc = cx.P.nc
    P = cx.P
    io = cx.io
    with P.scope():
        convw = P.sbuf("convw", [128, NMC, 4], F32)
        convb = P.sbuf("convb", [128, NMC], F32)
        dtb = P.sbuf("dtb", [32, 1], F32)
        alog = P.sbuf("alog", [32, 1], F32)
        nega = P.sbuf("nega", [32, 1], F32)
        tri = P.sbuf("tri", [128, 2, 256], F32)
        maskd = P.sbuf("maskd", [128, 2, 256], F32)
        ones32 = P.sbuf("ones32", [1, 128], F32)
        dma(nc.sync, convw[:, :, :].rearrange("p a b -> p (a b)"), io["conv_w"][:, :], cx.s_ld)
        dma(nc.sync, convb[:, :], io["conv_b"][:, :], cx.s_ld)
        dma(nc.sync, dtb[:, :], io["dt_bias"][:, :], cx.s_ld)
        dma(nc.sync, alog[:, :], io["a_log"][:, :], cx.s_ld)
        dma(nc.sync, tri[:, :, :].rearrange("p a b -> p (a b)"), io["tri"][:, :], cx.s_ld)
        dma(nc.sync, maskd[:, :, :].rearrange("p a b -> p (a b)"), io["maskd"][:, :], cx.s_ld)
        ipba = P.sbuf("ipba", [128, NMC], F32)
        ipbh = P.sbuf("ipbh", [128, NMC], F32)
        ipbdt = P.sbuf("ipbdt", [32, 1], F32)
        dma(nc.sync, ipba[:, :], io["ipb_a"][:, :], cx.s_ld)
        dma(nc.sync, ipbh[:, :], io["ipb_halo"][:, :], cx.s_ld)
        dma(nc.sync, ipbdt[:, :], io["ipb_dt"][:, :], cx.s_ld)
        uprev = P.sbuf("uprevt", [3, D], BF16)
        dma(nc.gpsimd, uprev[:, :], io["uprev"][:, :], cx.s_ld)
        nc.vector.wait_ge(cx.s_ld.h, cx.s_ld.n)
        nc.vector.memset(ones32[0:1, :], 1.0)
        nc.scalar.wait_ge(cx.s_ld.h, cx.s_ld.n)
        A(cx, nc.scalar.activation(out=nega[:, :], in_=alog[:, :], func=AF.Exp))
        A(cx, nc.scalar.mul(out=nega[:, :], in_=nega[:, :], mul=-1.0))

        h3 = P.sbuf("h3", [3, 8], F32)
        xn3t = P.sbuf("xn3t", [3, D], BF16)
        u3sq = xn3t
        xn3c = P.sbuf("xn3c", [128, NCK, 4], BF16)
        nc.vector.memset(xn3c[:, :, :], 0.0)
        V(cx, nc.vector.tensor_reduce(out=h3[0:3, 0:1], in_=uprev[0:3, :], axis=mybir.AxisListType.X, op=ALU.add))
        V(cx, nc.vector.tensor_tensor(out=u3sq[0:3, :], in0=uprev[0:3, :], in1=uprev[0:3, :], op=ALU.mult))
        V(cx, nc.vector.tensor_reduce(out=h3[0:3, 1:2], in_=u3sq[0:3, :], axis=mybir.AxisListType.X, op=ALU.add))
        V(cx, nc.vector.tensor_scalar(out=h3[0:3, 0:2], in0=h3[0:3, 0:2], scalar1=1.0 / D, scalar2=None,
                                      op0=ALU.mult))
        V(cx, nc.vector.tensor_tensor(out=h3[0:3, 2:3], in0=h3[0:3, 0:1], in1=h3[0:3, 0:1], op=ALU.mult))
        ev = V(cx, nc.vector.tensor_tensor(out=h3[0:3, 1:2], in0=h3[0:3, 1:2], in1=h3[0:3, 2:3],
                                           op=ALU.subtract))
        nc.scalar.wait_ge(cx.s_ev.h, ev)
        A(cx, nc.scalar.activation(out=h3[0:3, 1:2], in_=h3[0:3, 1:2], func=AF.Ln,
                                   bias=cx.consts[0:3, 2:3], scale=1.0))
        ac = A(cx, nc.scalar.activation(out=h3[0:3, 1:2], in_=h3[0:3, 1:2], func=AF.Exp, bias=0.0, scale=-0.5))
        nc.vector.wait_ge(cx.s_ac.h, ac)
        V(cx, nc.vector.tensor_scalar(out=xn3t[0:3, :], in0=uprev[0:3, :], scalar1=h3[0:3, 0:1],
                                      scalar2=h3[0:3, 1:2], op0=ALU.subtract, op1=ALU.mult))
        nc.tensor.wait_ge(cx.s_ev.h, cx.s_ev.n)
        rot = cx.rott
        for g0 in range(0, NCK, 4):
            bank = rot.acquire(nc.tensor)
            rot.acquired()
            for k in range(4):
                ck = g0 + k
                nc.tensor.transpose(out=bank[:, k * 4:k * 4 + 3], in_=xn3t[0:3, ck * 128:(ck + 1) * 128],
                                    identity=cx.ident[0:3, 0:3])
            last = nc.tensor.transpose(out=bank[:, 384:512], in_=cx.ident[:, :], identity=cx.ident[:, :])
            tok = rot.commit(last)
            nc.vector.wait_ge(rot.sp.h, tok)
            lastv = None
            for k in range(4):
                ck = g0 + k
                lastv = nc.vector.tensor_copy(out=xn3c[:, ck, 0:3], in_=bank[:, k * 4:k * 4 + 3])
            rot.release(lastv)

        iw = [P.sbuf("iwA", [128, NCK, 256], BF16) for _ in range(2)]
        iw_free = [0, 0]
        xbraw = P.sbuf("xbraw", [128, NMC, 515], BF16)
        xbconv = P.sbuf("xbconv", [128, NMC, 256], BF16)
        bA = P.sbuf("bAs", [128, 512], BF16)
        dtt = P.sbuf("dtt", [32, 512], F32)
        att = P.sbuf("att", [32, 512], F32)
        dts = P.sbuf("dts", [32, 512], F32)
        dttm = P.sbuf("dttm", [128, 2, 32], F32)
        a_tm = P.sbuf("a_tm", [128, 2, 32], F32)
        cum_tm = P.sbuf("cum_tm", [128, 2, 32], F32)
        cum_cm = P.sbuf("cum_cm", [32, 256], F32)
        cumrow = P.sbuf("cumrow", [1, 8 * 256], F32)
        exprow_bf = P.sbuf("exprow_bf", [1, 8 * 256], BF16)
        tmp32 = P.sbuf("tmp32", [32, 128], F32)
        ds = P.sbuf("dss", [128, 2, 32], F32)
        lastraw = P.sbuf("lastraw", [1, 32], F32)
        gam = P.sbuf("gam", [1, 32], F32)
        gamb = P.sbuf("gamb", [1, 32], BF16)
        gamr = P.sbuf("gamr", [64, 32], F32)
        pref = P.sbuf("pref", [1, 256], F32)
        prefsc = P.sbuf("prefsc", [1, 32], F32)
        btm = P.sbuf("btm", [128, 2, 64], BF16)
        btm2 = P.sbuf("btm2", [128, 64], BF16)
        ctbase = P.sbuf("ctbase", [64, 256], BF16)
        xth = P.sbuf("xth", [128, 2, 128], BF16)
        dxs = P.sbuf("dxs", [128, 256], F32)
        mt = P.sbuf("mts", [128, 2, 256], BF16)
        ctc = P.sbuf("ctc", [64, 256], BF16)
        srun_bf = P.sbuf("srun_bf", [64, NH, 128], BF16)
        stmp = P.sbuf("stmp", [64, 128], F32)
        ychunk = P.sbuf("ychunk", [128, NH, 256], BF16)
        nc.vector.memset(srun_bf[:, :, :], 0.0)
        nc.vector.memset(xbraw[:, :, :], 0.0)
        V(cx, nc.vector.memset(prefsc[0:1, :], 1.0))
        Ls = dict(convw=convw, convb=convb, dtb=dtb, nega=nega, tri=tri, maskd=maskd, ones32=ones32,
                  xbraw=xbraw, xbconv=xbconv, bA=bA, dtt=dtt, att=att, dts=dts, dttm=dttm, a_tm=a_tm, cum_tm=cum_tm,
                  cum_cm=cum_cm, cumrow=cumrow, exprow_bf=exprow_bf, tmp32=tmp32,
                  ds=ds, lastraw=lastraw, gam=gam, gamb=gamb, gamr=gamr, pref=pref,
                  prefsc=prefsc, btm=btm, btm2=btm2, ctbase=ctbase, xth=xth, dxs=dxs, mt=mt, ctc=ctc,
                  srun_bf=srun_bf, stmp=stmp, ychunk=ychunk)
        chunk_idx = 0
        for sub in range(NSUB):
            t0 = sub * SUB
            emit_bcast_row(cx, cx.stat_bf[0:1, D + t0:D + t0 + 512], bA[:, :], 512)
            if sub > 0:
                V(cx, nc.vector.tensor_copy(out=xbraw[:, :, 0:3], in_=xbraw[:, :, 512:515]))
            for e0 in range(DIN, DIN + 4256, 256):
                ew = min(256, DIN + 4256 - e0)
                slot = (e0 // 256) % 2
                if iw_free[slot]:
                    nc.gpsimd.wait_ge(cx.s_mm.h, iw_free[slot])
                dma(nc.gpsimd, iw[slot][:, :, 0:ew],
                    io["ipT"].rearrange("(kc p) e -> p kc e", p=128)[:, :, e0:e0 + ew], cx.s_ld)
                ldm = cx.s_ld.n
                nc.tensor.wait_ge(cx.s_ld.h, ldm)
                lastmm = None
                for q in range((ew + 127) // 128):
                    mc = (e0 - DIN) // 128 + q
                    mw = min(128, ew - q * 128)
                    bank = cx.rot.acquire(nc.tensor)
                    cx.rot.acquired()
                    for kc in range(NCK):
                        lastmm = nc.tensor.matmul(out=bank[0:mw, :],
                                                  lhsT=iw[slot][:, kc, q * 128:q * 128 + mw],
                                                  rhs=cx.xc[:, kc, t0:t0 + 512],
                                                  start=(kc == 0), stop=(kc == NCK - 1))
                    tokm = cx.rot.commit(lastmm)
                    nc.vector.wait_ge(cx.rot.sp.h, tokm)
                    if mc < NMC:
                        vv = nc.vector.tensor_tensor(out=xbraw[:, mc, 3:515], in0=bank[:, :],
                                                     in1=bA[:, :], op=ALU.mult)
                        cx.rot.release(vv)
                        nc.vector.tensor_scalar(out=xbraw[:, mc, 3:515], in0=xbraw[:, mc, 3:515],
                                                scalar1=ipba[:, mc:mc + 1], scalar2=None, op0=ALU.add)
                    else:
                        vv = nc.vector.tensor_tensor(out=dtt[0:32, :], in0=bank[0:32, :],
                                                     in1=bA[0:32, :], op=ALU.mult)
                        cx.rot.release(vv)
                    if sub == 0 and mc < NMC:
                        bankh = cx.rotm.acquire(nc.tensor)
                        cx.rotm.acquired()
                        lh = None
                        for kc in range(NCK):
                            lh = nc.tensor.matmul(out=bankh[0:mw, 0:4],
                                                  lhsT=iw[slot][:, kc, q * 128:q * 128 + mw],
                                                  rhs=xn3c[:, kc, 0:4],
                                                  start=(kc == 0), stop=(kc == NCK - 1))
                        tokh = cx.rotm.commit(lh)
                        nc.vector.wait_ge(cx.rotm.sp.h, tokh)
                        vh = nc.vector.tensor_scalar(out=xbraw[:, mc, 0:3], in0=bankh[:, 0:3],
                                                     scalar1=ipbh[:, mc:mc + 1], scalar2=None, op0=ALU.add)
                        cx.rotm.release(vh)
                iw_free[slot] = cx.s_mm.n
            evm = cx.s_ev.n
            nc.vector.wait_ge(cx.s_ev.h, evm)
            V(cx, nc.vector.tensor_scalar(out=att[:, :], in0=dtt[:, :], scalar1=ipbdt[:, 0:1],
                                          scalar2=dtb[:, 0:1], op0=ALU.add, op1=ALU.add))
            ev = cx.s_ev.n
            nc.scalar.wait_ge(cx.s_ev.h, ev)
            A(cx, nc.scalar.activation(out=dtt[:, :], in_=att[:, :], func=AF.Abs))
            A(cx, nc.scalar.activation(out=dtt[:, :], in_=dtt[:, :], func=AF.Exp, bias=0.0, scale=-1.0))
            ac = A(cx, nc.scalar.activation(out=dtt[:, :], in_=dtt[:, :], func=AF.Ln,
                                            bias=cx.consts[0:32, 1:2], scale=1.0))
            nc.vector.wait_ge(cx.s_ac.h, ac)
            V(cx, nc.vector.tensor_scalar(out=att[:, :], in0=att[:, :], scalar1=0.0, scalar2=None,
                                          op0=ALU.max))
            V(cx, nc.vector.tensor_tensor(out=dts[:, :], in0=att[:, :], in1=dtt[:, :], op=ALU.add))
            V(cx, nc.vector.tensor_scalar(out=att[:, :], in0=dts[:, :], scalar1=nega[0:32, 0:1],
                                          scalar2=None, op0=ALU.mult))
            for cch in range(2):
                c0 = cch * 256
                for mc in range(NMC):
                    nc.vector.tensor_scalar(out=xbconv[:, mc, :], in0=xbraw[:, mc, c0:c0 + 256],
                                            scalar1=convw[:, mc, 0:1], scalar2=None, op0=ALU.mult)
                    for k in range(1, 4):
                        nc.vector.scalar_tensor_tensor(out=xbconv[:, mc, :],
                                                       in0=xbraw[:, mc, c0 + k:c0 + k + 256],
                                                       scalar=convw[:, mc, k:k + 1], in1=xbconv[:, mc, :],
                                                       op0=ALU.mult, op1=ALU.add)
                ev = cx.s_ev.inc(nc.vector.tensor_copy(out=dxs[:, 0:1], in_=xbconv[:, 0, 0:1]), 1)
                nc.scalar.wait_ge(cx.s_ev.h, ev)
                acv = None
                for mc in range(NMC):
                    acv = nc.scalar.activation(out=xbconv[:, mc, :], in_=xbconv[:, mc, :], func=AF.Silu,
                                               bias=convb[:, mc:mc + 1], scale=1.0)
                convdone = cx.s_ac.inc(acv, 1)
                nc.vector.wait_ge(cx.s_ac.h, convdone)
                nc.tensor.wait_ge(cx.s_ac.h, convdone)
                if getattr(cx, "dbg", None) is not None and chunk_idx == 0:
                    nc.sync.wait_ge(cx.s_ac.h, convdone)
                    dma(nc.sync, cx.dbg["dbg_xbc"][:, :], xbconv[:, :, :].rearrange("p a b -> p (a b)"), cx.s_st)
                    evx = cx.s_ev.n
                    nc.sync.wait_ge(cx.s_ev.h, evx)
                    dma(nc.sync, cx.dbg["dbg_att"][:, :], att[:, :], cx.s_st)
                    dma(nc.sync, cx.dbg["dbg_halo"].rearrange("p (a b) -> p a b", a=NMC), xbraw[:, :, 0:3], cx.s_st)
                    dma(nc.sync, cx.dbg["dbg_xn3"][:, :], xn3c[:, :, :].rearrange("p a b -> p (a b)"), cx.s_st)
                emit_ssd_chunk(cx, Ls, chunk_idx, cch)
                evm = cx.s_ev.n
                nc.sync.wait_ge(cx.s_ev.h, evm)
                dma(nc.sync, outs["y_local"][:, chunk_idx * NH * 256:(chunk_idx + 1) * NH * 256],
                    ychunk[:, :, :].rearrange("p a b -> p (a b)"), cx.s_st)
                stm = cx.s_st.n
                nc.vector.wait_ge(cx.s_st.h, stm)
                chunk_idx += 1
        evm = cx.s_ev.n
        nc.sync.wait_ge(cx.s_ev.h, evm)
        dma(nc.sync, outs["s_fin"][:, :], srun_bf[:, :, :].rearrange("p a b -> p (a b)"), cx.s_st)
        dma(nc.sync, outs["pref"][:, :], pref[:, :], cx.s_st)
        nc.sync.wait_ge(cx.s_st.h, cx.s_st.n)
        join_all(cx)


def emit_ssd_chunk(cx, L, chunk_idx, cch):
    nc = cx.P.nc
    c0 = cch * 256
    rotm = cx.rotm
    rot = cx.rot
    rott = cx.rott
    att = L["att"]
    xbconv = L["xbconv"]
    bank = rotm.acquire(nc.tensor)
    rotm.acquired()
    nc.tensor.transpose(out=bank[:, 0:32], in_=att[0:32, c0:c0 + 128], identity=cx.ident32[0:32, 0:32])
    nc.tensor.transpose(out=bank[:, 32:64], in_=att[0:32, c0 + 128:c0 + 256],
                        identity=cx.ident32[0:32, 0:32])
    nc.tensor.transpose(out=bank[:, 64:96], in_=L["dts"][0:32, c0:c0 + 128],
                        identity=cx.ident32[0:32, 0:32])
    tok = rotm.commit(nc.tensor.transpose(out=bank[:, 96:128], in_=L["dts"][0:32, c0 + 128:c0 + 256],
                                          identity=cx.ident32[0:32, 0:32]))
    nc.vector.wait_ge(rotm.sp.h, tok)
    nc.vector.tensor_copy(out=L["a_tm"][:, 0, :], in_=bank[:, 0:32])
    nc.vector.tensor_copy(out=L["a_tm"][:, 1, :], in_=bank[:, 32:64])
    V(cx, nc.vector.tensor_copy(out=L["dttm"][:, 0, :], in_=bank[:, 64:96]))
    V(cx, nc.vector.tensor_copy(out=L["dttm"][:, 1, :], in_=bank[:, 96:128]))
    lv = nc.vector.tensor_copy(out=L["a_tm"][:, 0, 0:1], in_=bank[:, 0:1])
    rotm.release(lv)
    atmd = cx.s_ev.n
    nc.tensor.wait_ge(cx.s_ev.h, atmd)
    bank = rotm.acquire(nc.tensor)
    rotm.acquired()
    lt = None
    for i in range(2):
        for j in range(2):
            lt = nc.tensor.matmul(out=bank[:, i * 32:(i + 1) * 32], lhsT=L["tri"][:, j, i * 128:(i + 1) * 128],
                                  rhs=L["a_tm"][:, j, :], start=(j == 0), stop=(j == 1), skip_group_check=True)
    tok = rotm.commit(lt)
    nc.vector.wait_ge(rotm.sp.h, tok)
    nc.vector.tensor_copy(out=L["cum_tm"][:, 0, :], in_=bank[:, 0:32])
    lv = nc.vector.tensor_copy(out=L["cum_tm"][:, 1, :], in_=bank[:, 32:64])
    rotm.release(lv)
    nc.vector.wait_ge(cx.s_ev.h, cx.s_ev.n)
    bank = rotm.acquire(nc.tensor)
    rotm.acquired()
    lt = None
    for j in range(2):
        lt = nc.tensor.matmul(out=bank[0:32, 0:256], lhsT=L["a_tm"][:, j, :], rhs=L["tri"][:, j, :],
                              start=(j == 0), stop=(j == 1))
    tok = rotm.commit(lt)
    nc.vector.wait_ge(rotm.sp.h, tok)
    lv = nc.vector.tensor_copy(out=L["cum_cm"][:, :], in_=bank[0:32, 0:256])
    rotm.release(lv)
    cumd = cx.s_ev.n
    nc.vector.wait_ge(cx.s_ev.h, cumd)
    nc.gpsimd.wait_ge(cx.s_ev.h, cumd)
    dma(nc.gpsimd, cx.cumdram.ap()[:, :], L["cum_cm"][:, :], cx.s_gp)
    gpm = cx.s_gp.n

    bank = rotm.acquire(nc.tensor)
    rotm.acquired()
    tok = rotm.commit(nc.tensor.transpose(out=bank[0:32, 0:128], in_=L["cum_tm"][:, 1, :],
                                          identity=cx.ident32[:, :]))
    nc.vector.wait_ge(rotm.sp.h, tok)
    lv = nc.vector.tensor_copy(out=L["tmp32"][:, :], in_=bank[0:32, 0:128])
    rotm.release(lv)
    ev = cx.s_ev.n
    nc.vector.wait_ge(cx.s_ev.h, ev)
    nc.tensor.wait_ge(cx.s_ev.h, ev)
    bank = rotm.acquire(nc.tensor)
    rotm.acquired()
    tok = rotm.commit(nc.tensor.transpose(out=bank[0:1, 0:32], in_=L["tmp32"][:, 127:128],
                                          identity=cx.ident32[0:32, 0:32]))
    nc.vector.wait_ge(rotm.sp.h, tok)
    lv = nc.vector.tensor_copy(out=L["lastraw"][0:1, :], in_=bank[0:1, 0:32])
    rotm.release(lv)
    ev = cx.s_ev.n
    nc.vector.wait_ge(cx.s_ev.h, ev)
    nc.scalar.wait_ge(cx.s_ev.h, ev)
    ac = A(cx, nc.scalar.activation(out=L["gam"][0:1, :], in_=L["lastraw"][0:1, :], func=AF.Exp))
    nc.vector.wait_ge(cx.s_ac.h, ac)
    V(cx, nc.vector.tensor_copy(out=L["gamb"][0:1, :], in_=L["gam"][0:1, :]))
    V(cx, nc.vector.tensor_copy(out=L["pref"][0:1, chunk_idx * 32:(chunk_idx + 1) * 32], in_=L["prefsc"][0:1, :]))
    V(cx, nc.vector.tensor_tensor(out=L["prefsc"][0:1, :], in0=L["prefsc"][0:1, :], in1=L["gam"][0:1, :],
                                  op=ALU.mult))
    evg = cx.s_ev.n
    nc.tensor.wait_ge(cx.s_ev.h, evg)
    bank = rotm.acquire(nc.tensor)
    rotm.acquired()
    tok = rotm.commit(nc.tensor.matmul(out=bank[0:64, 0:32], lhsT=cx.onesrow[0:1, 0:64],
                                       rhs=L["gamb"][0:1, :], start=True, stop=True))
    nc.vector.wait_ge(rotm.sp.h, tok)
    lv = nc.vector.tensor_copy(out=L["gamr"][:, :], in_=bank[0:64, 0:32])
    rotm.release(lv)
    nc.vector.wait_ge(cx.s_ev.h, cx.s_ev.n)
    bank = rotm.acquire(nc.tensor)
    rotm.acquired()
    tok = rotm.commit(nc.tensor.matmul(out=bank[:, 0:32], lhsT=L["ones32"][0:1, :],
                                       rhs=L["lastraw"][0:1, :], start=True, stop=True))
    nc.vector.wait_ge(rotm.sp.h, tok)
    nc.vector.tensor_tensor(out=L["ds"][:, 0, :], in0=bank[:, 0:32], in1=L["cum_tm"][:, 0, :],
                            op=ALU.subtract)
    lv = nc.vector.tensor_tensor(out=L["ds"][:, 1, :], in0=bank[:, 0:32], in1=L["cum_tm"][:, 1, :],
                                 op=ALU.subtract)
    rotm.release(lv)
    ev = cx.s_ev.n
    nc.vector.wait_ge(cx.s_ev.h, ev)
    nc.scalar.wait_ge(cx.s_ev.h, ev)
    A(cx, nc.scalar.activation(out=L["ds"][:, 0, :], in_=L["ds"][:, 0, :], func=AF.Exp))
    dsd = A(cx, nc.scalar.activation(out=L["ds"][:, 1, :], in_=L["ds"][:, 1, :], func=AF.Exp))
    nc.vector.wait_ge(cx.s_ac.h, dsd)
    nc.tensor.wait_ge(cx.s_ac.h, dsd)
    bank = rott.acquire(nc.tensor)
    rott.acquired()
    nc.tensor.transpose(out=bank[:, 0:64], in_=xbconv[0:64, 32, 0:128], identity=cx.ident[0:64, 0:64])
    tok = rott.commit(nc.tensor.transpose(out=bank[:, 64:128], in_=xbconv[0:64, 32, 128:256],
                                          identity=cx.ident[0:64, 0:64]))
    nc.vector.wait_ge(rott.sp.h, tok)
    nc.vector.tensor_copy(out=L["btm"][:, 0, :], in_=bank[:, 0:64])
    lv = nc.vector.tensor_copy(out=L["btm"][:, 1, :], in_=bank[:, 64:128])
    rott.release(lv)
    V(cx, nc.vector.tensor_copy(out=L["ctbase"][:, :], in_=xbconv[64:128, 32, :]))
    evct = cx.s_ev.n
    nc.sync.wait_ge(cx.s_ev.h, evct)
    dma(nc.sync, cx.ct_all_out[:, chunk_idx * 256:(chunk_idx + 1) * 256], L["ctbase"][:, :], cx.s_st)
    for hh in range(NH):
        if hh % 8 == 0:
            nc.gpsimd.wait_ge(cx.s_gp.h, cx.s_gp.n)
            nc.gpsimd.wait_ge(cx.s_ev.h, cx.s_ev.n)
            nc.gpsimd.wait_ge(cx.s_mm.h, cx.s_mm.n)
            nc.gpsimd.wait_ge(cx.s_st.h, cx.s_st.n)
            dma(nc.gpsimd, L["cumrow"][0:1, :],
                cx.cumdram.ap()[hh:hh + 8, :].rearrange("h l -> (h l)").unsqueeze(0), cx.s_gp)
            gpm = cx.s_gp.n
            nc.tensor.wait_ge(cx.s_gp.h, gpm)
            nc.scalar.wait_ge(cx.s_gp.h, gpm)
            exd = A(cx, nc.scalar.activation(out=L["exprow_bf"][0:1, :], in_=L["cumrow"][0:1, :],
                                             func=AF.Exp))
            if cx.cumrows_out is not None:
                nc.sync.wait_ge(cx.s_ac.h, exd)
                dma(nc.sync, cx.cumrows_out[chunk_idx:chunk_idx + 1, hh * 256:(hh + 8) * 256],
                    L["exprow_bf"][0:1, :], cx.s_st)
            nc.vector.wait_ge(cx.s_ac.h, exd)
        hro = (hh % 8) * 256
        bank = rott.acquire(nc.tensor)
        rott.acquired()
        nc.tensor.transpose(out=bank[:, 0:128], in_=xbconv[:, hh, 0:128], identity=cx.ident[:, :])
        tok = rott.commit(nc.tensor.transpose(out=bank[:, 128:256], in_=xbconv[:, hh, 128:256],
                                              identity=cx.ident[:, :]))
        if cx.prev_ytok:
            nc.vector.wait_ge(cx.s_mm.h, cx.prev_ytok)
        nc.vector.wait_ge(rott.sp.h, tok)
        nc.vector.tensor_copy(out=L["xth"][:, 0, :], in_=bank[:, 0:128])
        lv = nc.vector.tensor_copy(out=L["xth"][:, 1, :], in_=bank[:, 128:256])
        rott.release(lv)
        xtd = cx.s_ev.n
        nc.vector.wait_ge(cx.s_ev.h, xtd)
        nc.tensor.wait_ge(cx.s_ev.h, xtd)
        gbank = rotm.acquire(nc.tensor)
        rotm.acquired()
        lg = None
        for j in range(2):
            lg = nc.tensor.matmul(out=gbank[:, j * 256:(j + 1) * 256],
                                  lhsT=xbconv[0:64, 32, j * 128:(j + 1) * 128],
                                  rhs=L["ctbase"][:, :], start=True, stop=True, skip_group_check=True)
        gtok = rotm.commit(lg)
        dbank = rotm.acquire(nc.tensor)
        rotm.acquired()
        dtok = rotm.commit(nc.tensor.matmul(out=dbank[:, 0:256], lhsT=L["ones32"][0:1, :],
                                            rhs=L["cumrow"][0:1, hro:hro + 256], start=True, stop=True))
        nc.vector.wait_ge(rotm.sp.h, dtok)
        lv = None
        for j in range(2):
            nc.vector.tensor_scalar(out=L["dxs"][:, :], in0=dbank[:, 0:256],
                                    scalar1=L["cum_tm"][:, j, hh:hh + 1], scalar2=None, op0=ALU.subtract)
            nc.vector.tensor_tensor(out=L["dxs"][:, :], in0=L["dxs"][:, :], in1=L["maskd"][:, j, :],
                                    op=ALU.add)
            ev = cx.s_ev.inc(nc.vector.tensor_copy(out=L["tmp32"][0:1, 0:1], in_=L["dxs"][0:1, 0:1]), 1)
            nc.scalar.wait_ge(cx.s_ev.h, ev)
            ac = cx.s_ac.inc(nc.scalar.activation(out=L["dxs"][:, :], in_=L["dxs"][:, :], func=AF.Exp), 1)
            nc.vector.wait_ge(cx.s_ac.h, ac)
            lv = nc.vector.scalar_tensor_tensor(out=L["mt"][:, j, :],
                                                in0=gbank[:, j * 256:(j + 1) * 256],
                                                scalar=L["dttm"][:, j, hh:hh + 1],
                                                in1=L["dxs"][:, :], op0=ALU.mult, op1=ALU.mult)
            if j == 0:
                cx.s_ev.inc(lv, 1)
                nc.vector.wait_ge(cx.s_ev.h, cx.s_ev.n)
        rotm.release(lv)
        lvg = nc.vector.tensor_copy(out=L["dxs"][0:1, 0:1], in_=L["mt"][0:1, 0, 0:1])
        rotm.release(lvg)
        mtd = cx.s_ev.n
        cbank = rotm.acquire(nc.tensor)
        rotm.acquired()
        ctok = rotm.commit(nc.tensor.matmul(out=cbank[0:64, 0:256], lhsT=cx.onesrow[0:1, 0:64],
                                            rhs=L["exprow_bf"][0:1, hro:hro + 256], start=True, stop=True))
        nc.vector.wait_ge(rotm.sp.h, ctok)
        lv = nc.vector.tensor_tensor(out=L["ctc"][:, :], in0=cbank[0:64, 0:256],
                                     in1=L["ctbase"][:, :], op=ALU.mult)
        rotm.release(lv)
        ctcd = cx.s_ev.n
        nc.vector.wait_ge(cx.s_ev.h, ctcd)
        nc.tensor.wait_ge(cx.s_ev.h, ctcd)
        ybank = rot.acquire(nc.tensor)
        rot.acquired()
        for j in range(2):
            nc.tensor.matmul(out=ybank[:, 0:256], lhsT=L["xth"][:, j, :], rhs=L["mt"][:, j, :],
                             start=(j == 0), stop=False)
        ly = nc.tensor.matmul(out=ybank[:, 0:256], lhsT=L["srun_bf"][:, hh, :], rhs=L["ctc"][:, :],
                              start=False, stop=True)
        ytok = rot.commit(ly)
        cx.prev_ytok = ytok
        ev2 = V(cx, nc.vector.tensor_scalar(out=L["btm2"][:, :], in0=L["btm"][:, 0, :],
                                            scalar1=L["ds"][:, 0, hh:hh + 1],
                                            scalar2=L["dttm"][:, 0, hh:hh + 1], op0=ALU.mult, op1=ALU.mult))
        nc.tensor.wait_ge(cx.s_ev.h, ev2)
        sbank = rotm.acquire(nc.tensor)
        rotm.acquired()
        mm0 = nc.tensor.matmul(out=sbank[0:64, 0:128], lhsT=L["btm2"][:, :], rhs=L["xth"][:, 0, :],
                               start=True, stop=False)
        mm0tok = cx.s_mm.inc(mm0, 1)
        nc.vector.wait_ge(cx.s_mm.h, mm0tok)
        ev3 = V(cx, nc.vector.tensor_scalar(out=L["btm2"][:, :], in0=L["btm"][:, 1, :],
                                            scalar1=L["ds"][:, 1, hh:hh + 1],
                                            scalar2=L["dttm"][:, 1, hh:hh + 1], op0=ALU.mult, op1=ALU.mult))
        nc.tensor.wait_ge(cx.s_ev.h, ev3)
        stok = rotm.commit(nc.tensor.matmul(out=sbank[0:64, 0:128], lhsT=L["btm2"][:, :],
                                            rhs=L["xth"][:, 1, :], start=False, stop=True))
        nc.vector.wait_ge(rotm.sp.h, stok)
        nc.vector.tensor_scalar(out=L["stmp"][:, :], in0=L["srun_bf"][:, hh, :],
                                scalar1=L["gamr"][:, hh:hh + 1], scalar2=None, op0=ALU.mult)
        lv = nc.vector.tensor_tensor(out=L["srun_bf"][:, hh, :], in0=L["stmp"][:, :],
                                     in1=sbank[0:64, 0:128], op=ALU.add)
        rotm.release(lv)
        nc.vector.wait_ge(rot.sp.h, ytok)
        lv = nc.vector.tensor_copy(out=L["ychunk"][:, hh, :], in_=ybank[:, 0:256])
        rot.release(lv)


def build_p_attn0A(debug=False):
    P = Prog()
    nc = P.nc
    io = {
        "xc_in": P.dram("xc_in", [128, NCK * TOK], BF16, kind="ExternalInput").ap(),
        "st_in": P.dram("st_in", [1, 2 * D], F32, kind="ExternalInput").ap(),
        "in_proj_w": P.dram("in_proj_w", [DIP, D], F32, kind="ExternalInput").ap(),
        "conv_w": P.dram("conv_w", [128, NMC * 4], F32, kind="ExternalInput").ap(),
        "conv_b": P.dram("conv_b", [128, NMC], F32, kind="ExternalInput").ap(),
        "dt_bias": P.dram("dt_bias", [32, 1], F32, kind="ExternalInput").ap(),
        "a_log": P.dram("a_log", [32, 1], F32, kind="ExternalInput").ap(),
        "tri": P.dram("tri", [128, 2 * 256], F32, kind="ExternalInput").ap(),
        "maskd": P.dram("maskd", [128, 2 * 256], F32, kind="ExternalInput").ap(),
        "uprev": P.dram("uprev", [3, D], F32, kind="ExternalInput").ap(),
        "ipb_a": P.dram("ipb_a", [128, NMC], F32, kind="ExternalInput").ap(),
        "ipb_halo": P.dram("ipb_halo", [128, NMC], F32, kind="ExternalInput").ap(),
        "ipb_dt": P.dram("ipb_dt", [32, 1], F32, kind="ExternalInput").ap(),
        "consts": P.dram("consts", [128, 8], F32, kind="ExternalInput").ap(),
        "onesrow": P.dram("onesrow", [1, 128], BF16, kind="ExternalInput").ap(),
        "onescol": P.dram("onescol", [128, 1], BF16, kind="ExternalInput").ap(),
        "ident": P.dram("ident", [128, 128], BF16, kind="ExternalInput").ap(),
        "ident32": P.dram("ident32", [128, 128], F32, kind="ExternalInput").ap(),
        "tieb": P.dram("tieb", [128, 16], F32, kind="ExternalInput").ap(),
    }
    outs = {
        "y_local": P.dram("y_local", [128, NH * TOK], BF16, kind="ExternalOutput").ap(),
        "s_fin": P.dram("s_fin", [64, NH * 128], BF16, kind="ExternalOutput").ap(),
        "ct_all": P.dram("ct_all", [64, SSD_CHUNKS * 256], BF16, kind="ExternalOutput").ap(),
        "pref": P.dram("pref", [1, 256], F32, kind="ExternalOutput").ap(),
        "cumrows": P.dram("cumrows", [SSD_CHUNKS, NH * 256], BF16, kind="ExternalOutput").ap(),
    }
    if debug:
        outs["dbg_xbc"] = P.dram("dbg_xbc", [128, NMC * 256], BF16, kind="ExternalOutput").ap()
        outs["dbg_att"] = P.dram("dbg_att", [32, 512], F32, kind="ExternalOutput").ap()
        outs["dbg_halo"] = P.dram("dbg_halo", [128, NMC * 3], BF16, kind="ExternalOutput").ap()
        outs["dbg_xn3"] = P.dram("dbg_xn3", [128, NCK * 4], BF16, kind="ExternalOutput").ap()
    ip_plain = P.dram("ip_plain", [DIP, D], BF16)
    ipT = P.dram("ipT", [D, DIP], BF16)
    io["ipT"] = ipT.ap()
    cumdram = P.dram("cumdram", [NH, 256], F32)
    with P.stack:
        cx = alloc_common(P, io)
        cx.cumdram = cumdram
        cx.cumrows_out = outs["cumrows"]
        cx.ct_all_out = outs["ct_all"]
        cx.dbg = outs if debug else None
        dma(nc.sync, cx.xc[:, :, :].rearrange("p c t -> p (c t)"), io["xc_in"][:, :], cx.s_ld)
        dma(nc.gpsimd, cx.stat_bf[0:1, :], io["st_in"][0:1, :], cx.s_ld)
        join_all(cx)
        emit_build_scratchT(cx, io["in_proj_w"], ip_plain.ap(), ipT.ap(), DIP, D)
        join_all(cx)
        emit_attn0A(cx, outs)
    return P


def emit_bcast_row32(cx, ones32, src_row, dst, n):
    nc = cx.P.nc
    assert n <= 512
    p = dst.shape[0]
    bank = cx.rotm.acquire(nc.tensor)
    cx.rotm.acquired()
    tok = cx.rotm.commit(nc.tensor.matmul(out=bank[0:p, 0:n], lhsT=ones32[0:1, 0:p], rhs=src_row,
                                          start=True, stop=True))
    nc.vector.wait_ge(cx.rotm.sp.h, tok)
    v = nc.vector.tensor_copy(out=dst, in_=bank[0:p, 0:n])
    cx.rotm.release(v)


def emit_attn0B(cx, io):
    nc = cx.P.nc
    P = cx.P
    with P.scope():
        lnw = P.sbuf("lnw", [128, 32], F32)
        lnb = P.sbuf("lnb", [128, 32], F32)
        ipbz = P.sbuf("ipbz", [128, 32], F32)
        opb = P.sbuf("opb", [128, 16], F32)
        ones32 = P.sbuf("ones32b", [1, 128], F32)
        sin = P.sbuf("sin", [64, NH, 128], BF16)
        sint = P.sbuf("sint", [64, 128], BF16)
        ctk = P.sbuf("ctk", [64, SSD_CHUNKS, 256], BF16)
        prefrep = P.sbuf("prefrep", [64, 256], F32)
        pr = P.sbuf("prrow", [1, 256], F32)
        expr8 = P.sbuf("expr8b", [1, NH * 256], BF16)
        ctc = P.sbuf("ctcb", [64, 256], BF16)
        ysub = P.sbuf("ysub", [128, NH, 512], BF16)
        gc = P.sbuf("gcs", [128, NCK, 512], BF16)
        sg = P.sbuf("sgs", [128, 512], F32)
        t1 = P.sbuf("t1s", [128, 512], F32)
        bA2 = P.sbuf("bA2", [128, 512], F32)
        bB2 = P.sbuf("bB2", [128, 512], F32)
        bMU = P.sbuf("bMU", [128, 512], BF16)
        bAr = P.sbuf("bAr", [128, 512], BF16)
        sq2 = P.sbuf("sq2", [128, 1024], BF16)
        strow = P.sbuf("strow", [1, 3 * 512], F32)
        opw = [P.sbuf("opw", [128, 32, 128], BF16) for _ in range(2)]
        opw_free = [0, 0]
        iwz = [P.sbuf("iwz", [128, NCK, 256], BF16) for _ in range(2)]
        iwz_free = [0, 0]
        dma(nc.sync, lnw[:, :], io["ln_w"][:, :], cx.s_ld)
        dma(nc.sync, lnb[:, :], io["ln_b"][:, :], cx.s_ld)
        dma(nc.sync, ipbz[:, :], io["ipb_z"][:, :], cx.s_ld)
        dma(nc.sync, opb[:, :], io["opb"][:, :], cx.s_ld)
        dma(nc.gpsimd, sin[:, :, :].rearrange("p a b -> p (a b)"), io["s_in"][:, :], cx.s_ld)
        dma(nc.sync, ctk[:, :, :].rearrange("p a b -> p (a b)"), io["ct_in"][:, :], cx.s_ld)
        dma(nc.sync, pr[0:1, :], io["pref_in"][0:1, :], cx.s_ld)
        nc.vector.wait_ge(cx.s_ld.h, cx.s_ld.n)
        ev = V(cx, nc.vector.memset(ones32[0:1, :], 1.0))
        nc.tensor.wait_ge(cx.s_ev.h, ev)
        nc.tensor.wait_ge(cx.s_ld.h, cx.s_ld.n)
        emit_bcast_row32(cx, ones32, pr[0:1, :], prefrep[:, :], 256)
        for sub in range(NSUB):
            t0 = sub * SUB
            emit_bcast_row(cx, cx.stat_bf[0:1, D + t0:D + t0 + 512], bAr[:, :], 512)
            emit_bcast_row(cx, cx.stat_bf[0:1, t0:t0 + 512], bMU[:, :], 512)
            gl_pre = cx.s_ev.n
            nc.sync.wait_ge(cx.s_ev.h, gl_pre)
            for ck in range(NCK):
                dma_t(nc.sync, gc[:, ck, :], io["g_in"][t0:t0 + 512, ck * 128:(ck + 1) * 128], cx.s_ld)
            nc.sync.wait_ge(cx.s_mm.h, cx.s_mm.n)
            for q in range(2):
                ch = sub * 2 + q
                dma(nc.sync, ysub[:, :, q * 256:(q + 1) * 256],
                    io["y_in"].rearrange("p (c h l) -> p c h l", c=SSD_CHUNKS, h=NH)[:, ch, :, :], cx.s_ld)
            yldm = cx.s_ld.n
            nc.vector.wait_ge(cx.s_ld.h, yldm)
            nc.tensor.wait_ge(cx.s_ld.h, yldm)
            for q in range(2):
                ch = sub * 2 + q
                ldm0 = cx.s_ld.n
                nc.gpsimd.wait_ge(cx.s_mm.h, cx.s_mm.n)
                dma(nc.gpsimd, expr8[0:1, :], io["cum_in"][ch:ch + 1, :], cx.s_ld)
                em = cx.s_ld.n
                nc.tensor.wait_ge(cx.s_ld.h, em)
                for hh in range(NH):
                    hro = hh * 256
                    if cx.prev_ytok:
                        nc.vector.wait_ge(cx.s_mm.h, cx.prev_ytok)
                    ev = V(cx, nc.vector.tensor_scalar(out=sint[:, :], in0=sin[:, hh, :],
                                                       scalar1=prefrep[:, ch * 32 + hh:ch * 32 + hh + 1],
                                                       scalar2=None, op0=ALU.mult))
                    nc.tensor.wait_ge(cx.s_ev.h, ev)
                    cbank = cx.rotm.acquire(nc.tensor)
                    cx.rotm.acquired()
                    ctok = cx.rotm.commit(nc.tensor.matmul(out=cbank[0:64, 0:256],
                                                           lhsT=cx.onesrow[0:1, 0:64],
                                                           rhs=expr8[0:1, hro:hro + 256], start=True, stop=True))
                    nc.vector.wait_ge(cx.rotm.sp.h, ctok)
                    lv = nc.vector.tensor_tensor(out=ctc[:, :], in0=cbank[0:64, 0:256],
                                                 in1=ctk[:, ch, :], op=ALU.mult)
                    cx.rotm.release(lv)
                    ctcd = cx.s_ev.n
                    nc.vector.wait_ge(cx.s_ev.h, ctcd)
                    nc.tensor.wait_ge(cx.s_ev.h, ctcd)
                    ybank = cx.rot.acquire(nc.tensor)
                    cx.rot.acquired()
                    ytok = cx.rot.commit(nc.tensor.matmul(out=ybank[:, 0:256], lhsT=sint[:, :],
                                                          rhs=ctc[:, :], start=True, stop=True))
                    cx.prev_ytok = ytok
                    nc.vector.wait_ge(cx.rot.sp.h, ytok)
                    lv = nc.vector.tensor_tensor(out=ysub[:, hh, q * 256:(q + 1) * 256],
                                                 in0=ysub[:, hh, q * 256:(q + 1) * 256],
                                                 in1=ybank[:, 0:256], op=ALU.add)
                    cx.rot.release(lv)
            if cx.dbg is not None and sub == 0:
                nc.sync.wait_ge(cx.s_ev.h, cx.s_ev.n)
                dma(nc.sync, cx.dbg["dbg_ycorr"][:, :], ysub[:, :, :].rearrange("p a b -> p (a b)"), cx.s_st)
                nc.vector.wait_ge(cx.s_st.h, cx.s_st.n)
            for e0 in range(0, DIN, 256):
                slot = (e0 // 256) % 2
                if iwz_free[slot]:
                    nc.gpsimd.wait_ge(cx.s_mm.h, iwz_free[slot])
                dma(nc.gpsimd, iwz[slot][:, :, :],
                    io["ipT"].rearrange("(kc p) e -> p kc e", p=128)[:, :, e0:e0 + 256], cx.s_ld)
                ldm = cx.s_ld.n
                nc.tensor.wait_ge(cx.s_ld.h, ldm)
                for q in range(2):
                    mc = e0 // 128 + q
                    bank = cx.rot.acquire(nc.tensor)
                    cx.rot.acquired()
                    lastmm = None
                    for kc in range(NCK):
                        lastmm = nc.tensor.matmul(out=bank[:, :],
                                                  lhsT=iwz[slot][:, kc, q * 128:(q + 1) * 128],
                                                  rhs=cx.xc[:, kc, t0:t0 + 512],
                                                  start=(kc == 0), stop=(kc == NCK - 1))
                    tokm = cx.rot.commit(lastmm)
                    nc.vector.wait_ge(cx.rot.sp.h, tokm)
                    vv = nc.vector.tensor_tensor(out=sg[:, :], in0=bank[:, :], in1=bAr[:, :], op=ALU.mult)
                    cx.rot.release(vv)
                    V(cx, nc.vector.tensor_scalar(out=sg[:, :], in0=sg[:, :],
                                                  scalar1=ipbz[:, mc:mc + 1], scalar2=None, op0=ALU.add))
                    ev = cx.s_ev.n
                    nc.scalar.wait_ge(cx.s_ev.h, ev)
                    acv = A(cx, nc.scalar.activation(out=sg[:, :], in_=sg[:, :], func=AF.Silu))
                    nc.vector.wait_ge(cx.s_ac.h, acv)
                    V(cx, nc.vector.tensor_tensor(out=ysub[:, mc, :], in0=ysub[:, mc, :],
                                                  in1=sg[:, :], op=ALU.mult))
                iwz_free[slot] = cx.s_mm.n
            if cx.dbg is not None and sub == 0:
                nc.sync.wait_ge(cx.s_ev.h, cx.s_ev.n)
                dma(nc.sync, cx.dbg["dbg_ygate"][:, :], ysub[:, :, :].rearrange("p a b -> p (a b)"), cx.s_st)
                nc.vector.wait_ge(cx.s_st.h, cx.s_st.n)
                nc.scalar.wait_ge(cx.s_st.h, cx.s_st.n)
            banks_ = cx.rotm.acquire(nc.tensor)
            cx.rotm.acquired()
            bankq_ = cx.rotm.acquire(nc.tensor)
            cx.rotm.acquired()
            mmq = None
            prev_q = [0, 0]
            for mc in range(NH):
                sl = sq2[:, (mc % 2) * 512:(mc % 2) * 512 + 512]
                if prev_q[mc % 2]:
                    nc.scalar.wait_ge(cx.s_mm.h, prev_q[mc % 2])
                a = nc.scalar.activation(out=sl, in_=ysub[:, mc, :], func=AF.Square)
                am = cx.s_ac.inc(a, 1)
                nc.tensor.matmul(out=banks_[0:1, :], lhsT=cx.onescol[:, :], rhs=ysub[:, mc, :],
                                 start=(mc == 0), stop=(mc == NH - 1), skip_group_check=True)
                nc.tensor.wait_ge(cx.s_ac.h, am)
                mmq = nc.tensor.matmul(out=bankq_[0:1, :], lhsT=cx.onescol[:, :], rhs=sl,
                                       start=(mc == 0), stop=(mc == NH - 1), skip_group_check=True)
                if mc < NH - 1:
                    prev_q[mc % 2] = cx.s_mm.inc(mmq, 1)
            tok = cx.rotm.commit(mmq)
            nc.vector.wait_ge(cx.rotm.sp.h, tok)
            v1 = nc.vector.tensor_scalar(out=strow[0:1, 0:512], in0=banks_[0:1, :],
                                         scalar1=1.0 / DIN, scalar2=None, op0=ALU.mult)
            cx.rotm.release(v1)
            v2 = nc.vector.tensor_scalar(out=strow[0:1, 512:1024], in0=bankq_[0:1, :],
                                         scalar1=1.0 / DIN, scalar2=None, op0=ALU.mult)
            cx.rotm.release(v2)
            nc.vector.wait_ge(cx.s_ev.h, cx.s_ev.n)
            nc.vector.tensor_tensor(out=strow[0:1, 1024:1536], in0=strow[0:1, 0:512],
                                    in1=strow[0:1, 0:512], op=ALU.mult)
            v = nc.vector.tensor_tensor(out=strow[0:1, 512:1024], in0=strow[0:1, 512:1024],
                                        in1=strow[0:1, 1024:1536], op=ALU.subtract)
            ev = cx.s_ev.inc(v, 1)
            nc.scalar.wait_ge(cx.s_ev.h, ev)
            nc.scalar.activation(out=strow[0:1, 512:1024], in_=strow[0:1, 512:1024],
                                 func=AF.Ln, bias=cx.consts[0:1, 2:3], scale=1.0)
            ac = A(cx, nc.scalar.activation(out=strow[0:1, 512:1024], in_=strow[0:1, 512:1024],
                                            func=AF.Exp, bias=0.0, scale=-0.5))
            nc.vector.wait_ge(cx.s_ac.h, ac)
            V(cx, nc.vector.tensor_tensor(out=strow[0:1, 1024:1536], in0=strow[0:1, 0:512],
                                          in1=strow[0:1, 512:1024], op=ALU.mult))
            ev = V(cx, nc.vector.tensor_scalar(out=strow[0:1, 1024:1536], in0=strow[0:1, 1024:1536],
                                               scalar1=-1.0, scalar2=None, op0=ALU.mult))
            nc.tensor.wait_ge(cx.s_ev.h, ev)
            emit_bcast_row32(cx, ones32, strow[0:1, 512:1024], bA2[:, :], 512)
            emit_bcast_row32(cx, ones32, strow[0:1, 1024:1536], bB2[:, :], 512)
            evb = cx.s_ev.n
            nc.vector.wait_ge(cx.s_ev.h, evb)
            if cx.dbg is not None and sub == 0:
                nc.sync.wait_ge(cx.s_ev.h, cx.s_ev.n)
                nc.sync.wait_ge(cx.s_ac.h, cx.s_ac.n)
                dma(nc.sync, cx.dbg["dbg_strow"][0:1, :], strow[0:1, :], cx.s_st)
                nc.vector.wait_ge(cx.s_st.h, cx.s_st.n)
                nc.scalar.wait_ge(cx.s_st.h, cx.s_st.n)
            for mc in range(NH):
                nc.vector.tensor_tensor(out=t1[:, :], in0=ysub[:, mc, :], in1=bA2[:, :], op=ALU.mult)
                nc.vector.tensor_tensor(out=t1[:, :], in0=t1[:, :], in1=bB2[:, :], op=ALU.add)
                nc.vector.tensor_scalar(out=ysub[:, mc, :], in0=t1[:, :], scalar1=lnw[:, mc:mc + 1],
                                        scalar2=lnb[:, mc:mc + 1], op0=ALU.mult, op1=ALU.add)
            lnd = cx.s_ev.inc(nc.vector.tensor_copy(out=t1[:, 0:1], in_=ysub[:, 0, 0:1]), 1)
            nc.tensor.wait_ge(cx.s_ev.h, lnd)
            nc.scalar.wait_ge(cx.s_ev.h, lnd)
            if cx.dbg is not None and sub == 0:
                nc.sync.wait_ge(cx.s_ev.h, cx.s_ev.n)
                dma(nc.sync, cx.dbg["dbg_yln"][:, :], ysub[:, :, :].rearrange("p a b -> p (a b)"), cx.s_st)
                nc.vector.wait_ge(cx.s_st.h, cx.s_st.n)
            nc.scalar.wait_ge(cx.s_ld.h, yldm)
            prev_vv = 0
            for mb in range(16):
                slot = mb % 2
                if opw_free[slot]:
                    nc.gpsimd.wait_ge(cx.s_mm.h, opw_free[slot])
                dma(nc.gpsimd, opw[slot][:, :, :],
                    io["opT"].rearrange("(kc p) e -> p kc e", p=128)[:, :, mb * 128:(mb + 1) * 128],
                    cx.s_ld)
                ldm = cx.s_ld.n
                nc.tensor.wait_ge(cx.s_ld.h, ldm)
                for q in range(1):
                    ck = mb
                    bank = cx.rot.acquire(nc.tensor)
                    cx.rot.acquired()
                    lastmm = None
                    for kc in range(32):
                        lastmm = nc.tensor.matmul(out=bank[:, :],
                                                  lhsT=opw[slot][:, kc, :],
                                                  rhs=ysub[:, kc, :],
                                                  start=(kc == 0), stop=(kc == 31))
                    tokm = cx.rot.commit(lastmm)
                    if prev_vv:
                        nc.scalar.wait_ge(cx.s_ev.h, prev_vv)
                    acv = A(cx, nc.scalar.activation(out=sg[:, :], in_=gc[:, ck, :], func=AF.Silu))
                    nc.vector.wait_ge(cx.s_ac.h, acv)
                    nc.vector.wait_ge(cx.rot.sp.h, tokm)
                    if cx.dbg is not None and sub == 0:
                        V(cx, nc.vector.tensor_copy(out=bA2[:, :], in_=bank[:, :]))
                        nc.sync.wait_ge(cx.s_ev.h, cx.s_ev.n)
                        dma(nc.sync, cx.dbg["dbg_sraw"][:, mb * 512:(mb + 1) * 512], bA2[:, :], cx.s_st)
                    vv = nc.vector.scalar_tensor_tensor(out=t1[:, :], in0=bank[:, :],
                                                        scalar=opb[:, mb:mb + 1], in1=sg[:, :],
                                                        op0=ALU.add, op1=ALU.mult)
                    cx.rot.release(vv)
                    prev_vv = cx.s_ev.n
                    nc.vector.tensor_tensor(out=t1[:, :], in0=t1[:, :], in1=bMU[:, :], op=ALU.add)
                    if cx.dbg is not None and sub == 0:
                        V(cx, nc.vector.tensor_copy(out=bB2[:, :], in_=t1[:, :]))
                        nc.sync.wait_ge(cx.s_ev.h, cx.s_ev.n)
                        dma(nc.sync, cx.dbg["dbg_t1"][:, mb * 512:(mb + 1) * 512], bB2[:, :], cx.s_st)
                        nc.vector.wait_ge(cx.s_st.h, cx.s_st.n)
                    V(cx, nc.vector.tensor_tensor(out=cx.xc[:, ck, t0:t0 + 512],
                                                  in0=cx.xc[:, ck, t0:t0 + 512], in1=t1[:, :], op=ALU.add))
                opw_free[slot] = cx.s_mm.n
        join_all(cx)


def build_p_attn0B(debug=False):
    P = Prog()
    nc = P.nc
    io = {
        "xc_in": P.dram("xc_in", [128, NCK * TOK], BF16, kind="ExternalInput").ap(),
        "st_in": P.dram("st_in", [1, 2 * D], F32, kind="ExternalInput").ap(),
        "in_proj_w": P.dram("in_proj_w", [DIP, D], F32, kind="ExternalInput").ap(),
        "out_proj_w": P.dram("out_proj_w", [D, DIN], F32, kind="ExternalInput").ap(),
        "ln_w": P.dram("ln_w", [128, 32], F32, kind="ExternalInput").ap(),
        "ln_b": P.dram("ln_b", [128, 32], F32, kind="ExternalInput").ap(),
        "g_in": P.dram("g_in", [TOK, D], BF16, kind="ExternalInput").ap(),
        "y_in": P.dram("y_in", [128, NH * TOK], BF16, kind="ExternalInput").ap(),
        "s_in": P.dram("s_in", [64, NH * 128], BF16, kind="ExternalInput").ap(),
        "ipb_z": P.dram("ipb_z", [128, 32], F32, kind="ExternalInput").ap(),
        "opb": P.dram("opb", [128, 16], F32, kind="ExternalInput").ap(),
        "ct_in": P.dram("ct_in", [64, SSD_CHUNKS * 256], BF16, kind="ExternalInput").ap(),
        "pref_in": P.dram("pref_in", [1, 256], F32, kind="ExternalInput").ap(),
        "cum_in": P.dram("cum_in", [SSD_CHUNKS, NH * 256], BF16, kind="ExternalInput").ap(),
        "consts": P.dram("consts", [128, 8], F32, kind="ExternalInput").ap(),
        "onesrow": P.dram("onesrow", [1, 128], BF16, kind="ExternalInput").ap(),
        "onescol": P.dram("onescol", [128, 1], BF16, kind="ExternalInput").ap(),
        "ident": P.dram("ident", [128, 128], BF16, kind="ExternalInput").ap(),
        "ident32": P.dram("ident32", [128, 128], F32, kind="ExternalInput").ap(),
        "tieb": P.dram("tieb", [128, 16], F32, kind="ExternalInput").ap(),
    }
    x1_out = P.dram("x1_out", [128, NCK * TOK], BF16, kind="ExternalOutput").ap()
    st1_out = P.dram("st1_out", [1, 2 * D], F32, kind="ExternalOutput").ap()
    ip_plain = P.dram("ip_plain2", [DIP, D], BF16)
    ipT = P.dram("ipT2", [D, DIP], BF16)
    op_plain = P.dram("op_plain", [D, DIN], BF16)
    opT = P.dram("opT", [DIN, D], BF16)
    io["ipT"] = ipT.ap()
    io["opT"] = opT.ap()
    with P.stack:
        cx = alloc_common(P, io)
        if debug:
            cx.dbg = {
                "dbg_ycorr": P.dram("dbg_ycorr", [128, NH * 512], BF16, kind="ExternalOutput").ap(),
                "dbg_ygate": P.dram("dbg_ygate", [128, NH * 512], BF16, kind="ExternalOutput").ap(),
                "dbg_strow": P.dram("dbg_strow", [1, 3 * 512], F32, kind="ExternalOutput").ap(),
                "dbg_yln": P.dram("dbg_yln", [128, NH * 512], BF16, kind="ExternalOutput").ap(),
                "dbg_sraw": P.dram("dbg_sraw", [128, 16 * 512], F32, kind="ExternalOutput").ap(),
                "dbg_t1": P.dram("dbg_t1", [128, 16 * 512], F32, kind="ExternalOutput").ap(),
            }
        dma(nc.sync, cx.xc[:, :, :].rearrange("p c t -> p (c t)"), io["xc_in"][:, :], cx.s_ld)
        dma(nc.gpsimd, cx.stat_bf[0:1, :], io["st_in"][0:1, :], cx.s_ld)
        join_all(cx)
        emit_build_scratchT(cx, io["in_proj_w"], ip_plain.ap(), ipT.ap(), DIP, D)
        join_all(cx)
        emit_build_scratchT(cx, io["out_proj_w"], op_plain.ap(), opT.ap(), D, DIN)
        join_all(cx)
        emit_attn0B(cx, io)
        with P.scope():
            sq = P.sbuf("sqb", [128, NCK, 512], BF16)
            emit_stats_center(cx, cx.xc, sq, st_out=st1_out)
        join_all(cx)
        nc.sync.wait_ge(cx.s_ev.h, cx.s_ev.n)
        dma(nc.sync, x1_out[:, :], cx.xc[:, :, :].rearrange("p c t -> p (c t)"), cx.s_st)
        nc.sync.wait_ge(cx.s_st.h, cx.s_st.n)
    return P


def host_consts():
    consts = np.zeros((128, 8), np.float32)
    consts[:, 1] = 1.0
    consts[:, 2] = LN_EPS
    onesrow = np.ones((1, 128), BF)
    onescol = np.ones((128, 1), BF)
    ident = np.eye(128, dtype=BF)
    tieb = np.broadcast_to(np.arange(16, dtype=np.float32) * -1e-6, (128, 16)).copy()
    return {"consts": consts, "onesrow": onesrow, "onescol": onescol, "ident": ident,
            "ident32": np.eye(128, dtype=np.float32), "tieb": tieb}


def host_ssd_consts(inp):
    tri_full = np.tril(np.ones((256, 256), np.float32), 0).T
    tri = np.zeros((128, 2 * 256), np.float32)
    tri[:, 0:256] = tri_full[0:128, :]
    tri[:, 256:512] = tri_full[128:256, :]
    maskd = np.zeros((128, 2 * 256), np.float32)
    for j in range(2):
        for p in range(128):
            s = j * 128 + p
            maskd[p, j * 256:(j + 1) * 256] = np.where(np.arange(256) >= s, 0.0, NEG)
    convw = np.zeros((128, NMC * 4), np.float32)
    cw = inp["L0_attn_conv_w"].reshape(CONVD, 4)
    for mc in range(NMC):
        convw[:, mc * 4:(mc + 1) * 4] = cw[mc * 128:(mc + 1) * 128, :]
    convb = inp["L0_attn_conv_b"].reshape(NMC, 128).T.copy()
    dtb = inp["L0_attn_dt_bias"].reshape(32, 1)
    alog = inp["L0_attn_A_log"].reshape(32, 1)
    return {"tri": tri, "maskd": maskd, "conv_w": convw, "conv_b": convb,
            "dt_bias": dtb, "a_log": alog}


def host_uprev(u, c):
    b, h = c // 2, c % 2
    if h == 0:
        return np.zeros((3, D), np.float32)
    return np.ascontiguousarray(u[b, h * TOK - 3:h * TOK, :])


def core_slices(full):
    out = []
    for c in range(8):
        b, h = c // 2, c % 2
        out.append(np.ascontiguousarray(full[b, h * TOK:(h + 1) * TOK, :]))
    return out


def build_p_load():
    P = Prog()
    nc = P.nc
    io = {
        "u": P.dram("u", [TOK, D], F32, kind="ExternalInput").ap(),
        "consts": P.dram("consts", [128, 8], F32, kind="ExternalInput").ap(),
        "onesrow": P.dram("onesrow", [1, 128], BF16, kind="ExternalInput").ap(),
        "onescol": P.dram("onescol", [128, 1], BF16, kind="ExternalInput").ap(),
        "ident": P.dram("ident", [128, 128], BF16, kind="ExternalInput").ap(),
        "ident32": P.dram("ident32", [128, 128], F32, kind="ExternalInput").ap(),
        "tieb": P.dram("tieb", [128, 16], F32, kind="ExternalInput").ap(),
    }
    xc_out = P.dram("xc_out", [128, NCK * TOK], BF16, kind="ExternalOutput").ap()
    st_out = P.dram("st_out", [1, 2 * D], F32, kind="ExternalOutput").ap()
    with P.stack:
        cx = alloc_common(P, io)
        with P.scope():
            sq = P.sbuf("sq", [128, NCK, 512], BF16)
            emit_load_x(cx)
            emit_stats_center(cx, cx.xc, sq, st_out=st_out)
            join_all(cx)
        nc.sync.wait_ge(cx.s_ev.h, cx.s_ev.n)
        dma(nc.sync, xc_out[:, :], cx.xc[:, :, :].rearrange("p c t -> p (c t)"), cx.s_st)
        nc.sync.wait_ge(cx.s_st.h, cx.s_st.n)
    return P


def build_p_unload():
    P = Prog()
    nc = P.nc
    io = {
        "xc_in": P.dram("xc_in", [128, NCK * TOK], BF16, kind="ExternalInput").ap(),
        "st_in": P.dram("st_in", [1, 2 * D], F32, kind="ExternalInput").ap(),
        "consts": P.dram("consts", [128, 8], F32, kind="ExternalInput").ap(),
        "onesrow": P.dram("onesrow", [1, 128], BF16, kind="ExternalInput").ap(),
        "onescol": P.dram("onescol", [128, 1], BF16, kind="ExternalInput").ap(),
        "ident": P.dram("ident", [128, 128], BF16, kind="ExternalInput").ap(),
        "ident32": P.dram("ident32", [128, 128], F32, kind="ExternalInput").ap(),
        "tieb": P.dram("tieb", [128, 16], F32, kind="ExternalInput").ap(),
    }
    out = P.dram("out", [TOK, D], F32, kind="ExternalOutput").ap()
    with P.stack:
        cx = alloc_common(P, io)
        dma(nc.sync, cx.xc[:, :, :].rearrange("p c t -> p (c t)"), io["xc_in"][:, :], cx.s_ld)
        dma(nc.gpsimd, cx.stat_bf[0:1, :], io["st_in"][0:1, :], cx.s_ld)
        nc.tensor.wait_ge(cx.s_ld.h, cx.s_ld.n)
        for tb in range(TOK // 512):
            bank = cx.rotm.acquire(nc.tensor)
            cx.rotm.acquired()
            mm = nc.tensor.matmul(out=bank[:, :], lhsT=cx.onesrow[0:1, :],
                                  rhs=cx.stat_bf[0:1, tb * 512:(tb + 1) * 512], start=True, stop=True)
            tok = cx.rotm.commit(mm)
            nc.vector.wait_ge(cx.rotm.sp.h, tok)
            last = None
            for ck in range(NCK):
                last = nc.vector.tensor_tensor(out=cx.xc[:, ck, tb * 512:(tb + 1) * 512],
                                               in0=cx.xc[:, ck, tb * 512:(tb + 1) * 512],
                                               in1=bank[:, :], op=ALU.add)
            cx.rotm.release(last)
        join_all(cx)
        emit_store_out(cx, cx.xc, out)
    return P


def build_p_load_ffn0():
    P = Prog()
    nc = P.nc
    io = {
        "u": P.dram("u", [TOK, D], F32, kind="ExternalInput").ap(),
        "w_router": P.dram("w_router", [D, 8], F32, kind="ExternalInput").ap(),
        "w_fc": P.dram("w_fc", [8, D, D], F32, kind="ExternalInput").ap(),
        "w_proj": P.dram("w_proj", [8, D, D], F32, kind="ExternalInput").ap(),
        "consts": P.dram("consts", [128, 8], F32, kind="ExternalInput").ap(),
        "onesrow": P.dram("onesrow", [1, 128], BF16, kind="ExternalInput").ap(),
        "onescol": P.dram("onescol", [128, 1], BF16, kind="ExternalInput").ap(),
        "ident": P.dram("ident", [128, 128], BF16, kind="ExternalInput").ap(),
        "ident32": P.dram("ident32", [128, 128], F32, kind="ExternalInput").ap(),
        "tieb": P.dram("tieb", [128, 16], F32, kind="ExternalInput").ap(),
    }
    g_out = P.dram("g_out", [128, NCK * TOK], BF16, kind="ExternalOutput").ap()
    xc_out = P.dram("xc_out", [128, NCK * TOK], BF16, kind="ExternalOutput").ap()
    st_out = P.dram("st_out", [1, 2 * D], F32, kind="ExternalOutput").ap()
    with P.stack:
        cx = alloc_common(P, io)
        with P.scope():
            sq = P.sbuf("sq", [128, NCK, 512], BF16)
            emit_load_x(cx)
            emit_stats_center(cx, cx.xc, sq, st_out=st_out)
            join_all(cx)
        nc.sync.wait_ge(cx.s_ev.h, cx.s_ev.n)
        dma(nc.sync, xc_out[:, :], cx.xc[:, :, :].rearrange("p c t -> p (c t)"), cx.s_st)
        gt = P.sbuf("gt", [128, 16, 2048], BF16)
        emit_ffn0(cx, gt, dbg=None)
        nc.sync.wait_ge(cx.s_ev.h, cx.s_ev.n)
        dma(nc.sync, g_out[:, :], gt[:, :, :].rearrange("p c t -> p (c t)"), cx.s_st)
        nc.sync.wait_ge(cx.s_st.h, cx.s_st.n)
    return P


def kernel(**inputs):
    from concourse.bass_utils import run_bass_kernel_spmd
    inp = inputs
    u = np.asarray(inp["L0_attn_u"], np.float32)
    consts = host_consts()
    sconsts = host_ssd_consts(inp)
    slices = core_slices(u)

    def run(P, maps):
        return run_bass_kernel_spmd(P.nc, maps, core_ids=list(range(8)))

    ipb = np.asarray(inp["L0_attn_in_proj_b"], np.float32)
    ipb_a = np.ascontiguousarray(ipb[4096:8320].reshape(33, 128).T)
    ipb_zero = np.zeros((128, 33), np.float32)
    ipb_dt = np.ascontiguousarray(ipb[8320:8352].reshape(32, 1))
    ipb_z = np.ascontiguousarray(ipb[0:4096].reshape(32, 128).T)
    opb = np.ascontiguousarray(np.asarray(inp["L0_attn_out_proj_b"], np.float32).reshape(16, 128).T)
    lnw0 = np.ascontiguousarray(np.asarray(inp["L0_attn_ln_w"], np.float32).reshape(32, 128).T)
    lnb0 = np.ascontiguousarray(np.asarray(inp["L0_attn_ln_b"], np.float32).reshape(32, 128).T)
    ebias256 = np.ascontiguousarray(
        np.broadcast_to(np.asarray(inp["L1_ffn_expert_bias"], np.float32)[None, None, :],
                        (128, 16, 16)).reshape(128, 256))
    lnsc1 = np.ascontiguousarray(np.asarray(inp["L1_attn_ln_scale"], np.float32).reshape(16, 128).T)

    maps = [dict(consts, u=slices[c],
                 w_router=inp["L0_ffn_w_router"], w_fc=inp["L0_ffn_w_fc"],
                 w_proj=inp["L0_ffn_w_proj"]) for c in range(8)]
    r2 = run(build_p_load_ffn0(), maps)
    r1 = r2

    maps = []
    for c in range(8):
        m = dict(consts, **sconsts)
        m["xc_in"] = r1.results[c]["xc_out"]
        m["st_in"] = r1.results[c]["st_out"]
        m["in_proj_w"] = inp["L0_attn_in_proj_w"]
        m["uprev"] = host_uprev(u, c)
        m["ipb_a"] = ipb_a
        m["ipb_halo"] = ipb_a if c % 2 == 1 else ipb_zero
        m["ipb_dt"] = ipb_dt
        maps.append(m)
    r3 = run(build_p_attn0A(), maps)

    maps = []
    for c in range(8):
        m = dict(consts)
        m["xc_in"] = r1.results[c]["xc_out"]
        m["st_in"] = r1.results[c]["st_out"]
        m["in_proj_w"] = inp["L0_attn_in_proj_w"]
        m["out_proj_w"] = inp["L0_attn_out_proj_w"]
        m["ln_w"] = lnw0
        m["ln_b"] = lnb0
        m["ipb_z"] = ipb_z
        m["opb"] = opb
        g2 = r2.results[c]["g_out"].reshape(128, 16, 2048)
        m["g_in"] = np.ascontiguousarray(np.transpose(g2, (1, 0, 2)).reshape(2048, 2048))
        m["y_in"] = r3.results[c]["y_local"]
        peer = c - 1 if c % 2 == 1 else None
        m["s_in"] = (r3.results[peer]["s_fin"] if peer is not None
                     else np.zeros((64, 32 * 128), ml_dtypes.bfloat16))
        m["ct_in"] = r3.results[c]["ct_all"]
        m["pref_in"] = r3.results[c]["pref"]
        m["cum_in"] = r3.results[c]["cumrows"]
        maps.append(m)
    r4 = run(build_p_attn0B(), maps)

    maps = []
    for c in range(8):
        m = dict(consts)
        m["xc_in"] = r4.results[c]["x1_out"]
        m["st_in"] = r4.results[c]["st1_out"]
        m["gate_weight"] = inp["L1_ffn_gate_weight"]
        m["ebias256"] = ebias256
        for k in ["gate_proj", "up_proj", "down_proj", "shared_gate", "shared_up", "shared_down"]:
            m[k] = inp["L1_ffn_" + k]
        maps.append(m)
    r5 = run(build_p_ffn1(), maps)

    P6 = build_p_attn1A()

    def a1maps(hins):
        maps = []
        for c in range(8):
            m = dict(consts)
            m["xc_in"] = r4.results[c]["x1_out"]
            m["st_in"] = r4.results[c]["st1_out"]
            m["Wf"] = inp["L1_attn_Wf"]
            m["Wc"] = inp["L1_attn_Wc"]
            m["h_in"] = hins[c]
            sela = 1.0 if c % 2 else 0.0
            m["selab"] = np.ascontiguousarray(
                np.broadcast_to(np.array([sela, 1.0 - sela], np.float32)[None, :], (128, 2)))
            maps.append(m)
        return maps

    z16 = np.zeros((128, 16), np.float32)
    r6a = run(P6, a1maps([z16] * 8))
    hins = [z16 if c % 2 == 0 else r6a.results[c - 1]["hfin_out"] for c in range(8)]
    r6b = run(P6, a1maps(hins))
    hres = [r6a.results[c] if c % 2 == 0 else r6b.results[c] for c in range(8)]

    maps = []
    for c in range(8):
        m = dict(consts)
        m["xc_in"] = r4.results[c]["x1_out"]
        m["st_in"] = r4.results[c]["st1_out"]
        m["Wr"] = inp["L1_attn_Wr"]
        m["Wo"] = inp["L1_attn_Wo"]
        m["lnsc"] = lnsc1
        m["h_in_full"] = hres[c]["h_out"]
        g2 = r5.results[c]["g_out"].reshape(128, 16, 2048)
        m["g_in"] = np.ascontiguousarray(np.transpose(g2, (1, 0, 2)).reshape(2048, 2048))
        maps.append(m)
    r8 = run(build_p_attn1B(), maps)

    out = np.empty((4, 4096, 2048), np.float32)
    for c in range(8):
        b, h = c // 2, c % 2
        out[b, h * TOK:(h + 1) * TOK, :] = r8.results[c]["x2_out"]
    return out
